# version 29
# baseline (speedup 1.0000x reference)
import numpy as np
import ml_dtypes
import concourse.bass as bass
import concourse.mybir as mybir
from concourse.bass_utils import run_bass_kernel_spmd

F32 = mybir.dt.float32
BF16 = mybir.dt.bfloat16
I32 = mybir.dt.int32
U32 = mybir.dt.uint32
AF = mybir.ActivationFunctionType
ALU = mybir.AluOpType
AX = mybir.AxisListType

B = 8
T = 4096
D = 1024
DP = 6432
DI = 2048
H = 32
P = 64
N = 64
L = 128
SCALE = float((N * 2) ** -0.5)
EPS = 1e-6
NEXP = 8
FF = 4096
CAP = 640
BIG = 1.0e6
TWO_PI = float(2.0 * np.pi)
HALF_PI = float(0.5 * np.pi)
PI = float(np.pi)

NCHUNK = 32
DO_FFN = True
DEBUG_TAPS = []
UNPIPE = False

ENG = ("sync", "tensor", "vector", "scalar", "gpsimd")
DMAQ = ("sync", "scalar", "gpsimd")
KPOOL = 6
AUTO_ORDER = ("tensor",)


class Sched:

    def __init__(self):
        self.sem_names = ["e_tensor", "e_vector", "e_scalar", "e_gpsimd", "cc"]
        self.sem_names += [f"q_{q}{i}" for q in DMAQ for i in range(KPOOL)]
        self.sem_index = {n: i for i, n in enumerate(self.sem_names)}
        self.nsem = len(self.sem_names)
        self.items = {e: [] for e in ENG}
        self.ecount = dict(tensor=0, vector=0, scalar=0, gpsimd=0)
        self.qcount = {q: 0 for q in DMAQ}
        self.cc_count = 0
        self.satisfied = {e: np.zeros(self.nsem, np.int64) for e in ENG}
        self.tokens = []
        self.regions = {}
        self.auto_extra = []

    def _dep_tokens(self, reads, writes):
        toks = list(self.auto_extra)
        for r in reads:
            reg = self.regions.get(r)
            if reg and reg[0] is not None:
                toks.append(reg[0])
        for w in writes:
            reg = self.regions.get(w)
            if reg:
                if reg[0] is not None:
                    toks.append(reg[0])
                toks.extend(reg[1])
        return toks

    def _update_regions(self, tok, reads, writes):
        for r in reads:
            self.regions.setdefault(r, [None, []])[1].append(tok)
        for w in writes:
            self.regions[w] = [tok, []]

    def _resolve(self, e, toks, forced=()):
        sat = self.satisfied[e]
        want = {}
        for t in toks:
            semi, th, _ = self.tokens[t]
            if sat[semi] < th:
                want[semi] = max(want.get(semi, 0), th)
        for (semname, val) in forced:
            si = self.sem_index[semname]
            if val > 0 and sat[si] < val:
                want[si] = max(want.get(si, 0), val)
        waits = sorted(want.items())
        for si, val in waits:
            if sat[si] < val:
                sat[si] = val
        for t in toks:
            np.maximum(sat, self.tokens[t][2], out=sat)
        return waits

    def _mk_token(self, e, semname, th):
        vc = self.satisfied[e].copy()
        si = self.sem_index[semname]
        vc[si] = max(vc[si], th)
        tok = len(self.tokens)
        self.tokens.append((si, th, vc))
        return tok

    def op(self, e, fn, reads=(), writes=(), extra=()):
        toks = self._dep_tokens(reads, writes) + [t for t in extra
                                                 if t is not None]
        waits = self._resolve(e, toks)
        self.ecount[e] += 1
        semname = "e_" + e
        tok = self._mk_token(e, semname, self.ecount[e])
        if e in AUTO_ORDER:
            self.satisfied[e][self.sem_index[semname]] = self.ecount[e]
        self.items[e].append((waits, fn, semname, 1))
        self._update_regions(tok, reads, writes)
        return tok

    def _dma_common(self, q, fn, reads, writes, extra):
        i = self.qcount[q]
        self.qcount[q] += 1
        semname = f"q_{q}{i % KPOOL}"
        rounds = i // KPOOL
        th = 16 * (rounds + 1)
        toks = self._dep_tokens(reads, writes) + [t for t in extra
                                                  if t is not None]
        forced = [(semname, 16 * rounds)] if rounds > 0 else []
        waits = self._resolve(q, toks, forced)
        tok = self._mk_token(q, semname, th)
        self.items[q].append((waits, fn, semname, 16))
        self._update_regions(tok, reads, writes)
        return tok

    def dma(self, q, out, in_, reads=(), writes=(), extra=(), **kw):
        def fn(eng, out=out, in_=in_, kw=kw):
            return eng.dma_start(out=out, in_=in_, **kw)
        return self._dma_common(q, fn, reads, writes, extra)

    def indirect(self, out, out_off, in_, in_off, reads=(), writes=(),
                 extra=()):
        def fn(eng, out=out, in_=in_):
            return eng.indirect_dma_start(out, out_off, in_, in_off,
                                          oob_is_err=False)
        return self._dma_common("gpsimd", fn, reads, writes, extra)

    def collective(self, fn, reads=(), writes=(), extra=()):
        toks = self._dep_tokens(reads, writes) + [t for t in extra
                                                  if t is not None]
        waits = self._resolve("gpsimd", toks)
        self.cc_count += 1
        tok = self._mk_token("gpsimd", "cc", self.cc_count)
        self.items["gpsimd"].append((waits, fn, "cc", 1))
        self._update_regions(tok, reads, writes)
        return tok

    def fence(self):
        forced = []
        for e in ("tensor", "vector", "scalar"):
            if self.ecount[e]:
                forced.append(("e_" + e, self.ecount[e]))
        for q in DMAQ:
            n = self.qcount[q]
            for i in range(min(n, KPOOL)):
                forced.append((f"q_{q}{i}", 16 * ((n - 1 - i) // KPOOL + 1)))
        waits = self._resolve("gpsimd", [], forced)
        self.ecount["gpsimd"] += 1
        tok = self._mk_token("gpsimd", "e_gpsimd",
                             self.ecount["gpsimd"])
        self.items["gpsimd"].append((waits, lambda eng: eng.nop(),
                                     "e_gpsimd", 1))
        return tok

    def finalize(self):
        forced = []
        for e in ("tensor", "vector", "scalar"):
            if self.ecount[e]:
                forced.append(("e_" + e, self.ecount[e]))
        for q in DMAQ:
            n = self.qcount[q]
            for i in range(min(n, KPOOL)):
                forced.append((f"q_{q}{i}", 16 * ((n - 1 - i) // KPOOL + 1)))
        waits = self._resolve("gpsimd", [], forced)
        self.items["gpsimd"].append((waits, lambda eng: eng.nop(), None, 0))

    def build(self, nc, block, ctx):
        sems = {n: ctx.enter_context(nc.semaphore(n)) for n in self.sem_names}
        names = self.sem_names
        for e in ENG:
            if not self.items[e]:
                continue

            def mk(e):
                items = self.items[e]

                def prog(eng):
                    for waits, fn, semname, inc in items:
                        for (si, val) in waits:
                            eng.wait_ge(sems[names[si]], val)
                        inst = fn(eng)
                        if inc:
                            inst.then_inc(sems[semname], inc)
                return prog

            getattr(block, e)(mk(e))


def _consts():
    bf = ml_dtypes.bfloat16
    c = {}
    c["id_f"] = np.eye(128, dtype=np.float32)
    c["id_b"] = np.eye(128, dtype=np.float32).astype(bf)
    c["triu_incl"] = np.triu(np.ones((128, 128), np.float32)).astype(bf)
    c["triu32_strict"] = np.triu(np.ones((32, 32), np.float32), 1).astype(bf)
    mask = np.where(np.triu(np.ones((128, 128), np.float32)) > 0, 0.0, -1e30)
    c["maskneg4"] = mask.astype(bf)
    c["maskscale"] = (np.triu(np.ones((128, 128), np.float32)) * SCALE
                      ).astype(np.float32)
    c["ones_b"] = np.ones((1, 128), np.float32).astype(bf)
    sel = np.zeros((128, 1024), np.float32)
    sel[0:8] = np.repeat(np.eye(8, dtype=np.float32), 128, axis=1)
    c["selrow32"] = sel.astype(bf)
    op = np.zeros((128, 128), np.float32)
    op[0] = 1.0
    c["onepad"] = op.astype(bf)
    c["shift1"] = np.eye(128, k=1, dtype=np.float32).astype(bf)
    s127 = np.zeros((128, 128), np.float32)
    s127[127, 0] = 1.0
    c["sel127"] = s127.astype(bf)
    c["iota8"] = np.tile(np.arange(8, dtype=np.float32)[None, :], (128, 1))
    c["iotamb"] = c["iota8"] - BIG
    return c


def build_program():
    from contextlib import ExitStack
    nc = bass.Bass("TRN2", target_bir_lowering=False, debug=False)
    S = Sched()
    ctx = ExitStack()

    def din(name, shape, dt=F32):
        return nc.dram_tensor(name, list(shape), dt, kind="ExternalInput").ap()

    x_h = din("x", (T, D))
    win_h = din("W_in", (D, DP))
    alog_h = din("A_log", (1, H))
    dtb_h = din("dt_bias", (1, H))
    bnb_h = din("B_nb", (1, N))
    cnb_h = din("C_nb", (1, N))
    wout_h = din("W_out", (DI, D))
    rw_h = din("router_w", (D, NEXP))
    w1_h = din("w1", (NEXP, D, FF))
    b1_h = din("b1", (NEXP, FF))
    w2_h = din("w2", (NEXP, FF, D))
    b2_h = din("b2", (NEXP, D))
    cmask_h = din("coremask64", (1, 64))
    id_f_h = din("id_f", (128, 128))
    id_b_h = din("id_b", (128, 128), BF16)
    triu_h = din("triu_incl", (128, 128), BF16)
    triu32_h = din("triu32_strict", (32, 32), BF16)
    maskneg_h = din("maskneg4", (128, 128), BF16)
    masksc_h = din("maskscale", (128, 128))
    ones_b_h = din("ones_b", (1, 128), BF16)
    selrow_h = din("selrow32", (128, 1024), BF16)
    onepad_h = din("onepad", (128, 128), BF16)
    shift1_h = din("shift1", (128, 128), BF16)
    sel127_h = din("sel127", (128, 128), BF16)
    iota8_h = din("iota8", (128, 8))
    iotamb_h = din("iotamb", (128, 8))

    out_h = nc.dram_tensor("out", [T, D], F32, kind="ExternalOutput").ap()

    xn_d = nc.dram_tensor("xn_d", [T, D], BF16).ap()
    s_d = nc.dram_tensor("s_d", [T, D], F32).ap()
    eb_d = nc.dram_tensor("eb_d", [NEXP * CAP, D], BF16).ap()
    eo_d = nc.dram_tensor("eo_d", [NEXP * CAP, D], BF16).ap()
    ccin_d = nc.dram_tensor("ccin_d", [1, NEXP], F32).ap()
    ccout_d = nc.dram_tensor("ccout_d", [B, NEXP], F32).ap()
    tot_d = nc.dram_tensor("tot_d", [1, 8 * NCHUNK], BF16).ap()
    off_d = nc.dram_tensor("off_d", [NCHUNK, 8], F32).ap()

    ctx1 = ExitStack()

    def sb(name, shape, dt=F32):
        return ctx.enter_context(nc.sbuf_tensor("sb_" + name, list(shape), dt))

    def sb1(name, shape, dt=F32):
        return ctx1.enter_context(nc.sbuf_tensor("sb_" + name, list(shape), dt))

    def psum(name, shape, dt=F32):
        return ctx.enter_context(nc.psum_tensor(name, list(shape), dt))

    xc = [sb("xc0", (128, D)), sb("xc1", (128, D))]
    id_f = sb("id_f", (128, 128))
    id_b = sb("id_b", (128, 128), BF16)
    triu32 = sb("triu32", (32, 32), BF16)
    ones_b = sb("ones_b", (1, 128), BF16)
    iota8 = sb("iota8", (128, 8))
    ones_c = sb("ones_c", (128, 1), BF16)
    cb_ln = sb("cb_ln", (128, 3))
    iotamb = sb("iotamb", (128, 8))
    cmask = sb("cmask", (1, 64))
    SCNT = sb("SCNT", (128, 8 * NCHUNK), BF16)
    IDX = sb("IDX", (128, NCHUNK))
    GATE = sb("GATE", (128, NCHUNK))
    emk = sb("emk", (128, 8), BF16)
    TOT = sb("TOT", (32, 8), BF16)
    OFFs = sb("OFFs", (32, 8))
    OFFrow = sb("OFFrow", (1, 256))
    OFFrh = sb("OFFrh", (1, 256), BF16)
    OFFrl = sb("OFFrl", (1, 256), BF16)
    ctot = sb("ctot", (1, 8))
    gtrow = sb("gtrow", (1, 64))
    myoff = sb("myoff", (1, 8))
    myoh = sb("myoh", (1, 8), BF16)
    myol = sb("myol", (1, 8), BF16)
    MYOFF = sb("MYOFF", (128, 8))
    zrow = sb("zrow", (1, 256), BF16)
    onepad = sb("onepad", (128, 128), BF16)
    WIN = sb1("WIN", (128, 8 * DP), BF16)
    WOUT = sb1("WOUT", (128, 16 * D), BF16)
    rw_f = sb1("rw_f", (128, 64))
    rwhl = sb1("rwhl", (128, 128), BF16)
    selrow = sb1("selrow", (128, 1024), BF16)
    shift1 = sb1("shift1", (128, 128), BF16)
    sel127 = sb1("sel127", (128, 128), BF16)
    triu = sb1("triu", (128, 128), BF16)
    maskneg = sb1("maskneg", (128, 128), BF16)
    masksc = sb1("masksc", (128, 128))
    abc = sb1("abc", (128, 32))
    dtbb = sb1("dtbb", (128, 32))
    bnb = sb1("bnb", (128, 64))
    cnb = sb1("cnb", (128, 64))
    rowtmp = sb1("rowtmp", (1, 64))
    rowh = sb1("rowh", (1, 64), BF16)
    rowl = sb1("rowl", (1, 64), BF16)
    xnT = sb1("xnT", (128, D), BF16)
    sz = sb1("szb", (128, DI), BF16)
    xh = sb1("xh", (128, 4096), BF16)
    xhp = sb1("xhp", (128, 4096), BF16)
    bcr = sb1("bcr", (128, 192), BF16)
    brt = bcr[:, 0:128]
    crt = bcr[:, 128:192]
    bbt = sb1("bbt", (128, 128), BF16)
    CrTz = [sb1("CrTz0", (128, 128), BF16), sb1("CrTz1", (128, 128), BF16)]
    lgbuf = sb1("lgbuf", (128, 24))
    lg8 = sb1("LG", (128, 8))
    cum = sb1("cum", (128, 64))
    cumg = [sb1("cumg0", (8, 128)), sb1("cumg1", (8, 128))]
    cumgh = [sb1("cumgh0", (128, 128), BF16), sb1("cumgh1", (128, 128), BF16)]
    cumgl = [sb1("cumgl0", (128, 128), BF16), sb1("cumgl1", (128, 128), BF16)]
    clast = sb1("clast", (1, 64))
    angchl = sb1("angchl", (128, 64), BF16)
    clhl = sb1("clhl", (128, 64), BF16)
    cdrow = sb1("cdrow", (1, 32))
    cdhl = sb1("cdhl", (128, 64), BF16)
    CD = sb1("CD", (64, 32))
    hilo = sb1("hilo", (128, 128), BF16)
    nchl = sb1("nchl", (128, 64), BF16)
    small = sb1("small", (128, 576))
    smcol = sb1("smcol", (128, 16))
    rms3 = sb1("rms3", (128, 4))
    scb = sb1("scb", (128, 64))
    sinb = scb[:, 0:32]
    cosb = scb[:, 32:64]
    ktmp = sb1("ktmp", (128, 64), I32)
    sgT = sb1("sgT", (128, 512), BF16)
    dgT = [sb1("dgT0", (128, 1024), BF16), sb1("dgT1", (128, 1024), BF16)]
    Wg = sb1("Wg", (128, 1024), BF16)
    xga = sb1("xga", (128, 1024), BF16)
    xba = sb1("xba", (128, 1024), BF16)
    ysc = sb1("ysc", (128, 512))
    YG = sb1("YG", (128, DI), BF16)
    Sst = sb1("Sst", (64, DI))
    sbfg = sb1("Sbf", (128, 512), BF16)
    BrT = [sb1("BrT0", (128, 132), BF16), sb1("BrT1", (128, 132), BF16)]
    SM = dict(dt=0, logdA=32, alpha=64, lam=96, lam1m=128, scal_g=160,
              scal_b=192, scal_gd=224, scal_bd=256, dec_in=288, dec_end=320,
              tdec=352, ang=384, red=416, kf=448, u2=480, rt1=512, rt2=544)
    s_sb = sz[:, :].bitcast(F32)
    xnTl = dgT[0][:, :]
    Wb = xnT[:, :]
    ysum = Wg[:, :].bitcast(F32)
    stmp = xga[0:64, :].bitcast(F32)
    sqb = YG[:, 0:384].bitcast(F32)
    Bn = YG[:, 384:640].bitcast(F32)
    Cn = YG[:, 640:768].bitcast(F32)
    praw = YG[:, 768:1344].bitcast(F32)
    xn = YG[:, :].bitcast(F32)
    ynT = xga[:, :]
    pA = [psum("pA0", (128, 512)), psum("pA1", (128, 512))]
    pT = psum("pT", (128, 1536))
    pTB = psum("pTB", (128, 1024), BF16)
    pY = psum("pY", (128, 512))
    pST = psum("pST", (64, 512))
    MB = 1024

    V, A, G_, TE = "vector", "scalar", "gpsimd", "tensor"
    dbg = []

    def tap(name, ap, reads):
        if name in DEBUG_TAPS:
            t = nc.dram_tensor("dbg_" + name, list(ap.shape),
                               ap.dtype, kind="ExternalOutput").ap()
            S.dma("sync", t, ap, reads=list(reads))
            dbg.append(name)

    for nm, dst, src in (("id_f", id_f, id_f_h), ("id_b", id_b, id_b_h),
                         ("triu", triu, triu_h), ("triu32", triu32, triu32_h),
                         ("maskneg", maskneg, maskneg_h),
                         ("masksc", masksc, masksc_h),
                         ("ones_b", ones_b, ones_b_h),
                         ("iota8", iota8, iota8_h),
                         ("iotamb", iotamb, iotamb_h),
                         ("selrow", selrow, selrow_h),
                         ("shift1", shift1, shift1_h),
                         ("sel127", sel127, sel127_h),
                         ("onepad", onepad, onepad_h),
                         ("cmask", cmask, cmask_h)):
        S.dma("sync", dst[:, :], src, writes=[nm])
    S.dma("sync", rw_f[:, :].rearrange("p (k e) -> p k e", k=8),
          rw_h.rearrange("(k p) e -> p k e", p=128), writes=["rw_f"])
    win_r = win_h.rearrange("(k p) f -> p k f", p=128)
    for k in range(8):
        S.dma("gpsimd", WIN[:, k * DP:(k + 1) * DP],
              win_r[:, k, :], writes=[f"WIN{k}"])
    wout_r = wout_h.rearrange("(k p) d -> p k d", p=128)
    for j in range(4):
        S.dma("gpsimd",
              WOUT[:, j * 4096:(j + 1) * 4096].rearrange(
                  "p (a d) -> p a d", a=4),
              wout_r[:, j * 4:(j + 1) * 4, :], writes=[f"WOUT{j}"])
    rwhl_v = rwhl[:, :].rearrange("p (k t e) -> p k t e", k=8, t=2)
    rwf_v = rw_f[:, :].rearrange("p (k e) -> p k e", k=8)
    S.op(V, lambda e: e.tensor_copy(rwhl_v[:, :, 0, :], rwf_v),
         reads=["rw_f"], writes=["rwhl"])
    S.op(V, lambda e: e.tensor_tensor(rwhl_v[:, :, 1, :], rwf_v,
                                      rwhl_v[:, :, 0, :], ALU.subtract),
         reads=["rw_f", "rwhl"], writes=["rwhl"])

    def bcast_row(src_h, width, dst, dreg, negexp=False):
        S.dma("sync", rowtmp[:, 0:width], src_h, writes=["rowtmp"])
        if negexp:
            S.op(A, lambda e: e.activation(rowtmp[:, 0:width],
                                           rowtmp[:, 0:width], AF.Exp),
                 reads=["rowtmp"], writes=["rowtmp"])
            S.op(V, lambda e: e.tensor_scalar(rowtmp[:, 0:width],
                                              rowtmp[:, 0:width], -1.0, None,
                                              ALU.mult),
                 reads=["rowtmp"], writes=["rowtmp"])
        S.op(V, lambda e: e.tensor_copy(rowh[:, 0:width], rowtmp[:, 0:width]),
             reads=["rowtmp"], writes=["rowh"])
        S.op(V, lambda e: e.tensor_tensor(rowl[:, 0:width], rowtmp[:, 0:width],
                                          rowh[:, 0:width], ALU.subtract),
             reads=["rowtmp", "rowh"], writes=["rowl"])
        S.op(TE, lambda e: e.matmul(pT[:, MB:MB + width], ones_b[:, 0:128],
                                    rowh[:, 0:width], start=True, stop=False),
             reads=["rowh", "ones_b"], writes=["pT_M"])
        S.op(TE, lambda e: e.matmul(pT[:, MB:MB + width], ones_b[:, 0:128],
                                    rowl[:, 0:width], start=False, stop=True),
             reads=["rowl", "ones_b"], writes=["pT_M"])
        S.op(V, lambda e: e.tensor_copy(dst[:, 0:width], pT[:, MB:MB + width]),
             reads=["pT_M"], writes=[dreg])

    bcast_row(alog_h, 32, abc, "abc", negexp=True)
    bcast_row(dtb_h, 32, dtbb, "dtbb")
    bcast_row(bnb_h, 64, bnb, "bnb")
    bcast_row(cnb_h, 64, cnb, "cnb")

    S.op("vector", lambda e: e.memset(ones_c[:, :], 1.0), writes=["ones_c"])
    S.op("vector", lambda e: e.memset(cb_ln[:, 0:1], 1e-5), writes=["cb_ln"])
    S.op("vector", lambda e: e.memset(cb_ln[:, 1:2], 1.0), writes=["cb_ln"])
    S.op("vector", lambda e: e.memset(cb_ln[:, 2:3], EPS), writes=["cb_ln"])
    for t_, r_ in ((Sst, "Sst"), (BrT[0], "BrT0"), (zrow, "zrow"),
                   (xh, "xh"), (bbt, "bbt"),
                   (angchl, "angchl"), (clhl, "clhl"), (cdhl, "cdhl"),
                   (cumgh[0], "cumgh0"), (cumgh[1], "cumgh1"),
                   (cumgl[0], "cumgl0"), (cumgl[1], "cumgl1"),
                   (CrTz[0], "CrTz0"), (CrTz[1], "CrTz1"),
                   (sbfg, "sbfg")):
        S.op(V, lambda e, t_=t_: e.memset(t_[:, :], 0.0), writes=[r_])
    for zq in range(4):
        S.dma("gpsimd", eb_d[0:1, zq * 256:(zq + 1) * 256], zrow[:, :],
              reads=["zrow"], writes=["ebz0"])
    ebz_keys = ["ebz0"]

    S.dma("sync", xc[0][:, :], x_h[0:128, :], writes=["xc0"])

    def sm(key, w=32):
        return small[:, SM[key]:SM[key] + w]

    def emit_ln(cc):
        xcb = xc[cc % 2]
        xreg = f"xc{cc % 2}"
        S.op(V, lambda e, xcb=xcb: e.reduce_sum(smcol[:, 0:1], xcb[:, :],
                                                axis=AX.X),
             reads=[xreg], writes=["smc0"])
        S.op(A, lambda e, xcb=xcb: e.activation(xga[:, :], xcb[:, :],
                                                AF.Square,
                                                accum_out=smcol[:, 1:2]),
             reads=[xreg], writes=["smc1", "xga"])
        S.op(G_, lambda e: e.tensor_scalar(smcol[:, 2:3], smcol[:, 0:1],
                                           1.0 / D, None, ALU.mult),
             reads=["smc0"], writes=["smc2"])
        S.op(G_, lambda e: e.tensor_scalar(smcol[:, 3:4], smcol[:, 1:2],
                                           1.0 / D, None, ALU.mult),
             reads=["smc1"], writes=["smc3"])
        S.op(G_, lambda e: e.tensor_tensor(smcol[:, 4:5], smcol[:, 2:3],
                                           smcol[:, 2:3], ALU.mult),
             reads=["smc2"], writes=["smc4"])
        S.op(G_, lambda e: e.tensor_tensor(smcol[:, 4:5], smcol[:, 3:4],
                                           smcol[:, 4:5], ALU.subtract),
             reads=["smc3", "smc4"], writes=["smc4"])
        S.op(A, lambda e: e.activation(smcol[:, 5:6], smcol[:, 4:5], AF.Ln,
                                       bias=cb_ln[:, 0:1]),
             reads=["smc4", "cb_ln"], writes=["smc5"])
        S.op(A, lambda e: e.activation(smcol[:, 5:6], smcol[:, 5:6], AF.Exp,
                                       scale=-0.5),
             reads=["smc5"], writes=["smc5"])
        S.op(G_, lambda e: e.tensor_scalar(smcol[:, 6:7], smcol[:, 2:3],
                                           -1.0, None, ALU.mult),
             reads=["smc2"], writes=["smc6"])
        S.op(V, lambda e, xcb=xcb: e.tensor_scalar(xn[:, :], xcb[:, :],
                                                   smcol[:, 6:7],
                                                   smcol[:, 5:6],
                                                   ALU.add, ALU.mult),
             reads=[xreg, "smc6", "smc5"], writes=["YG"])
        tap(f"xn{cc}", xn[:, :], ["YG"])
        if DO_FFN:
            S.dma("gpsimd", xn_d[cc * 128:(cc + 1) * 128, :], xn[:, :],
                  reads=["YG"], writes=[f"xnd{cc}"])


    for c in range(NCHUNK):
        par = c % 2
        xcb = xc[par]
        xreg = f"xc{par}"
        bpar = c % 2
        if c + 1 < NCHUNK:
            S.dma("sync", xc[1 - par][:, :],
                  x_h[(c + 1) * 128:(c + 2) * 128, :], writes=[f"xc{1 - par}"])
        if c == 0:
            emit_ln(0)

        for k in range(8):
            S.op(TE, lambda e, k=k: e.transpose(pT[:, k * 128:(k + 1) * 128],
                                                xn[:, k * 128:(k + 1) * 128],
                                                id_f[:, :]),
                 reads=["YG", "id_f"], writes=["pT_A"])
        S.op(A, lambda e: e.copy(xnT[:, :], pT[:, 0:1024]),
             reads=["pT_A"], writes=["xnT"])
        S.op(V, lambda e: e.tensor_tensor(xnTl[:, :], pT[:, 0:1024],
                                          xnT[:, :], ALU.subtract),
             reads=["pT_A", "xnT"], writes=["dgT0"])

        def emit_router(c=c):
            for k in range(8):
                S.op(TE, lambda e, k=k: e.matmul(
                    pT[:, MB:MB + 16], xnT[:, k * 128:(k + 1) * 128],
                    rwhl[:, k * 16:(k + 1) * 16], start=(k == 0), stop=False),
                    reads=["xnT", "rwhl"], writes=["pT_M"])
            for k in range(8):
                S.op(TE, lambda e, k=k: e.matmul(
                    pT[:, MB + 16:MB + 24], xnTl[:, k * 128:(k + 1) * 128],
                    rwhl[:, k * 16:k * 16 + 8], start=(k == 0), stop=(k == 7)),
                    reads=["dgT0", "rwhl"], writes=["pT_M"])
            S.op(A, lambda e: e.copy(lgbuf[:, :], pT[:, MB:MB + 24]),
                 reads=["pT_M"], writes=["lgbuf"])
            lgc = lg8[:, :]
            S.op(V, lambda e: e.tensor_tensor(lgbuf[:, 0:8], lgbuf[:, 0:8],
                                              lgbuf[:, 8:16], ALU.add),
                 reads=["lgbuf"], writes=["lgbuf"])
            S.op(V, lambda e, lgc=lgc: e.tensor_tensor(lgc, lgbuf[:, 0:8],
                                                       lgbuf[:, 16:24], ALU.add),
                 reads=["lgbuf"], writes=["LG"])
            tap(f"logits{c}", lgc, ["LG"])
            S.op(V, lambda e, lgc=lgc: e.reduce_max(smcol[:, 7:8], lgc, axis=AX.X),
                 reads=["LG"], writes=["smc7"])
            S.op(V, lambda e: e.tensor_scalar(smcol[:, 8:9], smcol[:, 7:8],
                                              -1.0, None, ALU.mult),
                 reads=["smc7"], writes=["smc8"])
            S.op(A, lambda e, lgc=lgc: e.activation(lgbuf[:, 0:8], lgc, AF.Exp,
                                                    bias=smcol[:, 8:9]),
                 reads=["LG", "smc8"], writes=["lgbuf"])
            S.op(V, lambda e: e.reduce_sum(smcol[:, 9:10], lgbuf[:, 0:8],
                                           axis=AX.X),
                 reads=["lgbuf"], writes=["smc9"])
            S.op(V, lambda e, c=c: e.reciprocal(GATE[:, c:c + 1], smcol[:, 9:10]),
                 reads=["smc9"], writes=["GATE"])
            S.op(V, lambda e, lgc=lgc: e.tensor_scalar(lgbuf[:, 8:16], lgc,
                                                       smcol[:, 7:8], None,
                                                       ALU.is_equal),
                 reads=["LG", "smc7"], writes=["lgbuf"])
            S.op(V, lambda e: e.scalar_tensor_tensor(lgbuf[:, 16:24],
                                                     lgbuf[:, 8:16], 1.0,
                                                     iotamb[:, :], ALU.mult,
                                                     ALU.mult),
                 reads=["lgbuf", "iotamb"], writes=["lgbuf"])
            S.op(V, lambda e: e.tensor_reduce(smcol[:, 10:11], lgbuf[:, 16:24],
                                              axis=AX.X, op=ALU.min),
                 reads=["lgbuf"], writes=["smc10"])
            S.op(V, lambda e, c=c: e.tensor_scalar(IDX[:, c:c + 1],
                                                   smcol[:, 10:11], BIG, None,
                                                   ALU.add),
                 reads=["smc10"], writes=["IDX"])
            S.op(V, lambda e, c=c: e.tensor_scalar(emk[:, :], iota8[:, :],
                                                   IDX[:, c:c + 1], None,
                                                   ALU.is_equal),
                 reads=["IDX", "iota8"], writes=["emk"])
            S.op(TE, lambda e: e.matmul(pT[:, MB:MB + 8], triu[:, :], emk[:, :],
                                        start=True, stop=True),
                 reads=["emk", "triu"], writes=["pT_M"])
            S.op(A, lambda e, c=c: e.copy(SCNT[:, c * 8:(c + 1) * 8],
                                          pT[:, MB:MB + 8]),
                 reads=["pT_M"], writes=["SCNT"])
            if DO_FFN and c == NCHUNK - 1:
                S.dma("sync", tot_d, SCNT[127:128, :], reads=["SCNT"],
                      writes=["tot_d"])
                S.dma("sync", TOT[:, :],
                      tot_d.rearrange("o (cc ee) -> (o cc) ee", ee=8),
                      reads=["tot_d"], writes=["TOT"])
                S.op(TE, lambda e: e.matmul(pT[0:32, MB + 256:MB + 264],
                                            triu32[:, :], TOT[:, :],
                                            start=True, stop=True),
                     reads=["TOT", "triu32"], writes=["pT_M"])
                S.op(V, lambda e: e.tensor_copy(OFFs[:, :],
                                                pT[0:32, MB + 256:MB + 264]),
                     reads=["pT_M"], writes=["OFFs"])
                S.op(TE, lambda e: e.matmul(pT[0:1, MB + 264:MB + 272],
                                            ones_c[0:32, 0:1], TOT[:, :],
                                            start=True, stop=True),
                     reads=["TOT", "ones_c"], writes=["pT_M"])
                S.op(V, lambda e: e.tensor_copy(ctot[:, :],
                                                pT[0:1, MB + 264:MB + 272]),
                     reads=["pT_M"], writes=["ctot"])
                S.dma("gpsimd", ccin_d, ctot[:, :], reads=["ctot"],
                      writes=["ccin"])

                def ccfn(e):
                    return e.collective_compute(
                        "AllGather", ALU.bypass,
                        replica_groups=[list(range(B))],
                        ins=[ccin_d.opt()], outs=[ccout_d.opt()])

                S.collective(ccfn, reads=["ccin"], writes=["ccout"])
                S.dma("sync", off_d, OFFs[:, :], reads=["OFFs"],
                      writes=["off_d"])
                S.dma("sync", OFFrow[:, :], off_d.flatten().unsqueeze(0),
                      reads=["off_d"], writes=["OFFrow"])

        def projmm(fstart, fwidth, pa, preg):
            for k in range(8):
                S.op(TE, lambda e, k=k, fstart=fstart, fwidth=fwidth, pa=pa:
                     e.matmul(pa[:, 0:fwidth],
                              xnT[:, k * 128:(k + 1) * 128],
                              WIN[:, k * DP + fstart:k * DP + fstart + fwidth],
                              start=(k == 0), stop=(k == 7)),
                     reads=["xnT", f"WIN{k}"], writes=[preg])

        projmm(6144, 288, pA[0], "pA0")
        S.op(A, lambda e: e.copy(praw[:, :], pA[0][:, 0:288]),
             reads=["pA0"], writes=["YG"])
        tap(f"praw{c}", praw[:, :], ["YG"])

        S.op(V, lambda e: e.tensor_tensor(sm("dt"), praw[:, 192:224],
                                          dtbb[:, 0:32], ALU.add),
             reads=["YG", "dtbb"], writes=["sm_dt"])
        S.op(A, lambda e: e.activation(sm("red"), sm("dt"), AF.Abs),
             reads=["sm_dt"], writes=["sm_red"])
        S.op(A, lambda e: e.activation(sm("red"), sm("red"), AF.Exp,
                                       scale=-1.0),
             reads=["sm_red"], writes=["sm_red"])
        S.op(A, lambda e: e.activation(sm("red"), sm("red"), AF.Ln,
                                       bias=cb_ln[:, 1:2]),
             reads=["sm_red", "cb_ln"], writes=["sm_red"])
        S.op(V, lambda e: e.tensor_scalar(sm("u2"), sm("dt"), 0.0, None,
                                          ALU.max),
             reads=["sm_dt"], writes=["sm_u2"])
        S.op(V, lambda e: e.tensor_tensor(sm("dt"), sm("red"), sm("u2"),
                                          ALU.add),
             reads=["sm_red", "sm_u2"], writes=["sm_dt"])
        S.op(V, lambda e: e.tensor_scalar(sm("dt"), sm("dt"), 1e-4, 0.5,
                                          ALU.max, ALU.min),
             reads=["sm_dt"], writes=["sm_dt"])
        tap(f"dt{c}", sm("dt"), ["sm_dt"])
        S.op(A, lambda e: e.activation(sm("lam1m"), praw[:, 224:256], AF.Exp,
                                       scale=-1.0),
             reads=["YG"], writes=["sm_lam1m"])
        S.op(G_, lambda e: e.tensor_scalar(sm("u2"), sm("lam1m"), 1.0, None,
                                          ALU.add),
             reads=["sm_lam1m"], writes=["sm_u2"])
        S.op(V, lambda e: e.reciprocal(sm("lam"), sm("u2")),
             reads=["sm_u2"], writes=["sm_lam"])
        S.op(G_, lambda e: e.tensor_tensor(sm("lam1m"), sm("lam"), sm("lam1m"),
                                          ALU.mult),
             reads=["sm_lam", "sm_lam1m"], writes=["sm_lam1m"])
        S.op(V, lambda e: e.tensor_tensor(sm("logdA"), sm("dt"), abc[:, 0:32],
                                          ALU.mult),
             reads=["sm_dt", "abc"], writes=["sm_logdA"])
        S.op(A, lambda e: e.activation(sm("alpha"), sm("logdA"), AF.Exp),
             reads=["sm_logdA"], writes=["sm_alpha"])
        S.op(G_, lambda e: e.tensor_tensor(sm("scal_g"), sm("lam"), sm("dt"),
                                          ALU.mult),
             reads=["sm_lam", "sm_dt"], writes=["sm_scal_g"])
        S.op(G_, lambda e: e.tensor_tensor(sm("scal_b"), sm("lam1m"), sm("dt"),
                                          ALU.mult),
             reads=["sm_lam1m", "sm_dt"], writes=["sm_scal_b"])
        S.op(G_, lambda e: e.tensor_tensor(sm("scal_b"), sm("scal_b"),
                                          sm("alpha"), ALU.mult),
             reads=["sm_scal_b", "sm_alpha"], writes=["sm_scal_b"])
        S.op(V, lambda e: e.reduce_sum(smcol[:, 11:12], sm("dt"), axis=AX.X),
             reads=["sm_dt"], writes=["smc11"])
        S.op(G_, lambda e: e.tensor_scalar(smcol[:, 11:12], smcol[:, 11:12],
                                          1.0 / 32, None, ALU.mult),
             reads=["smc11"], writes=["smc11"])
        S.op(V, lambda e: e.tensor_scalar(sm("ang"), praw[:, 256:288],
                                          smcol[:, 11:12], None, ALU.mult),
             reads=["YG", "smc11"], writes=["sm_ang"])
        S.op(V, lambda e: e.tensor_copy(hilo[:, 0:32], sm("logdA")),
             reads=["sm_logdA"], writes=["hilo"])
        S.op(V, lambda e: e.tensor_copy(hilo[:, 32:64], sm("ang")),
             reads=["sm_ang"], writes=["hilo"])
        S.op(V, lambda e: e.tensor_tensor(hilo[:, 64:96], sm("logdA"),
                                          hilo[:, 0:32], ALU.subtract),
             reads=["sm_logdA", "hilo"], writes=["hilo"])
        S.op(V, lambda e: e.tensor_tensor(hilo[:, 96:128], sm("ang"),
                                          hilo[:, 32:64], ALU.subtract),
             reads=["sm_ang", "hilo"], writes=["hilo"])
        S.op(A, lambda e: e.activation(sqb[:, :], praw[:, 0:192], AF.Square),
             reads=["YG"], writes=["YG"])
        for i in range(3):
            S.op(V, lambda e, i=i: e.reduce_sum(rms3[:, i:i + 1],
                                                sqb[:, i * 64:(i + 1) * 64],
                                                axis=AX.X),
                 reads=["YG"], writes=["rms3"])
        S.op(A, lambda e: e.activation(rms3[:, 0:3], rms3[:, 0:3], AF.Ln,
                                       scale=1.0 / 64, bias=cb_ln[:, 2:3]),
             reads=["rms3", "cb_ln"], writes=["rms3"])
        S.op(A, lambda e: e.activation(rms3[:, 0:3], rms3[:, 0:3], AF.Exp,
                                       scale=-0.5),
             reads=["rms3"], writes=["rms3"])
        for r in range(2):
            S.op(V, lambda e, r=r: e.scalar_tensor_tensor(
                Bn[:, r * 64:(r + 1) * 64], praw[:, r * 64:(r + 1) * 64],
                rms3[:, r:r + 1], bnb[:, 0:64], ALU.mult, ALU.mult),
                reads=["YG", "rms3", "bnb"], writes=["YG"])
        S.op(V, lambda e: e.scalar_tensor_tensor(
            Cn[:, :], praw[:, 128:192], rms3[:, 2:3], cnb[:, 0:64],
            ALU.mult, ALU.mult),
            reads=["YG", "rms3", "cnb"], writes=["YG"])

        for gz in range(4):
            pa, preg = pA[gz % 2], f"pA{gz % 2}"
            projmm(gz * 512, 512, pa, preg)
            S.op(A, lambda e, gz=gz, pa=pa: e.activation(
                sz[:, gz * 512:(gz + 1) * 512], pa[:, :], AF.Silu),
                reads=[preg], writes=["sz"])
        def emit_cum(c=c):
            S.op(TE, lambda e: e.matmul(pT[:, MB:MB + 64], triu[:, :],
                                        hilo[:, 0:64], start=True, stop=False),
                 reads=["hilo", "triu"], writes=["pT_M"])
            S.op(TE, lambda e: e.matmul(pT[:, MB:MB + 64], triu[:, :],
                                        hilo[:, 64:128], start=False, stop=False),
                 reads=["hilo", "triu"], writes=["pT_M"])
            S.op(TE, lambda e: e.matmul(pT[:, MB + 32:MB + 64], onepad[:, :],
                                        angchl[:, 0:32], start=False, stop=False),
                 reads=["angchl", "onepad"], writes=["pT_M"])
            S.op(TE, lambda e: e.matmul(pT[:, MB + 32:MB + 64], onepad[:, :],
                                        angchl[:, 32:64], start=False, stop=True),
                 reads=["angchl", "onepad"], writes=["pT_M"])
            S.op(A, lambda e: e.copy(cum[:, :], pT[:, MB:MB + 64]),
                 reads=["pT_M"], writes=["cum"])
            tap(f"cum{c}", cum[:, :], ["cum"])
            S.op(TE, lambda e: e.matmul(pT[0:1, MB + 64:MB + 128],
                                        id_f[:, 127:128], cum[:, :],
                                        start=True, stop=True),
                 reads=["cum", "id_f"], writes=["pT_M"])
            S.op(A, lambda e: e.copy(clast[:, :], pT[0:1, MB + 64:MB + 128]),
                 reads=["pT_M"], writes=["clast"])
            S.op(V, lambda e: e.tensor_scalar(sm("lam"), cum[:, 0:32], -1.0, None,
                                              ALU.mult),
                 reads=["cum"], writes=["sm_lam"])
            S.op(G_, lambda e: e.tensor_copy(nchl[:, 0:32], sm("lam")),
                 reads=["sm_lam"], writes=["nchl"])
            S.op(G_, lambda e: e.tensor_tensor(nchl[:, 32:64], sm("lam"),
                                               nchl[:, 0:32], ALU.subtract),
                 reads=["sm_lam", "nchl"], writes=["nchl"])
            S.op(G_, lambda e: e.tensor_copy(angchl[0:1, 0:32], clast[:, 32:64]),
                 reads=["clast"], writes=["angchl"])
            S.op(G_, lambda e: e.tensor_tensor(angchl[0:1, 32:64],
                                               clast[:, 32:64],
                                               angchl[0:1, 0:32], ALU.subtract),
                 reads=["clast", "angchl"], writes=["angchl"])
            S.op(G_, lambda e: e.tensor_copy(clhl[0:1, 0:32], clast[:, 0:32]),
                 reads=["clast"], writes=["clhl"])
            S.op(G_, lambda e: e.tensor_tensor(clhl[0:1, 32:64], clast[:, 0:32],
                                               clhl[0:1, 0:32], ALU.subtract),
                 reads=["clast", "clhl"], writes=["clhl"])
        emit_cum()
        for gx in range(8):
            pa, preg = pA[gx % 2], f"pA{gx % 2}"
            po = (gx % 2) * 512
            S.op(TE, lambda e, gx=gx, po=po: e.matmul(
                pT[:, po:po + 512], sel127[:, :],
                xh[:, gx * 512:(gx + 1) * 512], start=True, stop=False),
                reads=["xh", "sel127"], writes=["pT_A"])
            projmm(2048 + gx * 512, 512, pa, preg)
            S.op(A, lambda e, gx=gx, pa=pa: e.activation(
                xh[:, gx * 512:(gx + 1) * 512], pa[:, :], AF.Silu),
                reads=[preg], writes=["xh"])
            S.op(TE, lambda e, gx=gx, po=po: e.matmul(
                pT[:, po:po + 512], shift1[:, :],
                xh[:, gx * 512:(gx + 1) * 512], start=False, stop=True),
                reads=["xh", "shift1"], writes=["pT_A"])
            S.op(A, lambda e, gx=gx, po=po: e.copy(
                xhp[:, gx * 512:(gx + 1) * 512],
                pT[:, po:po + 512]),
                reads=["pT_A"], writes=["xhp"])
        tap(f"xh{c}", xh[:, :], ["xh"])
        emit_router()
        zsl = small[:, SM["red"]:SM["red"] + 64]
        t64 = small[:, SM["tdec"]:SM["tdec"] + 64]
        kf64 = small[:, SM["rt1"]:SM["rt1"] + 64]
        S.op(G_, lambda e: e.tensor_copy(zsl[:, 0:32], cum[:, 32:64]),
             reads=["cum"], writes=["sm_red", "sm_kf"])
        S.op(G_, lambda e: e.tensor_scalar(zsl[:, 32:64], cum[:, 32:64],
                                           HALF_PI, None, ALU.add),
             reads=["cum"], writes=["sm_red", "sm_kf"])
        S.op(G_, lambda e: e.tensor_scalar(t64, zsl, 1.0 / TWO_PI, None,
                                           ALU.mult),
             reads=["sm_red", "sm_kf"], writes=["sm_tdec", "sm_ang"])
        S.op(V, lambda e: e.tensor_copy(ktmp[:, :], t64),
             reads=["sm_tdec", "sm_ang"], writes=["ktmp"])
        S.op(V, lambda e: e.tensor_copy(kf64, ktmp[:, :]),
             reads=["ktmp"], writes=["sm_rt1", "sm_rt2"])
        S.op(V, lambda e: e.scalar_tensor_tensor(zsl, kf64, -TWO_PI,
                                                 zsl, ALU.mult, ALU.add),
             reads=["sm_rt1", "sm_rt2", "sm_red", "sm_kf"],
             writes=["sm_red", "sm_kf"])
        S.op(G_, lambda e: e.tensor_scalar(zsl, zsl, PI, -PI,
                                           ALU.min, ALU.max),
             reads=["sm_red", "sm_kf"], writes=["sm_red", "sm_kf"])
        S.op(A, lambda e: e.activation(scb[:, :], zsl, AF.Sin),
             reads=["sm_red", "sm_kf"], writes=["sinb", "cosb"])

        vsrc = YG[:, 384:768].bitcast(F32).rearrange(
            "p (g tw h) -> p g tw h", tw=2, h=32)
        bcrv = bcr[:, :].rearrange("p (g tw h) -> p g tw h", tw=2, h=32)
        cos_e = cosb.unsqueeze(1).broadcast_to((128, 3, 32))
        sin_e = sinb.unsqueeze(1).broadcast_to((128, 3, 32))
        rr1 = ysc[:, 0:96].rearrange("p (g h) -> p g h", h=32)
        rr2 = ysc[:, 96:192].rearrange("p (g h) -> p g h", h=32)
        S.op(V, lambda e: e.tensor_tensor(rr1, vsrc[:, :, 0, :], cos_e,
                                          ALU.mult),
             reads=["YG", "cosb"], writes=["ysc"])
        S.op(V, lambda e: e.tensor_tensor(rr2, vsrc[:, :, 1, :], sin_e,
                                          ALU.mult),
             reads=["YG", "sinb"], writes=["ysc"])
        S.op(V, lambda e: e.tensor_tensor(bcrv[:, :, 0, :], rr1, rr2,
                                          ALU.add),
             reads=["ysc"], writes=["brt", "crt"])
        S.op(V, lambda e: e.tensor_tensor(rr1, vsrc[:, :, 1, :], cos_e,
                                          ALU.mult),
             reads=["YG", "cosb"], writes=["ysc"])
        S.op(V, lambda e: e.tensor_tensor(rr2, vsrc[:, :, 0, :], sin_e,
                                          ALU.mult),
             reads=["YG", "sinb"], writes=["ysc"])
        S.op(V, lambda e: e.tensor_tensor(bcrv[:, :, 1, :], rr1, rr2,
                                          ALU.subtract),
             reads=["ysc"], writes=["brt", "crt"])
        tap(f"Br{c}", brt[:, :], ["brt"])
        tap(f"Cr{c}", crt[:, :], ["crt"])

        S.dma("gpsimd", bbt[1:128, :], brt[0:127, :], reads=["brt"],
              writes=["bbt"])

        S.op(TE, lambda e: e.transpose(pTB[:, 0:128], brt[:, :], id_b[:, :]),
             reads=["brt", "id_b"], writes=["pTB"])
        S.op(A, lambda e, bpar=bpar: e.copy(BrT[bpar][:, 1:129],
                                            pTB[:, 0:128]),
             reads=["pTB"], writes=[f"BrT{bpar}"])
        if c > 0:
            S.op(V, lambda e, bpar=bpar: e.tensor_copy(
                BrT[bpar][:, 0:1], BrT[1 - bpar][:, 128:129]),
                reads=[f"BrT{1 - bpar}"], writes=[f"BrT{bpar}"])
        S.op(TE, lambda e: e.transpose(pTB[0:64, 128:256], crt[:, :], id_b[:, :]),
             reads=["crt", "id_b"], writes=["pTB"])
        S.op(TE, lambda e: e.transpose(pTB[64:128, 128:256], crt[:, :],
                                       id_b[:, :]),
             reads=["crt", "id_b"], writes=["pTB"])
        S.op(A, lambda e: e.copy(CrTz[0][0:64, :], pTB[0:64, 128:256]),
             reads=["pTB"], writes=["CrTz0"])
        S.op(A, lambda e: e.copy(CrTz[1][64:128, :], pTB[64:128, 128:256]),
             reads=["pTB"], writes=["CrTz1"])

        for si in range(2):
            for r in range(2):
                co = (si * 2 + r) * 128
                lo = 1 - si
                S.op(TE, lambda e, r=r, lo=lo, co=co, bpar=bpar: e.matmul(
                    pT[:, co:co + 128],
                    BrT[bpar][:, lo:lo + 128],
                    CrTz[r][:, :], start=True, stop=True),
                    reads=[f"BrT{bpar}", f"CrTz{r}"], writes=["pT_A"])
                S.op(V, lambda e, co=co: e.tensor_tensor(
                    sgT[:, co:co + 128], pT[:, co:co + 128], masksc[:, :],
                    ALU.mult),
                    reads=["pT_A", "masksc"], writes=["sgT"])
        tap(f"sgT{c}", sgT[:, :], ["sgT"])
        tap(f"bbt{c}", bbt[:, :], ["bbt"])


        S.op(TE, lambda e: e.matmul(pT[:, MB + 64:MB + 96], onepad[:, :],
                                    clhl[:, 0:32], start=True, stop=False),
             reads=["clhl", "onepad"], writes=["pT_M"])
        S.op(TE, lambda e: e.matmul(pT[:, MB + 64:MB + 96], onepad[:, :],
                                    clhl[:, 32:64], start=False, stop=True),
             reads=["clhl", "onepad"], writes=["pT_M"])
        S.op(V, lambda e: e.tensor_tensor(sm("tdec"), pT[:, MB + 64:MB + 96],
                                          cum[:, 0:32], ALU.subtract),
             reads=["pT_M", "cum"], writes=["sm_tdec"])
        S.op(A, lambda e: e.activation(sm("dec_end"), sm("tdec"), AF.Exp),
             reads=["sm_tdec"], writes=["sm_dec_end"])
        S.op(A, lambda e: e.activation(sm("dec_in"), cum[:, 0:32], AF.Exp),
             reads=["cum"], writes=["sm_dec_in"])
        S.op(G_, lambda e: e.tensor_scalar(sm("dec_in"), sm("dec_in"), SCALE,
                                          None, ALU.mult),
             reads=["sm_dec_in"], writes=["sm_dec_in"])
        S.op(G_, lambda e: e.tensor_scalar(sm("tdec"), sm("dec_in"), -1.0,
                                          None, ALU.add),
             reads=["sm_dec_in"], writes=["sm_tdec"])
        tap(f"dec_in{c}", sm("dec_in"), ["sm_dec_in"])
        tap(f"dec_end{c}", sm("dec_end"), ["sm_dec_end"])
        S.op(G_, lambda e: e.tensor_tensor(sm("scal_gd"), sm("scal_g"),
                                          sm("dec_end"), ALU.mult),
             reads=["sm_scal_g", "sm_dec_end"], writes=["sm_scal_gd"])
        S.op(G_, lambda e: e.tensor_tensor(sm("scal_bd"), sm("scal_b"),
                                          sm("dec_end"), ALU.mult),
             reads=["sm_scal_b", "sm_dec_end"], writes=["sm_scal_bd"])
        S.op(A, lambda e: e.activation(cdrow[:, :], clast[:, 0:32], AF.Exp),
             reads=["clast"], writes=["cdrow"])
        S.op(G_, lambda e: e.tensor_copy(cdhl[0:1, 0:32], cdrow[:, :]),
             reads=["cdrow"], writes=["cdhl"])
        S.op(V, lambda e: e.tensor_tensor(cdhl[0:1, 32:64], cdrow[:, :],
                                          cdhl[0:1, 0:32], ALU.subtract),
             reads=["cdrow", "cdhl"], writes=["cdhl"])
        S.op(TE, lambda e: e.matmul(pT[:, MB + 96:MB + 128],
                                    onepad[:, :], cdhl[:, 0:32],
                                    start=True, stop=False),
             reads=["cdhl", "onepad"], writes=["pT_M"])
        S.op(TE, lambda e: e.matmul(pT[:, MB + 96:MB + 128],
                                    onepad[:, :], cdhl[:, 32:64],
                                    start=False, stop=True),
             reads=["cdhl", "onepad"], writes=["pT_M"])
        S.op(A, lambda e: e.copy(CD[:, :], pT[0:64, MB + 96:MB + 128]),
             reads=["pT_M"], writes=["CD"])
        def emit_bc(g, dpar):
            cg, rg = cumg[dpar], f"cumg{dpar}"
            cgh, rgh = cumgh[dpar], f"cumgh{dpar}"
            cgl, rgl = cumgl[dpar], f"cumgl{dpar}"
            S.op(TE, lambda e, g=g: e.transpose(
                pT[0:8, MB + 128:MB + 256], cum[:, g * 8:(g + 1) * 8],
                id_f[:, :]),
                reads=["cum", "id_f"], writes=["pT_M"])
            S.op(A, lambda e, cg=cg: e.copy(cg[:, :],
                                            pT[0:8, MB + 128:MB + 256]),
                 reads=["pT_M"], writes=[rg])
            S.op(G_, lambda e, cg=cg, cgh=cgh: e.tensor_copy(cgh[0:8, :],
                                                             cg[:, :]),
                 reads=[rg], writes=[rgh])
            S.op(G_, lambda e, cg=cg, cgh=cgh, cgl=cgl: e.tensor_tensor(
                cgl[0:8, :], cg[:, :], cgh[0:8, :], ALU.subtract),
                reads=[rg, rgh], writes=[rgl])
            mask_b = maskneg[:, :].unsqueeze(1).broadcast_to((128, 4, 128))
            for half in range(2):
                hsl = slice(half * 512, (half + 1) * 512)
                pvm = pT[:, hsl].rearrange("p (h l) -> p h l", h=4)
                S.op(TE, lambda e, pvm=pvm, mask_b=mask_b: e.matmul(
                    pvm, id_b[:, :], mask_b,
                    start=True, stop=False),
                    reads=["maskneg", "id_b"], writes=["pT_A"])
                for hh in range(4):
                    hr = half * 4 + hh
                    co = half * 512 + hh * 128
                    sel = selrow[:, hr * 128:(hr + 1) * 128]
                    S.op(TE, lambda e, sel=sel, co=co, cgh=cgh: e.matmul(
                        pT[:, co:co + 128], sel,
                        cgh[:, :], start=False, stop=False),
                        reads=[rgh, "selrow"], writes=["pT_A"])
                    S.op(TE, lambda e, sel=sel, co=co, cgl=cgl: e.matmul(
                        pT[:, co:co + 128], sel,
                        cgl[:, :], start=False, stop=False),
                        reads=[rgl, "selrow"], writes=["pT_A"])
                nce_h = nchl[:, g * 8 + half * 4:g * 8 + half * 4 + 4
                             ].unsqueeze(2).broadcast_to((128, 4, 128))
                nce_l = nchl[:, 32 + g * 8 + half * 4:32 + g * 8 + half * 4
                             + 4].unsqueeze(2).broadcast_to((128, 4, 128))
                S.op(TE, lambda e, pvm=pvm, nce_h=nce_h: e.matmul(
                    pvm, id_b[:, :], nce_h, start=False, stop=False),
                    reads=["nchl", "id_b"], writes=["pT_A"])
                S.op(TE, lambda e, pvm=pvm, nce_l=nce_l: e.matmul(
                    pvm, id_b[:, :], nce_l, start=False, stop=True),
                    reads=["nchl", "id_b"], writes=["pT_A"])
            S.op(A, lambda e, dpar=dpar: e.activation(
                dgT[dpar][:, :], pT[:, 0:1024], AF.Exp),
                reads=["pT_A"], writes=[f"dgT{dpar}"])

        for g in range(4):
            dpar = g % 2
            g8 = slice(g * 8, (g + 1) * 8)
            gsl = slice(g * 1024, (g + 1) * 1024)
            g512 = slice(g * 512, (g + 1) * 512)
            if g == 0 and not UNPIPE:
                emit_bc(0, 0)
            if UNPIPE:
                emit_bc(g, dpar)
            def xvar(eng, dst, dreg, src, sreg, key, g=g, gsl=gsl):
                sc = small[:, SM[key] + g * 8:SM[key] + g * 8 + 8]
                sce = sc.unsqueeze(2).broadcast_to((128, 8, 128))
                S.op(eng, lambda e, sce=sce, dst=dst, src=src, gsl=gsl:
                     e.tensor_tensor(
                         dst[:, :].rearrange("p (h q) -> p h q", h=8),
                         src[:, gsl].rearrange("p (h q) -> p h q", h=8),
                         sce, ALU.mult),
                     reads=[sreg, "sm_" + key], writes=[dreg])

            xvar(V, xga, "xga", xh, "xh", "scal_g")
            xvar(V, xba, "xba", xhp, "xhp", "scal_b")
            if g == 0:
                tap(f"xga{c}", xga[:, :], ["xga"])
                tap(f"xba{c}", xba[:, :], ["xba"])


            if g == 0:
                tap(f"dgT{c}", dgT[0][:, :], ["dgT0"])

            def wbuild(eng, dst, dreg, co):
                sge = sgT[:, co:co + 128].unsqueeze(1).broadcast_to(
                    (128, 8, 128))
                S.op(eng, lambda e, sge=sge, dst=dst, dpar=dpar:
                     e.tensor_tensor(
                         dst[:, :].rearrange("p (h l) -> p h l", h=8),
                         sge, dgT[dpar][:, :].rearrange("p (h l) -> p h l",
                                                        h=8),
                         ALU.mult),
                     reads=["sgT", f"dgT{dpar}"], writes=[dreg])

            if g + 1 < 4 and not UNPIPE:
                emit_bc(g + 1, (g + 1) % 2)
            S.op(A, lambda e, g512=g512: e.copy(sbfg[0:64, :], Sst[:, g512]),
                 reads=["Sst"], writes=["sbfg"])
            if g == 0:
                tap(f"sbfg{c}", sbfg[:, :], ["sbfg"])
            S.op(TE, lambda e: e.matmul(
                pY[:, 0:512], CrTz[0][:, :], sbfg[:, 0:512],
                start=True, stop=False),
                reads=["CrTz0", "sbfg"], writes=["pY"])
            dece = small[:, SM["tdec"] + g * 8:SM["tdec"] + g * 8 + 8
                         ].unsqueeze(2).broadcast_to((128, 8, 64))
            S.op(V, lambda e, dece=dece: e.tensor_tensor(
                ysc[:, :].rearrange("p (h q) -> p h q", h=8),
                pY[:, :].rearrange("p (h q) -> p h q", h=8),
                dece, ALU.mult),
                reads=["pY", "sm_tdec"], writes=["ysc"])
            if g == 0:
                tap(f"ysc{c}", ysc[:, :], ["ysc"])
            for r in range(2):
                wbuild(V, Wg, "Wg", r * 128)
                wbuild(V, Wb, "xnT", 256 + r * 128)
                if g == 0 and r == 0:
                    tap(f"Wg{c}", Wg[:, :], ["Wg"])
                    tap(f"Wb{c}", Wb[:, :], ["xnT"])
                for hh in range(8):
                    S.op(TE, lambda e, hh=hh, r=r: e.matmul(
                        pY[:, hh * 64:(hh + 1) * 64],
                        Wg[:, hh * 128:(hh + 1) * 128],
                        xga[:, hh * 128 + r * 64:hh * 128 + r * 64 + 64],
                        start=False, stop=False),
                        reads=["Wg", "xga", "ysc"], writes=["pY"])
                    S.op(TE, lambda e, hh=hh, r=r: e.matmul(
                        pY[:, hh * 64:(hh + 1) * 64],
                        Wb[:, hh * 128:(hh + 1) * 128],
                        xba[:, hh * 128 + r * 64:hh * 128 + r * 64 + 64],
                        start=False, stop=(r == 1 and hh == 7)),
                        reads=["xnT", "xba"], writes=["pY"])
            xvar(V, xga, "xga", xh, "xh", "scal_gd")
            xvar(V, xba, "xba", xhp, "xhp", "scal_bd")
            if g == 0:
                tap(f"xgd{c}", xga[:, :], ["xga"])
                tap(f"xbd{c}", xba[:, :], ["xba"])
            S.op(V, lambda e: e.tensor_tensor(ysum[:, :], pY[:, :], ysc[:, :],
                                              ALU.add),
                 reads=["pY", "ysc"], writes=["Wg"])
            if g == 0:
                tap(f"ysum{c}", ysum[:, :], ["Wg"])
            S.op(V, lambda e, g512=g512: e.tensor_tensor(
                YG[:, g512], ysum[:, :], sz[:, g512], ALU.mult),
                reads=["Wg", "sz"], writes=["YG"])
            pSTv = pST[:, :].rearrange("p (h q) -> p h q", h=8)
            for si in range(2):
                lb, lreg = (brt, "brt") if si == 0 else (bbt, "bbt")
                xv, xr2 = (xga, "xga") if si == 0 else (xba, "xba")
                for r in range(2):
                    st_f = (si == 0 and r == 0)
                    sp_f = (si == 1 and r == 1)
                    xvv = xv[:, :].rearrange("p (h q) -> p h q",
                                             h=8)[:, :, r * 64:(r + 1) * 64]
                    S.op(TE, lambda e, r=r, lb=lb, xvv=xvv, st_f=st_f,
                         sp_f=sp_f: e.matmul(
                             pSTv, lb[:, r * 64:(r + 1) * 64],
                             xvv, start=st_f, stop=sp_f),
                         reads=[lreg, xr2], writes=["pST"])
            cde = CD[:, g * 8:(g + 1) * 8].unsqueeze(2).broadcast_to(
                (64, 8, 64))
            S.op(V, lambda e, cde=cde, g512=g512: e.tensor_tensor(
                stmp[:, :].rearrange("p (h q) -> p h q", h=8),
                Sst[:, g512].rearrange("p (h q) -> p h q", h=8),
                cde, ALU.mult),
                reads=["Sst", "CD"], writes=["xga"])
            S.op(V, lambda e, g512=g512: e.tensor_tensor(
                Sst[:, g512], stmp[:, :], pST[:, :], ALU.add),
                reads=["xga", "pST"], writes=["Sst"])
            if g == 3:
                tap(f"Sst{c}", Sst[:, :], ["Sst"])
        tap(f"y{c}", YG[:, :], ["YG"])
        S.dma("gpsimd", bbt[0:1, :], brt[127:128, :], reads=["brt"],
              writes=["bbt"])

        S.op(A, lambda e: e.activation(sz[:, 0:2048], YG[:, :], AF.Square,
                                       accum_out=smcol[:, 13:14]),
             reads=["YG"], writes=["smc13", "sz"])
        S.op(A, lambda e: e.activation(smcol[:, 12:13], smcol[:, 13:14],
                                       AF.Ln, scale=1.0 / DI,
                                       bias=cb_ln[:, 2:3]),
             reads=["smc13", "cb_ln"], writes=["smc12"])
        S.op(A, lambda e: e.activation(smcol[:, 12:13], smcol[:, 12:13],
                                       AF.Exp, scale=-0.5),
             reads=["smc12"], writes=["smc12"])
        S.op(V, lambda e: e.tensor_scalar(sz[:, :], YG[:, :],
                                          smcol[:, 12:13], None, ALU.mult),
             reads=["YG", "smc12"], writes=["sz"])
        tap(f"ynorm{c}", sz[:, :], ["sz"])
        if c + 1 < NCHUNK:
            emit_ln(c + 1)
        for half in range(2):
            for kk in range(8):
                k = half * 8 + kk
                S.op(TE, lambda e, kk=kk, k=k: e.transpose(
                    pTB[:, kk * 128:(kk + 1) * 128],
                    sz[:, k * 128:(k + 1) * 128], id_b[:, :]),
                    reads=["sz", "id_b"], writes=["pTB"])
            S.op(A, lambda e: e.copy(ynT[:, :], pTB[:, 0:1024]),
                 reads=["pTB"], writes=["xga"])
            for fg in range(2):
                pa, preg = pA[fg], f"pA{fg}"
                for kk in range(8):
                    k = half * 8 + kk
                    S.op(TE, lambda e, kk=kk, k=k, fg=fg, pa=pa: e.matmul(
                        pa[:, :], ynT[:, kk * 128:(kk + 1) * 128],
                        WOUT[:, k * 1024 + fg * 512:k * 1024 + fg * 512
                             + 512],
                        start=(k == 0), stop=(k == 15)),
                        reads=["xga", f"WOUT{k // 4}"], writes=[preg])
        for fg in range(2):
            pa, preg = pA[fg], f"pA{fg}"
            S.op(A, lambda e, fg=fg, pa=pa: e.copy(
                s_sb[:, fg * 512:(fg + 1) * 512], pa[:, :]),
                reads=[preg, "sz"], writes=["sz"])
        tap(f"s{c}", s_sb[:, :], ["sz"])
        S.dma("sync", s_d[c * 128:(c + 1) * 128, :], s_sb[:, :],
              reads=["sz"], writes=[f"sd{c}"])

    if DO_FFN:
        S.dma("gpsimd", gtrow[:, :], ccout_d.flatten().unsqueeze(0),
              reads=["ccout"], writes=["gtrow"])
        S.op(V, lambda e: e.tensor_tensor(gtrow[:, :], gtrow[:, :],
                                          cmask[:, :], ALU.mult),
             reads=["gtrow", "cmask"], writes=["gtrow"])
        S.op(V, lambda e: e.tensor_reduce(
            myoff[:, :].unsqueeze(2),
            gtrow[:, :].rearrange("o (a e) -> o e a", a=8), axis=AX.X,
            op=ALU.add),
            reads=["gtrow"], writes=["myoff"])
        moff_b = myoff[:, :].unsqueeze(1).broadcast_to((1, 32, 8))
        S.op(V, lambda e: e.tensor_tensor(
            OFFrow[:, :].rearrange("o (cc ee) -> o cc ee", ee=8),
            OFFrow[:, :].rearrange("o (cc ee) -> o cc ee", ee=8),
            moff_b, ALU.add),
            reads=["OFFrow", "myoff"], writes=["OFFrow"])
        S.op(V, lambda e: e.tensor_copy(OFFrh[:, :], OFFrow[:, :]),
             reads=["OFFrow"], writes=["OFFrh"])
        S.op(V, lambda e: e.tensor_tensor(OFFrl[:, :], OFFrow[:, :],
                                          OFFrh[:, :], ALU.subtract),
             reads=["OFFrow", "OFFrh"], writes=["OFFrl"])
        S.op(V, lambda e: e.tensor_copy(myoh[:, :], myoff[:, :]),
             reads=["myoff"], writes=["myoh"])
        S.op(V, lambda e: e.tensor_tensor(myol[:, :], myoff[:, :],
                                          myoh[:, :], ALU.subtract),
             reads=["myoff", "myoh"], writes=["myol"])
        S.op(TE, lambda e: e.matmul(pT[:, MB:MB + 8], ones_b[:, 0:128],
                                    myoh[:, :], start=True, stop=False),
             reads=["myoh", "ones_b"], writes=["pT_M"])
        S.op(TE, lambda e: e.matmul(pT[:, MB:MB + 8], ones_b[:, 0:128],
                                    myol[:, :], start=False, stop=True),
             reads=["myol", "ones_b"], writes=["pT_M"])
        S.op(V, lambda e: e.tensor_copy(MYOFF[:, :], pT[:, MB:MB + 8]),
             reads=["pT_M"], writes=["MYOFF"])

    ctx1.close()
    if DO_FFN:
        _build_ffn(nc, S, ctx, sb, locals())

    S.finalize()
    block = ctx.enter_context(nc.Block())
    S.build(nc, block, ctx)
    ctx.close()
    return nc


def _build_ffn(nc, S, ctx, sb, env):
    V, A, G_, TE = "vector", "scalar", "gpsimd", "tensor"
    pT, pTB, pA, pY = env["pT"], env["pTB"], env["pA"], env["pY"]
    ones_b, triu32, id_f, id_b = (env["ones_b"], env["triu32"], env["id_f"],
                                  env["id_b"])
    SCNT, IDX, GATE, iota8, cmask = (env["SCNT"], env["IDX"], env["GATE"],
                                     env["iota8"], env["cmask"])
    emk, xc = env["emk"], env["xc"]
    xn_d, s_d, eb_d, eo_d = env["xn_d"], env["s_d"], env["eb_d"], env["eo_d"]
    x_h, out_h = env["x_h"], env["out_h"]
    w1_h, b1_h, w2_h, b2_h = env["w1_h"], env["b1_h"], env["w2_h"], env["b2_h"]
    ebz_keys, tap = env["ebz_keys"], env["tap"]
    OFFrh, OFFrl, MYOFF = env["OFFrh"], env["OFFrl"], env["MYOFF"]
    MB = 1024
    S.auto_extra = [S.fence()]

    DEST_u = sb("DEST_u", (128, 32), U32)
    GSRC_u = sb("GSRC_u", (128, 32), U32)
    GATEM = sb("GATEM", (128, 32))

    emaskall = sb("emaskall", (128, 256), BF16)
    t256 = sb("t256", (128, 256))
    POS = sb("POS", (128, 32))
    MYT = sb("MYT", (128, 32))
    SLOT = sb("SLOT", (128, 32))
    KEEP2 = sb("KEEP2", (128, 32))
    D1 = sb("D1", (128, 32))
    S.op(TE, lambda e: e.matmul(pT[:, MB:MB + 256], ones_b[:, 0:128],
                                OFFrh[:, :], start=True, stop=False),
         reads=["OFFrh", "ones_b"], writes=["pT_M"])
    S.op(TE, lambda e: e.matmul(pT[:, MB:MB + 256], ones_b[:, 0:128],
                                OFFrl[:, :], start=False, stop=True),
         reads=["OFFrl", "ones_b"], writes=["pT_M"])
    idxe = IDX[:, :].unsqueeze(2).broadcast_to((128, 32, 8))
    iote = iota8[:, :].unsqueeze(1).broadcast_to((128, 32, 8))
    emv = emaskall[:, :].rearrange("p (t e) -> p t e", e=8)
    S.op(V, lambda e: e.tensor_tensor(emv, idxe, iote, ALU.is_equal),
         reads=["IDX", "iota8"], writes=["emaskall"])
    S.op(V, lambda e: e.tensor_tensor(t256[:, :], SCNT[:, :],
                                      pT[:, MB:MB + 256], ALU.add),
         reads=["SCNT", "pT_M"], writes=["t256"])
    S.op(V, lambda e: e.tensor_tensor(t256[:, :], t256[:, :], emaskall[:, :],
                                      ALU.mult),
         reads=["t256", "emaskall"], writes=["t256"])
    S.op(V, lambda e: e.tensor_reduce(
        POS[:, :].unsqueeze(2), t256[:, :].rearrange("p (t e) -> p t e", e=8),
        axis=AX.X, op=ALU.add),
        reads=["t256"], writes=["POS"])
    mye = MYOFF[:, :].unsqueeze(1).broadcast_to((128, 32, 8))
    S.op(V, lambda e: e.tensor_tensor(
        t256[:, :].rearrange("p (t e) -> p t e", e=8), mye,
        emaskall[:, :].rearrange("p (t e) -> p t e", e=8), ALU.mult),
        reads=["MYOFF", "emaskall"], writes=["t256"])
    S.op(V, lambda e: e.tensor_reduce(
        MYT[:, :].unsqueeze(2), t256[:, :].rearrange("p (t e) -> p t e", e=8),
        axis=AX.X, op=ALU.add),
        reads=["t256"], writes=["MYT"])
    S.op(V, lambda e: e.scalar_tensor_tensor(SLOT[:, :], POS[:, :], -1.0,
                                             MYT[:, :], ALU.add,
                                             ALU.subtract),
         reads=["POS", "MYT"], writes=["SLOT"])
    S.op(V, lambda e: e.tensor_scalar(KEEP2[:, :], SLOT[:, :], CAP - 0.5,
                                      None, ALU.is_le),
         reads=["SLOT"], writes=["KEEP2"])
    S.op(V, lambda e: e.tensor_scalar(t256[:, 0:32], POS[:, :], 4096.0, None,
                                      ALU.is_le),
         reads=["POS"], writes=["t256"])
    S.op(V, lambda e: e.tensor_tensor(KEEP2[:, :], KEEP2[:, :], t256[:, 0:32],
                                      ALU.mult),
         reads=["KEEP2", "t256"], writes=["KEEP2"])
    S.op(V, lambda e: e.scalar_tensor_tensor(D1[:, :], IDX[:, :], float(CAP),
                                             SLOT[:, :], ALU.mult, ALU.add),
         reads=["IDX", "SLOT"], writes=["D1"])
    S.op(V, lambda e: e.tensor_scalar(t256[:, 0:32], KEEP2[:, :], -BIG, BIG,
                                      ALU.mult, ALU.add),
         reads=["KEEP2"], writes=["t256"])
    S.op(V, lambda e: e.tensor_tensor(t256[:, 32:64], D1[:, :], t256[:, 0:32],
                                      ALU.add),
         reads=["D1", "t256"], writes=["t256"])
    S.op(V, lambda e: e.tensor_copy(DEST_u[:, :], t256[:, 32:64]),
         reads=["t256"], writes=["DEST_u"])
    S.op(V, lambda e: e.tensor_tensor(t256[:, 0:32], D1[:, :], KEEP2[:, :],
                                      ALU.mult),
         reads=["D1", "KEEP2"], writes=["t256"])
    S.op(V, lambda e: e.tensor_copy(GSRC_u[:, :], t256[:, 0:32]),
         reads=["t256"], writes=["GSRC_u"])
    S.op(V, lambda e: e.tensor_tensor(GATEM[:, :], GATE[:, :], KEEP2[:, :],
                                      ALU.mult),
         reads=["GATE", "KEEP2"], writes=["GATEM"])
    tap("gatem", GATEM[:, :], ["GATEM"])
    tap("destu", DEST_u[:, :], ["DEST_u"])
    tap("gsrcu", GSRC_u[:, :], ["GSRC_u"])
    tap("posx", POS[:, :], ["POS"])
    tap("keep2", KEEP2[:, :], ["KEEP2"])

    xsc = [sb(f"xsc{i}", (128, D), BF16) for i in range(4)]
    for t in range(32):
        p4 = t % 4
        S.dma("sync", xsc[p4][:, :], xn_d[t * 128:(t + 1) * 128, :],
              reads=[f"xnd{t}"], writes=[f"xsc{p4}"])
        S.indirect(eb_d, bass.IndirectOffsetOnAxis(ap=DEST_u[:, t:t + 1],
                                                   axis=0),
                   xsc[p4][:, :], None,
                   reads=[f"xsc{p4}", "DEST_u"] + ebz_keys,
                   writes=[f"eb{t}"])
    eb_keys = [f"eb{t}" for t in range(32)] + ebz_keys

    CT = CAP // 128
    XeT = sb("XeT", (128, 8 * CAP), BF16)
    hT = sb("hT", (128, 32 * CAP), BF16)
    w1sb = [sb("w1sb0", (128, 4096), BF16), sb("w1sb1", (128, 4096), BF16)]
    w2res = sb("w2res", (128, 32 * 1024), BF16)
    b1c32 = sb("b1c32", (32, 128))
    b1cols = sb("b1cols", (128, 32))
    b2row = sb("b2row", (1, 1024))
    b2h = sb("b2h", (128, 1024), BF16)
    b2l = sb("b2l", (128, 1024), BF16)
    onepad = env["onepad"]
    S.op(V, lambda e: e.memset(b2h[:, :], 0.0), writes=["b2h"])
    S.op(V, lambda e: e.memset(b2l[:, :], 0.0), writes=["b2l"])
    outsb = [sb("outsb0", (128, D), BF16), sb("outsb1", (128, D), BF16)]
    xe_sb = [sb("xesb0", (128, D), BF16), sb("xesb1", (128, D), BF16)]

    psl = [(pT, 0, "pT_A"), (pT, 512, "pT_A"), (pA[0], 0, "pA0"),
           (pA[1], 0, "pA1"), (pY, 0, "pY"), (pT, 1024, "pT_M")]
    w1g = w1_h.rearrange("ee (k p) (tg j) -> ee p k tg j", p=128, j=512)
    XeT_v = XeT[:, :].rearrange("p (k cc) -> p k cc", k=8)

    def w1load(ex, tg):
        p2 = tg % 2
        S.dma("gpsimd", w1sb[p2][:, :].rearrange("p (k j) -> p k j", k=8),
              w1g[ex, :, :, tg, :], writes=[f"w1sb{p2}"])

    for ex in range(NEXP):
        w1load(ex, 0)
        S.dma("sync", b1c32[:, :],
              b1_h[ex:ex + 1, :].rearrange("o (t p) -> o t p", p=128
                                           ).squeeze(0),
              writes=["b1c32"])
        S.op(TE, lambda e: e.transpose(pT[:, MB:MB + 32], b1c32[:, :],
                                       id_f[0:32, 0:32]),
             reads=["b1c32", "id_f"], writes=["pT_M"])
        S.op(V, lambda e: e.tensor_copy(b1cols[:, :], pT[:, MB:MB + 32]),
             reads=["pT_M"], writes=["b1cols"])
        S.dma("sync", b2row[:, :], b2_h[ex:ex + 1, :], writes=["b2row"])
        S.op(V, lambda e: e.tensor_copy(b2h[0:1, :], b2row[:, :]),
             reads=["b2row"], writes=["b2h"])
        S.op(V, lambda e: e.tensor_tensor(b2l[0:1, :], b2row[:, :],
                                          b2h[0:1, :], ALU.subtract),
             reads=["b2row", "b2h"], writes=["b2l"])
        for i in range(CT):
            p2 = i % 2
            S.dma("sync", xe_sb[p2][:, :],
                  eb_d[ex * CAP + i * 128:ex * CAP + (i + 1) * 128, :],
                  reads=eb_keys, writes=[f"xesb{p2}"])
            for k in range(8):
                S.op(TE, lambda e, k=k, p2=p2: e.transpose(
                    pTB[:, k * 128:(k + 1) * 128],
                    xe_sb[p2][:, k * 128:(k + 1) * 128], id_b[:, :]),
                    reads=[f"xesb{p2}", "id_b"], writes=["pTB"])
            S.op(V, lambda e, i=i: e.tensor_copy(
                XeT_v[:, :, i * 128:(i + 1) * 128],
                pTB[:, 0:1024].rearrange("p (k q) -> p k q", k=8)),
                reads=["pTB"], writes=["XeT"])
        for tg in range(8):
            p2 = tg % 2
            if tg + 1 < 8:
                w1load(ex, tg + 1)
            for ftl in range(4):
                ft = tg * 4 + ftl
                for half, hw in ((0, 512), (1, CAP - 512)):
                    pa, preg = pA[half], f"pA{half}"
                    hsl = slice(half * 512, half * 512 + hw)
                    for k in range(8):
                        S.op(TE, lambda e, k=k, ftl=ftl, hsl=hsl, p2=p2,
                             pa=pa, hw=hw:
                             e.matmul(pa[:, 0:hw],
                                      w1sb[p2][:, k * 512 + ftl * 128:
                                               k * 512 + (ftl + 1) * 128],
                                      XeT_v[:, k, hsl],
                                      start=(k == 0), stop=(k == 7)),
                             reads=[f"w1sb{p2}", "XeT"], writes=[preg])
                    S.op(A, lambda e, ft=ft, hsl=hsl, hw=hw, pa=pa:
                         e.activation(
                             hT[:, ft * CAP + hsl.start:
                                ft * CAP + hsl.start + hw],
                             pa[:, 0:hw], AF.Relu, bias=b1cols[:, ft:ft + 1]),
                         reads=[preg, "b1cols"], writes=["hT"])
        for k in range(32):
            S.dma("gpsimd", w2res[:, k * 1024:(k + 1) * 1024],
                  w2_h[ex, k * 128:(k + 1) * 128, :], writes=[f"w2res{k}"])
        for ct in range(CT):
            p2 = ct % 2
            for dh in range(2):
                ps, pco, preg = psl[(ct % 3) * 2 + dh]
                S.op(TE, lambda e, ps=ps, pco=pco, dh=dh: e.matmul(
                    ps[:, pco:pco + 512], onepad[:, :],
                    b2h[:, dh * 512:(dh + 1) * 512], start=True, stop=False),
                    reads=["b2h", "onepad"], writes=[preg])
                S.op(TE, lambda e, ps=ps, pco=pco, dh=dh: e.matmul(
                    ps[:, pco:pco + 512], onepad[:, :],
                    b2l[:, dh * 512:(dh + 1) * 512], start=False, stop=False),
                    reads=["b2l", "onepad"], writes=[preg])
                for k in range(32):
                    S.op(TE, lambda e, ps=ps, pco=pco, k=k, ct=ct, dh=dh:
                         e.matmul(
                             ps[:, pco:pco + 512],
                             hT[:, k * CAP + ct * 128:
                                k * CAP + (ct + 1) * 128],
                             w2res[:, k * 1024 + dh * 512:
                                   k * 1024 + dh * 512 + 512],
                             start=False, stop=(k == 31)),
                         reads=["hT", f"w2res{k}"], writes=[preg])
            for dh in range(2):
                ps, pco, preg = psl[(ct % 3) * 2 + dh]
                S.op(V, lambda e, ps=ps, pco=pco, dh=dh, p2=p2: e.tensor_copy(
                    outsb[p2][:, dh * 512:(dh + 1) * 512],
                    ps[:, pco:pco + 512]),
                    reads=[preg], writes=[f"outsb{p2}"])
            S.dma("sync",
                  eo_d[ex * CAP + ct * 128:ex * CAP + (ct + 1) * 128, :],
                  outsb[p2][:, :], reads=[f"outsb{p2}"],
                  writes=[f"eo{ex}_{ct}"])
    eo_keys = [f"eo{ex}_{ct}" for ex in range(NEXP) for ct in range(CT)]

    if "ebdump" in DEBUG_TAPS:
        ebo = nc.dram_tensor("dbg_ebdump", [NEXP * CAP, D], BF16,
                             kind="ExternalOutput").ap()
        eoo = nc.dram_tensor("dbg_eodump", [NEXP * CAP, D], F32,
                             kind="ExternalOutput").ap()
        for i in range(NEXP * CAP // 512):
            sl_ = slice(i * 512, (i + 1) * 512)
            S.dma("sync", ebo[sl_, :], eb_d[sl_, :], reads=eb_keys)
            S.dma("sync", eoo[sl_, :], eo_d[sl_, :], reads=eo_keys)
    eo_sb = [outsb[0], outsb[1], sb("eosb2", (128, D), BF16),
             sb("eosb3", (128, D), BF16)]
    eo_rn = ["outsb0", "outsb1", "eosb2", "eosb3"]
    svb = [sb(f"svb{i}", (128, D)) for i in range(2)]
    gbuf = [sb(f"gbuf{i}", (128, D)) for i in range(2)]
    outb = [sb(f"outb{i}", (128, D)) for i in range(2)]
    for t in range(32):
        p4, p2 = t % 4, t % 2
        S.indirect(eo_sb[p4][:, :], None, eo_d,
                   bass.IndirectOffsetOnAxis(ap=GSRC_u[:, t:t + 1], axis=0),
                   reads=eo_keys + ["GSRC_u"], writes=[eo_rn[p4]]),
        tap(f"eo{t}", eo_sb[p4][:, :], [eo_rn[p4]])
        S.dma("sync", xc[p2][:, :], x_h[t * 128:(t + 1) * 128, :],
              writes=[f"xc{p2}"])
        S.dma("scalar", svb[p2][:, :], s_d[t * 128:(t + 1) * 128, :],
              reads=[f"sd{t}"], writes=[f"svb{p2}"])
        S.op(A, lambda e, t=t, p4=p4, p2=p2: e.activation(
            gbuf[p2][:, :], eo_sb[p4][:, :], AF.Silu,
            scale=GATEM[:, t:t + 1]),
            reads=[eo_rn[p4], "GATEM"], writes=[f"gbuf{p2}"])
        S.op(V, lambda e, p2=p2: e.tensor_tensor(
            gbuf[p2][:, :], gbuf[p2][:, :], svb[p2][:, :], ALU.mult),
            reads=[f"gbuf{p2}", f"svb{p2}"], writes=[f"gbuf{p2}"])
        S.op(V, lambda e, p2=p2: e.tensor_tensor(
            outb[p2][:, :], gbuf[p2][:, :], xc[p2][:, :], ALU.add),
            reads=[f"gbuf{p2}", f"xc{p2}"], writes=[f"outb{p2}"])
        S.dma("sync", out_h[t * 128:(t + 1) * 128, :], outb[p2][:, :],
              reads=[f"outb{p2}"], writes=[f"out{t}"])

    if "findump" in DEBUG_TAPS:
        outk = [f"out{t}" for t in range(32)]
        feo = nc.dram_tensor("dbg_feo", [NEXP * CAP, D], F32,
                             kind="ExternalOutput").ap()
        fsd = nc.dram_tensor("dbg_fsd", [T, D], F32,
                             kind="ExternalOutput").ap()
        fgs = nc.dram_tensor("dbg_fgs", [128, 32], U32,
                             kind="ExternalOutput").ap()
        fgm = nc.dram_tensor("dbg_fgm", [128, 32], F32,
                             kind="ExternalOutput").ap()
        fxc = nc.dram_tensor("dbg_fxc", [256, D], F32,
                             kind="ExternalOutput").ap()
        fes = nc.dram_tensor("dbg_fes", [256, D], F32,
                             kind="ExternalOutput").ap()
        for i in range(12):
            sl_ = slice(i * 512, (i + 1) * 512)
            S.dma("sync", feo[sl_, :], eo_d[sl_, :],
                  reads=eo_keys, extra=[S.regions["out31"][0]])
        for i in range(8):
            sl_ = slice(i * 512, (i + 1) * 512)
            S.dma("sync", fsd[sl_, :], s_d[sl_, :],
                  extra=[S.regions["out31"][0]])
        S.dma("sync", fgs[:, :], GSRC_u[:, :], reads=["GSRC_u"],
              extra=[S.regions["out31"][0]])
        S.dma("sync", fgm[:, :], GATEM[:, :], reads=["GATEM"],
              extra=[S.regions["out31"][0]])
        S.dma("sync", fxc[0:128, :], xc[0][:, :], reads=["xc0"],
              extra=[S.regions["out31"][0]])
        S.dma("sync", fxc[128:256, :], xc[1][:, :], reads=["xc1"],
              extra=[S.regions["out31"][0]])
        S.dma("sync", fes[0:128, :], eo_sb[0][:, :], reads=["eosb0"],
              extra=[S.regions["out31"][0]])
        S.dma("sync", fes[128:256, :], eo_sb[1][:, :], reads=["eosb1"],
              extra=[S.regions["out31"][0]])


def kernel(L0_attn_x, L0_attn_W_in, L0_attn_A_log, L0_attn_dt_bias,
           L0_attn_B_norm_bias, L0_attn_C_norm_bias, L0_attn_W_out,
           L0_ffn_router_w, L0_ffn_w1, L0_ffn_b1, L0_ffn_w2, L0_ffn_b2):
    x = np.ascontiguousarray(np.asarray(L0_attn_x, dtype=np.float32))
    consts = _consts()
    nc = build_program()
    in_maps = []
    for b in range(B):
        cm = np.zeros((1, 64), np.float32)
        cm[0, :b * 8] = 1.0
        m = dict(
            x=x[b],
            W_in=np.ascontiguousarray(np.asarray(L0_attn_W_in, np.float32)),
            A_log=np.asarray(L0_attn_A_log, np.float32).reshape(1, H),
            dt_bias=np.asarray(L0_attn_dt_bias, np.float32).reshape(1, H),
            B_nb=np.asarray(L0_attn_B_norm_bias, np.float32).reshape(1, N),
            C_nb=np.asarray(L0_attn_C_norm_bias, np.float32).reshape(1, N),
            W_out=np.ascontiguousarray(np.asarray(L0_attn_W_out, np.float32)),
            router_w=np.ascontiguousarray(
                np.asarray(L0_ffn_router_w, np.float32)),
            w1=np.ascontiguousarray(np.asarray(L0_ffn_w1, np.float32)),
            b1=np.ascontiguousarray(np.asarray(L0_ffn_b1, np.float32)),
            w2=np.ascontiguousarray(np.asarray(L0_ffn_w2, np.float32)),
            b2=np.ascontiguousarray(np.asarray(L0_ffn_b2, np.float32)),
            coremask64=cm,
        )
        m.update(consts)
        in_maps.append(m)
    res = run_bass_kernel_spmd(nc, in_maps, core_ids=list(range(B)))
    return np.stack([res.results[b]["out"] for b in range(B)], axis=0)


# version 31
# speedup vs baseline: 1.1472x; 1.1472x over previous
import numpy as np
import ml_dtypes
import concourse.bass as bass
import concourse.mybir as mybir
from concourse.bass_utils import run_bass_kernel_spmd

F32 = mybir.dt.float32
BF16 = mybir.dt.bfloat16
I32 = mybir.dt.int32
U32 = mybir.dt.uint32
AF = mybir.ActivationFunctionType
ALU = mybir.AluOpType
AX = mybir.AxisListType

B = 8
T = 4096
D = 1024
DP = 6432
DI = 2048
H = 32
P = 64
N = 64
L = 128
SCALE = float((N * 2) ** -0.5)
EPS = 1e-6
NEXP = 8
FF = 4096
CAP = 640
BIG = 1.0e6
TWO_PI = float(2.0 * np.pi)
HALF_PI = float(0.5 * np.pi)
PI = float(np.pi)

NCHUNK = 32
DO_FFN = True
DEBUG_TAPS = []
UNPIPE = False

ENG = ("sync", "tensor", "vector", "scalar", "gpsimd")
DMAQ = ("sync", "scalar", "gpsimd")
KPOOL = 6
AUTO_ORDER = ("tensor",)


class Sched:

    def __init__(self):
        self.sem_names = ["e_tensor", "e_vector", "e_scalar", "e_gpsimd", "cc"]
        self.sem_names += [f"q_{q}{i}" for q in DMAQ for i in range(KPOOL)]
        self.sem_index = {n: i for i, n in enumerate(self.sem_names)}
        self.nsem = len(self.sem_names)
        self.items = {e: [] for e in ENG}
        self.ecount = dict(tensor=0, vector=0, scalar=0, gpsimd=0)
        self.qcount = {q: 0 for q in DMAQ}
        self.cc_count = 0
        self.satisfied = {e: np.zeros(self.nsem, np.int64) for e in ENG}
        self.tokens = []
        self.regions = {}
        self.auto_extra = []

    def _dep_tokens(self, reads, writes):
        toks = list(self.auto_extra)
        for r in reads:
            reg = self.regions.get(r)
            if reg and reg[0] is not None:
                toks.append(reg[0])
        for w in writes:
            reg = self.regions.get(w)
            if reg:
                if reg[0] is not None:
                    toks.append(reg[0])
                toks.extend(reg[1])
        return toks

    def _update_regions(self, tok, reads, writes):
        for r in reads:
            self.regions.setdefault(r, [None, []])[1].append(tok)
        for w in writes:
            self.regions[w] = [tok, []]

    def _resolve(self, e, toks, forced=()):
        sat = self.satisfied[e]
        want = {}
        for t in toks:
            semi, th, _ = self.tokens[t]
            if sat[semi] < th:
                want[semi] = max(want.get(semi, 0), th)
        for (semname, val) in forced:
            si = self.sem_index[semname]
            if val > 0 and sat[si] < val:
                want[si] = max(want.get(si, 0), val)
        waits = sorted(want.items())
        for si, val in waits:
            if sat[si] < val:
                sat[si] = val
        for t in toks:
            np.maximum(sat, self.tokens[t][2], out=sat)
        return waits

    def _mk_token(self, e, semname, th):
        vc = self.satisfied[e].copy()
        si = self.sem_index[semname]
        vc[si] = max(vc[si], th)
        tok = len(self.tokens)
        self.tokens.append((si, th, vc))
        return tok

    def op(self, e, fn, reads=(), writes=(), extra=()):
        toks = self._dep_tokens(reads, writes) + [t for t in extra
                                                 if t is not None]
        waits = self._resolve(e, toks)
        self.ecount[e] += 1
        semname = "e_" + e
        tok = self._mk_token(e, semname, self.ecount[e])
        if e in AUTO_ORDER:
            self.satisfied[e][self.sem_index[semname]] = self.ecount[e]
        self.items[e].append((waits, fn, semname, 1))
        self._update_regions(tok, reads, writes)
        return tok

    def _dma_common(self, q, fn, reads, writes, extra):
        i = self.qcount[q]
        self.qcount[q] += 1
        semname = f"q_{q}{i % KPOOL}"
        rounds = i // KPOOL
        th = 16 * (rounds + 1)
        toks = self._dep_tokens(reads, writes) + [t for t in extra
                                                  if t is not None]
        forced = [(semname, 16 * rounds)] if rounds > 0 else []
        waits = self._resolve(q, toks, forced)
        tok = self._mk_token(q, semname, th)
        self.items[q].append((waits, fn, semname, 16))
        self._update_regions(tok, reads, writes)
        return tok

    def dma(self, q, out, in_, reads=(), writes=(), extra=(), **kw):
        def fn(eng, out=out, in_=in_, kw=kw):
            return eng.dma_start(out=out, in_=in_, **kw)
        return self._dma_common(q, fn, reads, writes, extra)

    def indirect(self, out, out_off, in_, in_off, reads=(), writes=(),
                 extra=()):
        def fn(eng, out=out, in_=in_):
            return eng.indirect_dma_start(out, out_off, in_, in_off,
                                          oob_is_err=False)
        return self._dma_common("gpsimd", fn, reads, writes, extra)

    def collective(self, fn, reads=(), writes=(), extra=()):
        toks = self._dep_tokens(reads, writes) + [t for t in extra
                                                  if t is not None]
        waits = self._resolve("gpsimd", toks)
        self.cc_count += 1
        tok = self._mk_token("gpsimd", "cc", self.cc_count)
        self.items["gpsimd"].append((waits, fn, "cc", 1))
        self._update_regions(tok, reads, writes)
        return tok

    def fence(self):
        forced = []
        for e in ("tensor", "vector", "scalar"):
            if self.ecount[e]:
                forced.append(("e_" + e, self.ecount[e]))
        for q in DMAQ:
            n = self.qcount[q]
            for i in range(min(n, KPOOL)):
                forced.append((f"q_{q}{i}", 16 * ((n - 1 - i) // KPOOL + 1)))
        waits = self._resolve("gpsimd", [], forced)
        self.ecount["gpsimd"] += 1
        tok = self._mk_token("gpsimd", "e_gpsimd",
                             self.ecount["gpsimd"])
        self.items["gpsimd"].append((waits, lambda eng: eng.nop(),
                                     "e_gpsimd", 1))
        return tok

    def finalize(self):
        forced = []
        for e in ("tensor", "vector", "scalar"):
            if self.ecount[e]:
                forced.append(("e_" + e, self.ecount[e]))
        for q in DMAQ:
            n = self.qcount[q]
            for i in range(min(n, KPOOL)):
                forced.append((f"q_{q}{i}", 16 * ((n - 1 - i) // KPOOL + 1)))
        waits = self._resolve("gpsimd", [], forced)
        self.items["gpsimd"].append((waits, lambda eng: eng.nop(), None, 0))

    def build(self, nc, block, ctx):
        sems = {n: ctx.enter_context(nc.semaphore(n)) for n in self.sem_names}
        names = self.sem_names
        for e in ENG:
            if not self.items[e]:
                continue

            def mk(e):
                items = self.items[e]

                def prog(eng):
                    for waits, fn, semname, inc in items:
                        for (si, val) in waits:
                            eng.wait_ge(sems[names[si]], val)
                        inst = fn(eng)
                        if inc:
                            inst.then_inc(sems[semname], inc)
                return prog

            getattr(block, e)(mk(e))


def _consts():
    bf = ml_dtypes.bfloat16
    c = {}
    c["id_f"] = np.eye(128, dtype=np.float32)
    c["id_b"] = np.eye(128, dtype=np.float32).astype(bf)
    c["triu_incl"] = np.triu(np.ones((128, 128), np.float32)).astype(bf)
    c["triu32_strict"] = np.triu(np.ones((32, 32), np.float32), 1).astype(bf)
    mask = np.where(np.triu(np.ones((128, 128), np.float32)) > 0, 0.0, -1e30)
    c["maskneg4"] = mask.astype(bf)
    c["maskscale"] = (np.triu(np.ones((128, 128), np.float32)) * SCALE
                      ).astype(np.float32)
    c["ones_b"] = np.ones((1, 128), np.float32).astype(bf)
    sel = np.zeros((128, 1024), np.float32)
    sel[0:8] = np.repeat(np.eye(8, dtype=np.float32), 128, axis=1)
    c["selrow32"] = sel.astype(bf)
    op = np.zeros((128, 128), np.float32)
    op[0] = 1.0
    c["onepad"] = op.astype(bf)
    c["shift1"] = np.eye(128, k=1, dtype=np.float32).astype(bf)
    s127 = np.zeros((128, 128), np.float32)
    s127[127, 0] = 1.0
    c["sel127"] = s127.astype(bf)
    c["iota8"] = np.tile(np.arange(8, dtype=np.float32)[None, :], (128, 1))
    c["iotamb"] = c["iota8"] - BIG
    return c


def build_program():
    from contextlib import ExitStack
    nc = bass.Bass("TRN2", target_bir_lowering=False, debug=False)
    S = Sched()
    ctx = ExitStack()

    def din(name, shape, dt=F32):
        return nc.dram_tensor(name, list(shape), dt, kind="ExternalInput").ap()

    x_h = din("x", (T, D))
    win_h = din("W_in", (D, DP))
    alog_h = din("A_log", (1, H))
    dtb_h = din("dt_bias", (1, H))
    bnb_h = din("B_nb", (1, N))
    cnb_h = din("C_nb", (1, N))
    wout_h = din("W_out", (DI, D))
    rw_h = din("router_w", (D, NEXP))
    w1_h = din("w1", (NEXP, D, FF))
    b1_h = din("b1", (NEXP, FF))
    w2_h = din("w2", (NEXP, FF, D))
    b2_h = din("b2", (NEXP, D))
    cmask_h = din("coremask64", (1, 64))
    id_f_h = din("id_f", (128, 128))
    id_b_h = din("id_b", (128, 128), BF16)
    triu_h = din("triu_incl", (128, 128), BF16)
    triu32_h = din("triu32_strict", (32, 32), BF16)
    maskneg_h = din("maskneg4", (128, 128), BF16)
    masksc_h = din("maskscale", (128, 128))
    ones_b_h = din("ones_b", (1, 128), BF16)
    selrow_h = din("selrow32", (128, 1024), BF16)
    onepad_h = din("onepad", (128, 128), BF16)
    shift1_h = din("shift1", (128, 128), BF16)
    sel127_h = din("sel127", (128, 128), BF16)
    iota8_h = din("iota8", (128, 8))
    iotamb_h = din("iotamb", (128, 8))

    out_h = nc.dram_tensor("out", [T, D], F32, kind="ExternalOutput").ap()

    xn_d = nc.dram_tensor("xn_d", [T, D], BF16).ap()
    s_d = nc.dram_tensor("s_d", [T, D], F32).ap()
    eb_d = nc.dram_tensor("eb_d", [NEXP * CAP, D], BF16).ap()
    eo_d = nc.dram_tensor("eo_d", [NEXP * CAP, D], BF16).ap()
    ccin_d = nc.dram_tensor("ccin_d", [1, NEXP], F32).ap()
    ccout_d = nc.dram_tensor("ccout_d", [B, NEXP], F32).ap()
    tot_d = nc.dram_tensor("tot_d", [1, 8 * NCHUNK], BF16).ap()
    off_d = nc.dram_tensor("off_d", [NCHUNK, 8], F32).ap()

    ctx1 = ExitStack()

    def sb(name, shape, dt=F32):
        return ctx.enter_context(nc.sbuf_tensor("sb_" + name, list(shape), dt))

    def sb1(name, shape, dt=F32):
        return ctx1.enter_context(nc.sbuf_tensor("sb_" + name, list(shape), dt))

    def psum(name, shape, dt=F32):
        return ctx.enter_context(nc.psum_tensor(name, list(shape), dt))

    xc = [sb("xc0", (128, D)), sb("xc1", (128, D))]
    id_f = sb("id_f", (128, 128))
    id_b = sb("id_b", (128, 128), BF16)
    triu32 = sb("triu32", (32, 32), BF16)
    ones_b = sb("ones_b", (1, 128), BF16)
    iota8 = sb("iota8", (128, 8))
    ones_c = sb("ones_c", (128, 1), BF16)
    cb_ln = sb("cb_ln", (128, 3))
    iotamb = sb("iotamb", (128, 8))
    cmask = sb("cmask", (1, 64))
    SCNT = sb("SCNT", (128, 8 * NCHUNK), BF16)
    IDX = sb("IDX", (128, NCHUNK))
    GATE = sb("GATE", (128, NCHUNK))
    emk = sb("emk", (128, 8), BF16)
    TOT = sb("TOT", (32, 8), BF16)
    OFFs = sb("OFFs", (32, 8))
    OFFrow = sb("OFFrow", (1, 256))
    OFFrh = sb("OFFrh", (1, 256), BF16)
    OFFrl = sb("OFFrl", (1, 256), BF16)
    ctot = sb("ctot", (1, 8))
    gtrow = sb("gtrow", (1, 64))
    myoff = sb("myoff", (1, 8))
    myoh = sb("myoh", (1, 8), BF16)
    myol = sb("myol", (1, 8), BF16)
    MYOFF = sb("MYOFF", (128, 8))
    zrow = sb("zrow", (1, 256), BF16)
    onepad = sb("onepad", (128, 128), BF16)
    WIN = sb1("WIN", (128, 8 * DP), BF16)
    WOUT = sb1("WOUT", (128, 16 * D), BF16)
    rw_f = sb1("rw_f", (128, 64))
    rwhl = sb1("rwhl", (128, 128), BF16)
    selrow = sb1("selrow", (128, 1024), BF16)
    shift1 = sb1("shift1", (128, 128), BF16)
    sel127 = sb1("sel127", (128, 128), BF16)
    triu = sb1("triu", (128, 128), BF16)
    maskneg = sb1("maskneg", (128, 128), BF16)
    masksc = sb1("masksc", (128, 128))
    abc = sb1("abc", (128, 32))
    dtbb = sb1("dtbb", (128, 32))
    bnb = sb1("bnb", (128, 64))
    cnb = sb1("cnb", (128, 64))
    rowtmp = sb1("rowtmp", (1, 64))
    rowh = sb1("rowh", (1, 64), BF16)
    rowl = sb1("rowl", (1, 64), BF16)
    xnT = sb1("xnT", (128, D), BF16)
    sz = sb1("szb", (128, DI), BF16)
    xh = sb1("xh", (128, 4096), BF16)
    xhp = sb1("xhp", (128, 4096), BF16)
    bcr = sb1("bcr", (128, 192), BF16)
    brt = bcr[:, 0:128]
    crt = bcr[:, 128:192]
    bbt = sb1("bbt", (128, 128), BF16)
    CrTz = [sb1("CrTz0", (128, 128), BF16), sb1("CrTz1", (128, 128), BF16)]
    lgbuf = sb1("lgbuf", (128, 24))
    lg8 = sb1("LG", (128, 8))
    cum = sb1("cum", (128, 64))
    cumg = [sb1("cumg0", (8, 128)), sb1("cumg1", (8, 128))]
    cumgh = [sb1("cumgh0", (128, 128), BF16), sb1("cumgh1", (128, 128), BF16)]
    cumgl = [sb1("cumgl0", (128, 128), BF16), sb1("cumgl1", (128, 128), BF16)]
    clast = sb1("clast", (1, 64))
    angchl = sb1("angchl", (128, 64), BF16)
    clhl = sb1("clhl", (128, 64), BF16)
    cdrow = sb1("cdrow", (1, 32))
    cdhl = sb1("cdhl", (128, 64), BF16)
    CD = sb1("CD", (64, 32))
    hilo = sb1("hilo", (128, 128), BF16)
    nchl = sb1("nchl", (128, 64), BF16)
    small = sb1("small", (128, 576))
    smcol = sb1("smcol", (128, 16))
    rms3 = sb1("rms3", (128, 4))
    scb = sb1("scb", (128, 64))
    sinb = scb[:, 0:32]
    cosb = scb[:, 32:64]
    ktmp = sb1("ktmp", (128, 64), I32)
    sgT = sb1("sgT", (128, 512), BF16)
    dgT = [sb1("dgT0", (128, 1024), BF16), sb1("dgT1", (128, 1024), BF16)]
    Wg = sb1("Wg", (128, 1024), BF16)
    xga = sb1("xga", (128, 1024), BF16)
    xba = sb1("xba", (128, 1024), BF16)
    ysc = sb1("ysc", (128, 512))
    YG = sb1("YG", (128, DI), BF16)
    Sst = sb1("Sst", (64, DI))
    sbfg = sb1("Sbf", (128, 512), BF16)
    BrT = [sb1("BrT0", (128, 132), BF16), sb1("BrT1", (128, 132), BF16)]
    SM = dict(dt=0, logdA=32, alpha=64, lam=96, lam1m=128, scal_g=160,
              scal_b=192, scal_gd=224, scal_bd=256, dec_in=288, dec_end=320,
              tdec=352, ang=384, red=416, kf=448, u2=480, rt1=512, rt2=544)
    s_sb = sz[:, :].bitcast(F32)
    xnTl = dgT[0][:, :]
    Wb = xnT[:, :]
    ysum = Wg[:, :].bitcast(F32)
    stmp = xga[0:64, :].bitcast(F32)
    sqb = YG[:, 0:384].bitcast(F32)
    Bn = YG[:, 384:640].bitcast(F32)
    Cn = YG[:, 640:768].bitcast(F32)
    praw = YG[:, 768:1344].bitcast(F32)
    xn = YG[:, :].bitcast(F32)
    ynT = xga[:, :]
    pA = [psum("pA0", (128, 512)), psum("pA1", (128, 512))]
    pT = psum("pT", (128, 1536))
    pTB = psum("pTB", (128, 1024), BF16)
    pY = psum("pY", (128, 512))
    pST = psum("pST", (64, 512))
    MB = 1024

    V, A, G_, TE = "vector", "scalar", "gpsimd", "tensor"
    dbg = []

    def tap(name, ap, reads):
        if name in DEBUG_TAPS:
            t = nc.dram_tensor("dbg_" + name, list(ap.shape),
                               ap.dtype, kind="ExternalOutput").ap()
            S.dma("sync", t, ap, reads=list(reads))
            dbg.append(name)

    for nm, dst, src in (("id_f", id_f, id_f_h), ("id_b", id_b, id_b_h),
                         ("triu", triu, triu_h), ("triu32", triu32, triu32_h),
                         ("maskneg", maskneg, maskneg_h),
                         ("masksc", masksc, masksc_h),
                         ("ones_b", ones_b, ones_b_h),
                         ("iota8", iota8, iota8_h),
                         ("iotamb", iotamb, iotamb_h),
                         ("selrow", selrow, selrow_h),
                         ("shift1", shift1, shift1_h),
                         ("sel127", sel127, sel127_h),
                         ("onepad", onepad, onepad_h),
                         ("cmask", cmask, cmask_h)):
        S.dma("sync", dst[:, :], src, writes=[nm])
    S.dma("sync", rw_f[:, :].rearrange("p (k e) -> p k e", k=8),
          rw_h.rearrange("(k p) e -> p k e", p=128), writes=["rw_f"])
    win_r = win_h.rearrange("(k p) f -> p k f", p=128)
    for k in range(8):
        S.dma("gpsimd", WIN[:, k * DP:(k + 1) * DP],
              win_r[:, k, :], writes=[f"WIN{k}"])
    wout_r = wout_h.rearrange("(k p) d -> p k d", p=128)
    for j in range(4):
        S.dma("gpsimd",
              WOUT[:, j * 4096:(j + 1) * 4096].rearrange(
                  "p (a d) -> p a d", a=4),
              wout_r[:, j * 4:(j + 1) * 4, :], writes=[f"WOUT{j}"])
    rwhl_v = rwhl[:, :].rearrange("p (k t e) -> p k t e", k=8, t=2)
    rwf_v = rw_f[:, :].rearrange("p (k e) -> p k e", k=8)
    S.op(V, lambda e: e.tensor_copy(rwhl_v[:, :, 0, :], rwf_v),
         reads=["rw_f"], writes=["rwhl"])
    S.op(V, lambda e: e.tensor_tensor(rwhl_v[:, :, 1, :], rwf_v,
                                      rwhl_v[:, :, 0, :], ALU.subtract),
         reads=["rw_f", "rwhl"], writes=["rwhl"])

    def bcast_row(src_h, width, dst, dreg, negexp=False):
        S.dma("sync", rowtmp[:, 0:width], src_h, writes=["rowtmp"])
        if negexp:
            S.op(A, lambda e: e.activation(rowtmp[:, 0:width],
                                           rowtmp[:, 0:width], AF.Exp),
                 reads=["rowtmp"], writes=["rowtmp"])
            S.op(V, lambda e: e.tensor_scalar(rowtmp[:, 0:width],
                                              rowtmp[:, 0:width], -1.0, None,
                                              ALU.mult),
                 reads=["rowtmp"], writes=["rowtmp"])
        S.op(V, lambda e: e.tensor_copy(rowh[:, 0:width], rowtmp[:, 0:width]),
             reads=["rowtmp"], writes=["rowh"])
        S.op(V, lambda e: e.tensor_tensor(rowl[:, 0:width], rowtmp[:, 0:width],
                                          rowh[:, 0:width], ALU.subtract),
             reads=["rowtmp", "rowh"], writes=["rowl"])
        S.op(TE, lambda e: e.matmul(pT[:, MB:MB + width], ones_b[:, 0:128],
                                    rowh[:, 0:width], start=True, stop=False),
             reads=["rowh", "ones_b"], writes=["pT_M"])
        S.op(TE, lambda e: e.matmul(pT[:, MB:MB + width], ones_b[:, 0:128],
                                    rowl[:, 0:width], start=False, stop=True),
             reads=["rowl", "ones_b"], writes=["pT_M"])
        S.op(V, lambda e: e.tensor_copy(dst[:, 0:width], pT[:, MB:MB + width]),
             reads=["pT_M"], writes=[dreg])

    bcast_row(alog_h, 32, abc, "abc", negexp=True)
    bcast_row(dtb_h, 32, dtbb, "dtbb")
    bcast_row(bnb_h, 64, bnb, "bnb")
    bcast_row(cnb_h, 64, cnb, "cnb")

    S.op("vector", lambda e: e.memset(ones_c[:, :], 1.0), writes=["ones_c"])
    S.op("vector", lambda e: e.memset(cb_ln[:, 0:1], 1e-5), writes=["cb_ln"])
    S.op("vector", lambda e: e.memset(cb_ln[:, 1:2], 1.0), writes=["cb_ln"])
    S.op("vector", lambda e: e.memset(cb_ln[:, 2:3], EPS), writes=["cb_ln"])
    for t_, r_ in ((Sst, "Sst"), (BrT[0], "BrT0"), (zrow, "zrow"),
                   (xh, "xh"), (bbt, "bbt"),
                   (angchl, "angchl"), (clhl, "clhl"), (cdhl, "cdhl"),
                   (cumgh[0], "cumgh0"), (cumgh[1], "cumgh1"),
                   (cumgl[0], "cumgl0"), (cumgl[1], "cumgl1"),
                   (CrTz[0], "CrTz0"), (CrTz[1], "CrTz1"),
                   (sbfg, "sbfg")):
        S.op(V, lambda e, t_=t_: e.memset(t_[:, :], 0.0), writes=[r_])
    for zq in range(4):
        S.dma("gpsimd", eb_d[0:1, zq * 256:(zq + 1) * 256], zrow[:, :],
              reads=["zrow"], writes=["ebz0"])
    ebz_keys = ["ebz0"]

    S.dma("sync", xc[0][:, :], x_h[0:128, :], writes=["xc0"])

    def sm(key, w=32):
        return small[:, SM[key]:SM[key] + w]

    def emit_ln(cc):
        xcb = xc[cc % 2]
        xreg = f"xc{cc % 2}"
        S.op(V, lambda e, xcb=xcb: e.reduce_sum(smcol[:, 0:1], xcb[:, :],
                                                axis=AX.X),
             reads=[xreg], writes=["smc0"])
        S.op(A, lambda e, xcb=xcb: e.activation(xga[:, :], xcb[:, :],
                                                AF.Square,
                                                accum_out=smcol[:, 1:2]),
             reads=[xreg], writes=["smc1", "xga"])
        S.op(G_, lambda e: e.tensor_scalar(smcol[:, 2:3], smcol[:, 0:1],
                                           1.0 / D, None, ALU.mult),
             reads=["smc0"], writes=["smc2"])
        S.op(G_, lambda e: e.tensor_scalar(smcol[:, 3:4], smcol[:, 1:2],
                                           1.0 / D, None, ALU.mult),
             reads=["smc1"], writes=["smc3"])
        S.op(G_, lambda e: e.tensor_tensor(smcol[:, 4:5], smcol[:, 2:3],
                                           smcol[:, 2:3], ALU.mult),
             reads=["smc2"], writes=["smc4"])
        S.op(G_, lambda e: e.tensor_tensor(smcol[:, 4:5], smcol[:, 3:4],
                                           smcol[:, 4:5], ALU.subtract),
             reads=["smc3", "smc4"], writes=["smc4"])
        S.op(A, lambda e: e.activation(smcol[:, 5:6], smcol[:, 4:5], AF.Ln,
                                       bias=cb_ln[:, 0:1]),
             reads=["smc4", "cb_ln"], writes=["smc5"])
        S.op(A, lambda e: e.activation(smcol[:, 5:6], smcol[:, 5:6], AF.Exp,
                                       scale=-0.5),
             reads=["smc5"], writes=["smc5"])
        S.op(G_, lambda e: e.tensor_scalar(smcol[:, 6:7], smcol[:, 2:3],
                                           -1.0, None, ALU.mult),
             reads=["smc2"], writes=["smc6"])
        S.op(V, lambda e, xcb=xcb: e.tensor_scalar(xn[:, :], xcb[:, :],
                                                   smcol[:, 6:7],
                                                   smcol[:, 5:6],
                                                   ALU.add, ALU.mult),
             reads=[xreg, "smc6", "smc5"], writes=["YG"])
        tap(f"xn{cc}", xn[:, :], ["YG"])
        if DO_FFN:
            S.dma("gpsimd", xn_d[cc * 128:(cc + 1) * 128, :], xn[:, :],
                  reads=["YG"], writes=[f"xnd{cc}"])


    for c in range(NCHUNK):
        par = c % 2
        xcb = xc[par]
        xreg = f"xc{par}"
        bpar = c % 2
        if c + 1 < NCHUNK:
            S.dma("sync", xc[1 - par][:, :],
                  x_h[(c + 1) * 128:(c + 2) * 128, :], writes=[f"xc{1 - par}"])
        if c == 0:
            emit_ln(0)

        for k in range(8):
            S.op(TE, lambda e, k=k: e.transpose(pT[:, k * 128:(k + 1) * 128],
                                                xn[:, k * 128:(k + 1) * 128],
                                                id_f[:, :]),
                 reads=["YG", "id_f"], writes=["pT_A"])
        S.op(A, lambda e: e.copy(xnT[:, :], pT[:, 0:1024]),
             reads=["pT_A"], writes=["xnT"])
        S.op(V, lambda e: e.tensor_tensor(xnTl[:, :], pT[:, 0:1024],
                                          xnT[:, :], ALU.subtract),
             reads=["pT_A", "xnT"], writes=["dgT0"])

        def emit_router(c=c):
            for k in range(8):
                S.op(TE, lambda e, k=k: e.matmul(
                    pT[:, MB:MB + 16], xnT[:, k * 128:(k + 1) * 128],
                    rwhl[:, k * 16:(k + 1) * 16], start=(k == 0), stop=False),
                    reads=["xnT", "rwhl"], writes=["pT_M"])
            for k in range(8):
                S.op(TE, lambda e, k=k: e.matmul(
                    pT[:, MB + 16:MB + 24], xnTl[:, k * 128:(k + 1) * 128],
                    rwhl[:, k * 16:k * 16 + 8], start=(k == 0), stop=(k == 7)),
                    reads=["dgT0", "rwhl"], writes=["pT_M"])
            S.op(A, lambda e: e.copy(lgbuf[:, :], pT[:, MB:MB + 24]),
                 reads=["pT_M"], writes=["lgbuf"])
            lgc = lg8[:, :]
            S.op(V, lambda e: e.tensor_tensor(lgbuf[:, 0:8], lgbuf[:, 0:8],
                                              lgbuf[:, 8:16], ALU.add),
                 reads=["lgbuf"], writes=["lgbuf"])
            S.op(V, lambda e, lgc=lgc: e.tensor_tensor(lgc, lgbuf[:, 0:8],
                                                       lgbuf[:, 16:24], ALU.add),
                 reads=["lgbuf"], writes=["LG"])
            tap(f"logits{c}", lgc, ["LG"])
            S.op(V, lambda e, lgc=lgc: e.reduce_max(smcol[:, 7:8], lgc, axis=AX.X),
                 reads=["LG"], writes=["smc7"])
            S.op(V, lambda e: e.tensor_scalar(smcol[:, 8:9], smcol[:, 7:8],
                                              -1.0, None, ALU.mult),
                 reads=["smc7"], writes=["smc8"])
            S.op(A, lambda e, lgc=lgc: e.activation(lgbuf[:, 0:8], lgc, AF.Exp,
                                                    bias=smcol[:, 8:9]),
                 reads=["LG", "smc8"], writes=["lgbuf"])
            S.op(V, lambda e: e.reduce_sum(smcol[:, 9:10], lgbuf[:, 0:8],
                                           axis=AX.X),
                 reads=["lgbuf"], writes=["smc9"])
            S.op(V, lambda e, c=c: e.reciprocal(GATE[:, c:c + 1], smcol[:, 9:10]),
                 reads=["smc9"], writes=["GATE"])
            S.op(V, lambda e, lgc=lgc: e.tensor_scalar(lgbuf[:, 8:16], lgc,
                                                       smcol[:, 7:8], None,
                                                       ALU.is_equal),
                 reads=["LG", "smc7"], writes=["lgbuf"])
            S.op(V, lambda e: e.scalar_tensor_tensor(lgbuf[:, 16:24],
                                                     lgbuf[:, 8:16], 1.0,
                                                     iotamb[:, :], ALU.mult,
                                                     ALU.mult),
                 reads=["lgbuf", "iotamb"], writes=["lgbuf"])
            S.op(V, lambda e: e.tensor_reduce(smcol[:, 10:11], lgbuf[:, 16:24],
                                              axis=AX.X, op=ALU.min),
                 reads=["lgbuf"], writes=["smc10"])
            S.op(V, lambda e, c=c: e.tensor_scalar(IDX[:, c:c + 1],
                                                   smcol[:, 10:11], BIG, None,
                                                   ALU.add),
                 reads=["smc10"], writes=["IDX"])
            S.op(V, lambda e, c=c: e.tensor_scalar(emk[:, :], iota8[:, :],
                                                   IDX[:, c:c + 1], None,
                                                   ALU.is_equal),
                 reads=["IDX", "iota8"], writes=["emk"])
            S.op(TE, lambda e: e.matmul(pT[:, MB:MB + 8], triu[:, :], emk[:, :],
                                        start=True, stop=True),
                 reads=["emk", "triu"], writes=["pT_M"])
            S.op(A, lambda e, c=c: e.copy(SCNT[:, c * 8:(c + 1) * 8],
                                          pT[:, MB:MB + 8]),
                 reads=["pT_M"], writes=["SCNT"])
            if DO_FFN and c == NCHUNK - 1:
                S.dma("sync", tot_d, SCNT[127:128, :], reads=["SCNT"],
                      writes=["tot_d"])
                S.dma("sync", TOT[:, :],
                      tot_d.rearrange("o (cc ee) -> (o cc) ee", ee=8),
                      reads=["tot_d"], writes=["TOT"])
                S.op(TE, lambda e: e.matmul(pT[0:32, MB + 256:MB + 264],
                                            triu32[:, :], TOT[:, :],
                                            start=True, stop=True),
                     reads=["TOT", "triu32"], writes=["pT_M"])
                S.op(V, lambda e: e.tensor_copy(OFFs[:, :],
                                                pT[0:32, MB + 256:MB + 264]),
                     reads=["pT_M"], writes=["OFFs"])
                S.op(TE, lambda e: e.matmul(pT[0:1, MB + 264:MB + 272],
                                            ones_c[0:32, 0:1], TOT[:, :],
                                            start=True, stop=True),
                     reads=["TOT", "ones_c"], writes=["pT_M"])
                S.op(V, lambda e: e.tensor_copy(ctot[:, :],
                                                pT[0:1, MB + 264:MB + 272]),
                     reads=["pT_M"], writes=["ctot"])
                S.dma("gpsimd", ccin_d, ctot[:, :], reads=["ctot"],
                      writes=["ccin"])

                def ccfn(e):
                    return e.collective_compute(
                        "AllGather", ALU.bypass,
                        replica_groups=[list(range(B))],
                        ins=[ccin_d.opt()], outs=[ccout_d.opt()])

                S.collective(ccfn, reads=["ccin"], writes=["ccout"])
                S.dma("sync", off_d, OFFs[:, :], reads=["OFFs"],
                      writes=["off_d"])
                S.dma("sync", OFFrow[:, :], off_d.flatten().unsqueeze(0),
                      reads=["off_d"], writes=["OFFrow"])

        def projmm(fstart, fwidth, pa, preg):
            for k in range(8):
                S.op(TE, lambda e, k=k, fstart=fstart, fwidth=fwidth, pa=pa:
                     e.matmul(pa[:, 0:fwidth],
                              xnT[:, k * 128:(k + 1) * 128],
                              WIN[:, k * DP + fstart:k * DP + fstart + fwidth],
                              start=(k == 0), stop=(k == 7)),
                     reads=["xnT", f"WIN{k}"], writes=[preg])

        projmm(6144, 288, pA[0], "pA0")
        S.op(A, lambda e: e.copy(praw[:, :], pA[0][:, 0:288]),
             reads=["pA0"], writes=["YG"])
        tap(f"praw{c}", praw[:, :], ["YG"])

        S.op(G_, lambda e: e.tensor_tensor(sm("dt"), praw[:, 192:224],
                                          dtbb[:, 0:32], ALU.add),
             reads=["YG", "dtbb"], writes=["sm_dt"])
        S.op(A, lambda e: e.activation(sm("red"), sm("dt"), AF.Abs),
             reads=["sm_dt"], writes=["sm_red"])
        S.op(A, lambda e: e.activation(sm("red"), sm("red"), AF.Exp,
                                       scale=-1.0),
             reads=["sm_red"], writes=["sm_red"])
        S.op(A, lambda e: e.activation(sm("red"), sm("red"), AF.Ln,
                                       bias=cb_ln[:, 1:2]),
             reads=["sm_red", "cb_ln"], writes=["sm_red"])
        S.op(G_, lambda e: e.tensor_scalar(sm("u2"), sm("dt"), 0.0, None,
                                          ALU.max),
             reads=["sm_dt"], writes=["sm_u2"])
        S.op(G_, lambda e: e.tensor_tensor(sm("dt"), sm("red"), sm("u2"),
                                          ALU.add),
             reads=["sm_red", "sm_u2"], writes=["sm_dt"])
        S.op(G_, lambda e: e.tensor_scalar(sm("dt"), sm("dt"), 1e-4, 0.5,
                                          ALU.max, ALU.min),
             reads=["sm_dt"], writes=["sm_dt"])
        tap(f"dt{c}", sm("dt"), ["sm_dt"])
        S.op(A, lambda e: e.activation(sm("lam1m"), praw[:, 224:256], AF.Exp,
                                       scale=-1.0),
             reads=["YG"], writes=["sm_lam1m"])
        S.op(G_, lambda e: e.tensor_scalar(sm("u2"), sm("lam1m"), 1.0, None,
                                          ALU.add),
             reads=["sm_lam1m"], writes=["sm_u2"])
        S.op(V, lambda e: e.reciprocal(sm("lam"), sm("u2")),
             reads=["sm_u2"], writes=["sm_lam"])
        S.op(G_, lambda e: e.tensor_tensor(sm("lam1m"), sm("lam"), sm("lam1m"),
                                          ALU.mult),
             reads=["sm_lam", "sm_lam1m"], writes=["sm_lam1m"])
        S.op(G_, lambda e: e.tensor_tensor(sm("logdA"), sm("dt"), abc[:, 0:32],
                                          ALU.mult),
             reads=["sm_dt", "abc"], writes=["sm_logdA"])
        S.op(A, lambda e: e.activation(sm("alpha"), sm("logdA"), AF.Exp),
             reads=["sm_logdA"], writes=["sm_alpha"])
        S.op(G_, lambda e: e.tensor_tensor(sm("scal_g"), sm("lam"), sm("dt"),
                                          ALU.mult),
             reads=["sm_lam", "sm_dt"], writes=["sm_scal_g"])
        S.op(G_, lambda e: e.tensor_tensor(sm("scal_b"), sm("lam1m"), sm("dt"),
                                          ALU.mult),
             reads=["sm_lam1m", "sm_dt"], writes=["sm_scal_b"])
        S.op(G_, lambda e: e.tensor_tensor(sm("scal_b"), sm("scal_b"),
                                          sm("alpha"), ALU.mult),
             reads=["sm_scal_b", "sm_alpha"], writes=["sm_scal_b"])
        S.op(V, lambda e: e.reduce_sum(smcol[:, 11:12], sm("dt"), axis=AX.X),
             reads=["sm_dt"], writes=["smc11"])
        S.op(G_, lambda e: e.tensor_scalar(smcol[:, 11:12], smcol[:, 11:12],
                                          1.0 / 32, None, ALU.mult),
             reads=["smc11"], writes=["smc11"])
        S.op(G_, lambda e: e.tensor_scalar(sm("ang"), praw[:, 256:288],
                                          smcol[:, 11:12], None, ALU.mult),
             reads=["YG", "smc11"], writes=["sm_ang"])
        S.op(G_, lambda e: e.tensor_copy(hilo[:, 0:32], sm("logdA")),
             reads=["sm_logdA"], writes=["hilo"])
        S.op(G_, lambda e: e.tensor_copy(hilo[:, 32:64], sm("ang")),
             reads=["sm_ang"], writes=["hilo"])
        S.op(G_, lambda e: e.tensor_tensor(hilo[:, 64:96], sm("logdA"),
                                          hilo[:, 0:32], ALU.subtract),
             reads=["sm_logdA", "hilo"], writes=["hilo"])
        S.op(G_, lambda e: e.tensor_tensor(hilo[:, 96:128], sm("ang"),
                                          hilo[:, 32:64], ALU.subtract),
             reads=["sm_ang", "hilo"], writes=["hilo"])
        def emit_cum(c=c):
            S.op(TE, lambda e: e.matmul(pT[:, MB:MB + 64], triu[:, :],
                                        hilo[:, 0:64], start=True, stop=False),
                 reads=["hilo", "triu"], writes=["pT_M"])
            S.op(TE, lambda e: e.matmul(pT[:, MB:MB + 64], triu[:, :],
                                        hilo[:, 64:128], start=False, stop=False),
                 reads=["hilo", "triu"], writes=["pT_M"])
            S.op(TE, lambda e: e.matmul(pT[:, MB + 32:MB + 64], onepad[:, :],
                                        angchl[:, 0:32], start=False, stop=False),
                 reads=["angchl", "onepad"], writes=["pT_M"])
            S.op(TE, lambda e: e.matmul(pT[:, MB + 32:MB + 64], onepad[:, :],
                                        angchl[:, 32:64], start=False, stop=True),
                 reads=["angchl", "onepad"], writes=["pT_M"])
            S.op(A, lambda e: e.copy(cum[:, :], pT[:, MB:MB + 64]),
                 reads=["pT_M"], writes=["cum"])
            tap(f"cum{c}", cum[:, :], ["cum"])
            S.op(TE, lambda e: e.matmul(pT[0:1, MB + 64:MB + 128],
                                        id_f[:, 127:128], cum[:, :],
                                        start=True, stop=True),
                 reads=["cum", "id_f"], writes=["pT_M"])
            S.op(A, lambda e: e.copy(clast[:, :], pT[0:1, MB + 64:MB + 128]),
                 reads=["pT_M"], writes=["clast"])
            S.op(V, lambda e: e.tensor_scalar(sm("lam"), cum[:, 0:32], -1.0, None,
                                              ALU.mult),
                 reads=["cum"], writes=["sm_lam"])
            S.op(G_, lambda e: e.tensor_copy(nchl[:, 0:32], sm("lam")),
                 reads=["sm_lam"], writes=["nchl"])
            S.op(G_, lambda e: e.tensor_tensor(nchl[:, 32:64], sm("lam"),
                                               nchl[:, 0:32], ALU.subtract),
                 reads=["sm_lam", "nchl"], writes=["nchl"])
            S.op(G_, lambda e: e.tensor_copy(angchl[0:1, 0:32], clast[:, 32:64]),
                 reads=["clast"], writes=["angchl"])
            S.op(G_, lambda e: e.tensor_tensor(angchl[0:1, 32:64],
                                               clast[:, 32:64],
                                               angchl[0:1, 0:32], ALU.subtract),
                 reads=["clast", "angchl"], writes=["angchl"])
            S.op(G_, lambda e: e.tensor_copy(clhl[0:1, 0:32], clast[:, 0:32]),
                 reads=["clast"], writes=["clhl"])
            S.op(G_, lambda e: e.tensor_tensor(clhl[0:1, 32:64], clast[:, 0:32],
                                               clhl[0:1, 0:32], ALU.subtract),
                 reads=["clast", "clhl"], writes=["clhl"])
        emit_cum()
        S.op(A, lambda e: e.activation(sqb[:, :], praw[:, 0:192], AF.Square),
             reads=["YG"], writes=["YG"])
        for i in range(3):
            S.op(V, lambda e, i=i: e.reduce_sum(rms3[:, i:i + 1],
                                                sqb[:, i * 64:(i + 1) * 64],
                                                axis=AX.X),
                 reads=["YG"], writes=["rms3"])
        S.op(A, lambda e: e.activation(rms3[:, 0:3], rms3[:, 0:3], AF.Ln,
                                       scale=1.0 / 64, bias=cb_ln[:, 2:3]),
             reads=["rms3", "cb_ln"], writes=["rms3"])
        S.op(A, lambda e: e.activation(rms3[:, 0:3], rms3[:, 0:3], AF.Exp,
                                       scale=-0.5),
             reads=["rms3"], writes=["rms3"])
        for r in range(2):
            S.op(V, lambda e, r=r: e.scalar_tensor_tensor(
                Bn[:, r * 64:(r + 1) * 64], praw[:, r * 64:(r + 1) * 64],
                rms3[:, r:r + 1], bnb[:, 0:64], ALU.mult, ALU.mult),
                reads=["YG", "rms3", "bnb"], writes=["YG"])
        S.op(V, lambda e: e.scalar_tensor_tensor(
            Cn[:, :], praw[:, 128:192], rms3[:, 2:3], cnb[:, 0:64],
            ALU.mult, ALU.mult),
            reads=["YG", "rms3", "cnb"], writes=["YG"])
        zsl = small[:, SM["red"]:SM["red"] + 64]
        t64 = small[:, SM["tdec"]:SM["tdec"] + 64]
        kf64 = small[:, SM["rt1"]:SM["rt1"] + 64]
        S.op(G_, lambda e: e.tensor_copy(zsl[:, 0:32], cum[:, 32:64]),
             reads=["cum"], writes=["sm_red", "sm_kf"])
        S.op(G_, lambda e: e.tensor_scalar(zsl[:, 32:64], cum[:, 32:64],
                                           HALF_PI, None, ALU.add),
             reads=["cum"], writes=["sm_red", "sm_kf"])
        S.op(G_, lambda e: e.tensor_scalar(t64, zsl, 1.0 / TWO_PI, None,
                                           ALU.mult),
             reads=["sm_red", "sm_kf"], writes=["sm_tdec", "sm_ang"])
        S.op(V, lambda e: e.tensor_copy(ktmp[:, :], t64),
             reads=["sm_tdec", "sm_ang"], writes=["ktmp"])
        S.op(V, lambda e: e.tensor_copy(kf64, ktmp[:, :]),
             reads=["ktmp"], writes=["sm_rt1", "sm_rt2"])
        S.op(V, lambda e: e.scalar_tensor_tensor(zsl, kf64, -TWO_PI,
                                                 zsl, ALU.mult, ALU.add),
             reads=["sm_rt1", "sm_rt2", "sm_red", "sm_kf"],
             writes=["sm_red", "sm_kf"])
        S.op(G_, lambda e: e.tensor_scalar(zsl, zsl, PI, -PI,
                                           ALU.min, ALU.max),
             reads=["sm_red", "sm_kf"], writes=["sm_red", "sm_kf"])
        S.op(A, lambda e: e.activation(scb[:, :], zsl, AF.Sin),
             reads=["sm_red", "sm_kf"], writes=["sinb", "cosb"])

        vsrc = YG[:, 384:768].bitcast(F32).rearrange(
            "p (g tw h) -> p g tw h", tw=2, h=32)
        bcrv = bcr[:, :].rearrange("p (g tw h) -> p g tw h", tw=2, h=32)
        cos_e = cosb.unsqueeze(1).broadcast_to((128, 3, 32))
        sin_e = sinb.unsqueeze(1).broadcast_to((128, 3, 32))
        rr1 = ysc[:, 0:96].rearrange("p (g h) -> p g h", h=32)
        rr2 = ysc[:, 96:192].rearrange("p (g h) -> p g h", h=32)
        S.op(V, lambda e: e.tensor_tensor(rr1, vsrc[:, :, 0, :], cos_e,
                                          ALU.mult),
             reads=["YG", "cosb"], writes=["ysc"])
        S.op(V, lambda e: e.tensor_tensor(rr2, vsrc[:, :, 1, :], sin_e,
                                          ALU.mult),
             reads=["YG", "sinb"], writes=["ysc"])
        S.op(V, lambda e: e.tensor_tensor(bcrv[:, :, 0, :], rr1, rr2,
                                          ALU.add),
             reads=["ysc"], writes=["brt", "crt"])
        S.op(V, lambda e: e.tensor_tensor(rr1, vsrc[:, :, 1, :], cos_e,
                                          ALU.mult),
             reads=["YG", "cosb"], writes=["ysc"])
        S.op(V, lambda e: e.tensor_tensor(rr2, vsrc[:, :, 0, :], sin_e,
                                          ALU.mult),
             reads=["YG", "sinb"], writes=["ysc"])
        S.op(V, lambda e: e.tensor_tensor(bcrv[:, :, 1, :], rr1, rr2,
                                          ALU.subtract),
             reads=["ysc"], writes=["brt", "crt"])
        tap(f"Br{c}", brt[:, :], ["brt"])
        tap(f"Cr{c}", crt[:, :], ["crt"])

        for gz in range(4):
            pa, preg = pA[gz % 2], f"pA{gz % 2}"
            projmm(gz * 512, 512, pa, preg)
            S.op(A, lambda e, gz=gz, pa=pa: e.activation(
                sz[:, gz * 512:(gz + 1) * 512], pa[:, :], AF.Silu),
                reads=[preg], writes=["sz"])
        for gx in range(8):
            pa, preg = pA[gx % 2], f"pA{gx % 2}"
            po = (gx % 2) * 512
            S.op(TE, lambda e, gx=gx, po=po: e.matmul(
                pT[:, po:po + 512], sel127[:, :],
                xh[:, gx * 512:(gx + 1) * 512], start=True, stop=False),
                reads=["xh", "sel127"], writes=["pT_A"])
            projmm(2048 + gx * 512, 512, pa, preg)
            S.op(A, lambda e, gx=gx, pa=pa: e.activation(
                xh[:, gx * 512:(gx + 1) * 512], pa[:, :], AF.Silu),
                reads=[preg], writes=["xh"])
            S.op(TE, lambda e, gx=gx, po=po: e.matmul(
                pT[:, po:po + 512], shift1[:, :],
                xh[:, gx * 512:(gx + 1) * 512], start=False, stop=True),
                reads=["xh", "shift1"], writes=["pT_A"])
            S.op(A, lambda e, gx=gx, po=po: e.copy(
                xhp[:, gx * 512:(gx + 1) * 512],
                pT[:, po:po + 512]),
                reads=["pT_A"], writes=["xhp"])
        tap(f"xh{c}", xh[:, :], ["xh"])
        emit_router()

        S.dma("gpsimd", bbt[1:128, :], brt[0:127, :], reads=["brt"],
              writes=["bbt"])

        S.op(TE, lambda e: e.transpose(pTB[:, 0:128], brt[:, :], id_b[:, :]),
             reads=["brt", "id_b"], writes=["pTB"])
        S.op(A, lambda e, bpar=bpar: e.copy(BrT[bpar][:, 1:129],
                                            pTB[:, 0:128]),
             reads=["pTB"], writes=[f"BrT{bpar}"])
        if c > 0:
            S.op(V, lambda e, bpar=bpar: e.tensor_copy(
                BrT[bpar][:, 0:1], BrT[1 - bpar][:, 128:129]),
                reads=[f"BrT{1 - bpar}"], writes=[f"BrT{bpar}"])
        S.op(TE, lambda e: e.transpose(pTB[0:64, 128:256], crt[:, :], id_b[:, :]),
             reads=["crt", "id_b"], writes=["pTB"])
        S.op(TE, lambda e: e.transpose(pTB[64:128, 128:256], crt[:, :],
                                       id_b[:, :]),
             reads=["crt", "id_b"], writes=["pTB"])
        S.op(A, lambda e: e.copy(CrTz[0][0:64, :], pTB[0:64, 128:256]),
             reads=["pTB"], writes=["CrTz0"])
        S.op(A, lambda e: e.copy(CrTz[1][64:128, :], pTB[64:128, 128:256]),
             reads=["pTB"], writes=["CrTz1"])

        for si in range(2):
            for r in range(2):
                co = (si * 2 + r) * 128
                lo = 1 - si
                S.op(TE, lambda e, r=r, lo=lo, co=co, bpar=bpar: e.matmul(
                    pT[:, co:co + 128],
                    BrT[bpar][:, lo:lo + 128],
                    CrTz[r][:, :], start=True, stop=True),
                    reads=[f"BrT{bpar}", f"CrTz{r}"], writes=["pT_A"])
                S.op(V, lambda e, co=co: e.tensor_tensor(
                    sgT[:, co:co + 128], pT[:, co:co + 128], masksc[:, :],
                    ALU.mult),
                    reads=["pT_A", "masksc"], writes=["sgT"])
        tap(f"sgT{c}", sgT[:, :], ["sgT"])
        tap(f"bbt{c}", bbt[:, :], ["bbt"])


        S.op(TE, lambda e: e.matmul(pT[:, MB + 64:MB + 96], onepad[:, :],
                                    clhl[:, 0:32], start=True, stop=False),
             reads=["clhl", "onepad"], writes=["pT_M"])
        S.op(TE, lambda e: e.matmul(pT[:, MB + 64:MB + 96], onepad[:, :],
                                    clhl[:, 32:64], start=False, stop=True),
             reads=["clhl", "onepad"], writes=["pT_M"])
        S.op(V, lambda e: e.tensor_tensor(sm("tdec"), pT[:, MB + 64:MB + 96],
                                          cum[:, 0:32], ALU.subtract),
             reads=["pT_M", "cum"], writes=["sm_tdec"])
        S.op(A, lambda e: e.activation(sm("dec_end"), sm("tdec"), AF.Exp),
             reads=["sm_tdec"], writes=["sm_dec_end"])
        S.op(A, lambda e: e.activation(sm("dec_in"), cum[:, 0:32], AF.Exp),
             reads=["cum"], writes=["sm_dec_in"])
        S.op(G_, lambda e: e.tensor_scalar(sm("dec_in"), sm("dec_in"), SCALE,
                                          None, ALU.mult),
             reads=["sm_dec_in"], writes=["sm_dec_in"])
        S.op(G_, lambda e: e.tensor_scalar(sm("tdec"), sm("dec_in"), -1.0,
                                          None, ALU.add),
             reads=["sm_dec_in"], writes=["sm_tdec"])
        tap(f"dec_in{c}", sm("dec_in"), ["sm_dec_in"])
        tap(f"dec_end{c}", sm("dec_end"), ["sm_dec_end"])
        S.op(G_, lambda e: e.tensor_tensor(sm("scal_gd"), sm("scal_g"),
                                          sm("dec_end"), ALU.mult),
             reads=["sm_scal_g", "sm_dec_end"], writes=["sm_scal_gd"])
        S.op(G_, lambda e: e.tensor_tensor(sm("scal_bd"), sm("scal_b"),
                                          sm("dec_end"), ALU.mult),
             reads=["sm_scal_b", "sm_dec_end"], writes=["sm_scal_bd"])
        S.op(A, lambda e: e.activation(cdrow[:, :], clast[:, 0:32], AF.Exp),
             reads=["clast"], writes=["cdrow"])
        S.op(G_, lambda e: e.tensor_copy(cdhl[0:1, 0:32], cdrow[:, :]),
             reads=["cdrow"], writes=["cdhl"])
        S.op(V, lambda e: e.tensor_tensor(cdhl[0:1, 32:64], cdrow[:, :],
                                          cdhl[0:1, 0:32], ALU.subtract),
             reads=["cdrow", "cdhl"], writes=["cdhl"])
        S.op(TE, lambda e: e.matmul(pT[:, MB + 96:MB + 128],
                                    onepad[:, :], cdhl[:, 0:32],
                                    start=True, stop=False),
             reads=["cdhl", "onepad"], writes=["pT_M"])
        S.op(TE, lambda e: e.matmul(pT[:, MB + 96:MB + 128],
                                    onepad[:, :], cdhl[:, 32:64],
                                    start=False, stop=True),
             reads=["cdhl", "onepad"], writes=["pT_M"])
        S.op(A, lambda e: e.copy(CD[:, :], pT[0:64, MB + 96:MB + 128]),
             reads=["pT_M"], writes=["CD"])
        def emit_bc(g, dpar):
            cg, rg = cumg[dpar], f"cumg{dpar}"
            cgh, rgh = cumgh[dpar], f"cumgh{dpar}"
            cgl, rgl = cumgl[dpar], f"cumgl{dpar}"
            S.op(TE, lambda e, g=g: e.transpose(
                pT[0:8, MB + 128:MB + 256], cum[:, g * 8:(g + 1) * 8],
                id_f[:, :]),
                reads=["cum", "id_f"], writes=["pT_M"])
            S.op(A, lambda e, cg=cg: e.copy(cg[:, :],
                                            pT[0:8, MB + 128:MB + 256]),
                 reads=["pT_M"], writes=[rg])
            S.op(G_, lambda e, cg=cg, cgh=cgh: e.tensor_copy(cgh[0:8, :],
                                                             cg[:, :]),
                 reads=[rg], writes=[rgh])
            S.op(G_, lambda e, cg=cg, cgh=cgh, cgl=cgl: e.tensor_tensor(
                cgl[0:8, :], cg[:, :], cgh[0:8, :], ALU.subtract),
                reads=[rg, rgh], writes=[rgl])
            mask_b = maskneg[:, :].unsqueeze(1).broadcast_to((128, 4, 128))
            for half in range(2):
                hsl = slice(half * 512, (half + 1) * 512)
                pvm = pT[:, hsl].rearrange("p (h l) -> p h l", h=4)
                S.op(TE, lambda e, pvm=pvm, mask_b=mask_b: e.matmul(
                    pvm, id_b[:, :], mask_b,
                    start=True, stop=False),
                    reads=["maskneg", "id_b"], writes=["pT_A"])
                for hh in range(4):
                    hr = half * 4 + hh
                    co = half * 512 + hh * 128
                    sel = selrow[:, hr * 128:(hr + 1) * 128]
                    S.op(TE, lambda e, sel=sel, co=co, cgh=cgh: e.matmul(
                        pT[:, co:co + 128], sel,
                        cgh[:, :], start=False, stop=False),
                        reads=[rgh, "selrow"], writes=["pT_A"])
                    S.op(TE, lambda e, sel=sel, co=co, cgl=cgl: e.matmul(
                        pT[:, co:co + 128], sel,
                        cgl[:, :], start=False, stop=False),
                        reads=[rgl, "selrow"], writes=["pT_A"])
                nce_h = nchl[:, g * 8 + half * 4:g * 8 + half * 4 + 4
                             ].unsqueeze(2).broadcast_to((128, 4, 128))
                nce_l = nchl[:, 32 + g * 8 + half * 4:32 + g * 8 + half * 4
                             + 4].unsqueeze(2).broadcast_to((128, 4, 128))
                S.op(TE, lambda e, pvm=pvm, nce_h=nce_h: e.matmul(
                    pvm, id_b[:, :], nce_h, start=False, stop=False),
                    reads=["nchl", "id_b"], writes=["pT_A"])
                S.op(TE, lambda e, pvm=pvm, nce_l=nce_l: e.matmul(
                    pvm, id_b[:, :], nce_l, start=False, stop=True),
                    reads=["nchl", "id_b"], writes=["pT_A"])
            S.op(A, lambda e, dpar=dpar: e.activation(
                dgT[dpar][:, :], pT[:, 0:1024], AF.Exp),
                reads=["pT_A"], writes=[f"dgT{dpar}"])

        for g in range(4):
            dpar = g % 2
            g8 = slice(g * 8, (g + 1) * 8)
            gsl = slice(g * 1024, (g + 1) * 1024)
            g512 = slice(g * 512, (g + 1) * 512)
            if g == 0 and not UNPIPE:
                emit_bc(0, 0)
            if UNPIPE:
                emit_bc(g, dpar)
            def xvar(eng, dst, dreg, src, sreg, key, g=g, gsl=gsl):
                sc = small[:, SM[key] + g * 8:SM[key] + g * 8 + 8]
                sce = sc.unsqueeze(2).broadcast_to((128, 8, 128))
                S.op(eng, lambda e, sce=sce, dst=dst, src=src, gsl=gsl:
                     e.tensor_tensor(
                         dst[:, :].rearrange("p (h q) -> p h q", h=8),
                         src[:, gsl].rearrange("p (h q) -> p h q", h=8),
                         sce, ALU.mult),
                     reads=[sreg, "sm_" + key], writes=[dreg])

            xvar(V, xga, "xga", xh, "xh", "scal_g")
            xvar(V, xba, "xba", xhp, "xhp", "scal_b")
            if g == 0:
                tap(f"xga{c}", xga[:, :], ["xga"])
                tap(f"xba{c}", xba[:, :], ["xba"])


            if g == 0:
                tap(f"dgT{c}", dgT[0][:, :], ["dgT0"])

            def wbuild(eng, dst, dreg, co):
                sge = sgT[:, co:co + 128].unsqueeze(1).broadcast_to(
                    (128, 8, 128))
                S.op(eng, lambda e, sge=sge, dst=dst, dpar=dpar:
                     e.tensor_tensor(
                         dst[:, :].rearrange("p (h l) -> p h l", h=8),
                         sge, dgT[dpar][:, :].rearrange("p (h l) -> p h l",
                                                        h=8),
                         ALU.mult),
                     reads=["sgT", f"dgT{dpar}"], writes=[dreg])

            if g + 1 < 4 and not UNPIPE:
                emit_bc(g + 1, (g + 1) % 2)
            S.op(A, lambda e, g512=g512: e.copy(sbfg[0:64, :], Sst[:, g512]),
                 reads=["Sst"], writes=["sbfg"])
            if g == 0:
                tap(f"sbfg{c}", sbfg[:, :], ["sbfg"])
            S.op(TE, lambda e: e.matmul(
                pY[:, 0:512], CrTz[0][:, :], sbfg[:, 0:512],
                start=True, stop=False),
                reads=["CrTz0", "sbfg"], writes=["pY"])
            dece = small[:, SM["tdec"] + g * 8:SM["tdec"] + g * 8 + 8
                         ].unsqueeze(2).broadcast_to((128, 8, 64))
            S.op(V, lambda e, dece=dece: e.tensor_tensor(
                ysc[:, :].rearrange("p (h q) -> p h q", h=8),
                pY[:, :].rearrange("p (h q) -> p h q", h=8),
                dece, ALU.mult),
                reads=["pY", "sm_tdec"], writes=["ysc"])
            if g == 0:
                tap(f"ysc{c}", ysc[:, :], ["ysc"])
            for r in range(2):
                wbuild(V, Wg, "Wg", r * 128)
                wbuild(V, Wb, "xnT", 256 + r * 128)
                if g == 0 and r == 0:
                    tap(f"Wg{c}", Wg[:, :], ["Wg"])
                    tap(f"Wb{c}", Wb[:, :], ["xnT"])
                for hh in range(8):
                    S.op(TE, lambda e, hh=hh, r=r: e.matmul(
                        pY[:, hh * 64:(hh + 1) * 64],
                        Wg[:, hh * 128:(hh + 1) * 128],
                        xga[:, hh * 128 + r * 64:hh * 128 + r * 64 + 64],
                        start=False, stop=False),
                        reads=["Wg", "xga", "ysc"], writes=["pY"])
                    S.op(TE, lambda e, hh=hh, r=r: e.matmul(
                        pY[:, hh * 64:(hh + 1) * 64],
                        Wb[:, hh * 128:(hh + 1) * 128],
                        xba[:, hh * 128 + r * 64:hh * 128 + r * 64 + 64],
                        start=False, stop=(r == 1 and hh == 7)),
                        reads=["xnT", "xba"], writes=["pY"])
            xvar(V, xga, "xga", xh, "xh", "scal_gd")
            xvar(V, xba, "xba", xhp, "xhp", "scal_bd")
            if g == 0:
                tap(f"xgd{c}", xga[:, :], ["xga"])
                tap(f"xbd{c}", xba[:, :], ["xba"])
            S.op(V, lambda e: e.tensor_tensor(ysum[:, :], pY[:, :], ysc[:, :],
                                              ALU.add),
                 reads=["pY", "ysc"], writes=["Wg"])
            if g == 0:
                tap(f"ysum{c}", ysum[:, :], ["Wg"])
            S.op(V, lambda e, g512=g512: e.tensor_tensor(
                YG[:, g512], ysum[:, :], sz[:, g512], ALU.mult),
                reads=["Wg", "sz"], writes=["YG"])
            pSTv = pST[:, :].rearrange("p (h q) -> p h q", h=8)
            for si in range(2):
                lb, lreg = (brt, "brt") if si == 0 else (bbt, "bbt")
                xv, xr2 = (xga, "xga") if si == 0 else (xba, "xba")
                for r in range(2):
                    st_f = (si == 0 and r == 0)
                    sp_f = (si == 1 and r == 1)
                    xvv = xv[:, :].rearrange("p (h q) -> p h q",
                                             h=8)[:, :, r * 64:(r + 1) * 64]
                    S.op(TE, lambda e, r=r, lb=lb, xvv=xvv, st_f=st_f,
                         sp_f=sp_f: e.matmul(
                             pSTv, lb[:, r * 64:(r + 1) * 64],
                             xvv, start=st_f, stop=sp_f),
                         reads=[lreg, xr2], writes=["pST"])
            cde = CD[:, g * 8:(g + 1) * 8].unsqueeze(2).broadcast_to(
                (64, 8, 64))
            S.op(V, lambda e, cde=cde, g512=g512: e.tensor_tensor(
                stmp[:, :].rearrange("p (h q) -> p h q", h=8),
                Sst[:, g512].rearrange("p (h q) -> p h q", h=8),
                cde, ALU.mult),
                reads=["Sst", "CD"], writes=["xga"])
            S.op(V, lambda e, g512=g512: e.tensor_tensor(
                Sst[:, g512], stmp[:, :], pST[:, :], ALU.add),
                reads=["xga", "pST"], writes=["Sst"])
            if g == 3:
                tap(f"Sst{c}", Sst[:, :], ["Sst"])
        tap(f"y{c}", YG[:, :], ["YG"])
        S.dma("gpsimd", bbt[0:1, :], brt[127:128, :], reads=["brt"],
              writes=["bbt"])

        S.op(A, lambda e: e.activation(sz[:, 0:2048], YG[:, :], AF.Square,
                                       accum_out=smcol[:, 13:14]),
             reads=["YG"], writes=["smc13", "sz"])
        S.op(A, lambda e: e.activation(smcol[:, 12:13], smcol[:, 13:14],
                                       AF.Ln, scale=1.0 / DI,
                                       bias=cb_ln[:, 2:3]),
             reads=["smc13", "cb_ln"], writes=["smc12"])
        S.op(A, lambda e: e.activation(smcol[:, 12:13], smcol[:, 12:13],
                                       AF.Exp, scale=-0.5),
             reads=["smc12"], writes=["smc12"])
        S.op(V, lambda e: e.tensor_scalar(sz[:, :], YG[:, :],
                                          smcol[:, 12:13], None, ALU.mult),
             reads=["YG", "smc12"], writes=["sz"])
        tap(f"ynorm{c}", sz[:, :], ["sz"])
        if c + 1 < NCHUNK:
            emit_ln(c + 1)
        for half in range(2):
            for kk in range(8):
                k = half * 8 + kk
                S.op(TE, lambda e, kk=kk, k=k: e.transpose(
                    pTB[:, kk * 128:(kk + 1) * 128],
                    sz[:, k * 128:(k + 1) * 128], id_b[:, :]),
                    reads=["sz", "id_b"], writes=["pTB"])
            S.op(A, lambda e: e.copy(ynT[:, :], pTB[:, 0:1024]),
                 reads=["pTB"], writes=["xga"])
            for fg in range(2):
                pa, preg = pA[fg], f"pA{fg}"
                for kk in range(8):
                    k = half * 8 + kk
                    S.op(TE, lambda e, kk=kk, k=k, fg=fg, pa=pa: e.matmul(
                        pa[:, :], ynT[:, kk * 128:(kk + 1) * 128],
                        WOUT[:, k * 1024 + fg * 512:k * 1024 + fg * 512
                             + 512],
                        start=(k == 0), stop=(k == 15)),
                        reads=["xga", f"WOUT{k // 4}"], writes=[preg])
        for fg in range(2):
            pa, preg = pA[fg], f"pA{fg}"
            S.op(A, lambda e, fg=fg, pa=pa: e.copy(
                s_sb[:, fg * 512:(fg + 1) * 512], pa[:, :]),
                reads=[preg, "sz"], writes=["sz"])
        tap(f"s{c}", s_sb[:, :], ["sz"])
        S.dma("sync", s_d[c * 128:(c + 1) * 128, :], s_sb[:, :],
              reads=["sz"], writes=[f"sd{c}"])

    if DO_FFN:
        S.dma("gpsimd", gtrow[:, :], ccout_d.flatten().unsqueeze(0),
              reads=["ccout"], writes=["gtrow"])
        S.op(V, lambda e: e.tensor_tensor(gtrow[:, :], gtrow[:, :],
                                          cmask[:, :], ALU.mult),
             reads=["gtrow", "cmask"], writes=["gtrow"])
        S.op(V, lambda e: e.tensor_reduce(
            myoff[:, :].unsqueeze(2),
            gtrow[:, :].rearrange("o (a e) -> o e a", a=8), axis=AX.X,
            op=ALU.add),
            reads=["gtrow"], writes=["myoff"])
        moff_b = myoff[:, :].unsqueeze(1).broadcast_to((1, 32, 8))
        S.op(V, lambda e: e.tensor_tensor(
            OFFrow[:, :].rearrange("o (cc ee) -> o cc ee", ee=8),
            OFFrow[:, :].rearrange("o (cc ee) -> o cc ee", ee=8),
            moff_b, ALU.add),
            reads=["OFFrow", "myoff"], writes=["OFFrow"])
        S.op(V, lambda e: e.tensor_copy(OFFrh[:, :], OFFrow[:, :]),
             reads=["OFFrow"], writes=["OFFrh"])
        S.op(V, lambda e: e.tensor_tensor(OFFrl[:, :], OFFrow[:, :],
                                          OFFrh[:, :], ALU.subtract),
             reads=["OFFrow", "OFFrh"], writes=["OFFrl"])
        S.op(V, lambda e: e.tensor_copy(myoh[:, :], myoff[:, :]),
             reads=["myoff"], writes=["myoh"])
        S.op(V, lambda e: e.tensor_tensor(myol[:, :], myoff[:, :],
                                          myoh[:, :], ALU.subtract),
             reads=["myoff", "myoh"], writes=["myol"])
        S.op(TE, lambda e: e.matmul(pT[:, MB:MB + 8], ones_b[:, 0:128],
                                    myoh[:, :], start=True, stop=False),
             reads=["myoh", "ones_b"], writes=["pT_M"])
        S.op(TE, lambda e: e.matmul(pT[:, MB:MB + 8], ones_b[:, 0:128],
                                    myol[:, :], start=False, stop=True),
             reads=["myol", "ones_b"], writes=["pT_M"])
        S.op(V, lambda e: e.tensor_copy(MYOFF[:, :], pT[:, MB:MB + 8]),
             reads=["pT_M"], writes=["MYOFF"])

    ctx1.close()
    if DO_FFN:
        _build_ffn(nc, S, ctx, sb, locals())

    S.finalize()
    block = ctx.enter_context(nc.Block())
    S.build(nc, block, ctx)
    ctx.close()
    return nc


def _build_ffn(nc, S, ctx, sb, env):
    V, A, G_, TE = "vector", "scalar", "gpsimd", "tensor"
    pT, pTB, pA, pY = env["pT"], env["pTB"], env["pA"], env["pY"]
    ones_b, triu32, id_f, id_b = (env["ones_b"], env["triu32"], env["id_f"],
                                  env["id_b"])
    SCNT, IDX, GATE, iota8, cmask = (env["SCNT"], env["IDX"], env["GATE"],
                                     env["iota8"], env["cmask"])
    emk, xc = env["emk"], env["xc"]
    xn_d, s_d, eb_d, eo_d = env["xn_d"], env["s_d"], env["eb_d"], env["eo_d"]
    x_h, out_h = env["x_h"], env["out_h"]
    w1_h, b1_h, w2_h, b2_h = env["w1_h"], env["b1_h"], env["w2_h"], env["b2_h"]
    ebz_keys, tap = env["ebz_keys"], env["tap"]
    OFFrh, OFFrl, MYOFF = env["OFFrh"], env["OFFrl"], env["MYOFF"]
    MB = 1024
    S.auto_extra = [S.fence()]

    DEST_u = sb("DEST_u", (128, 32), U32)
    GSRC_u = sb("GSRC_u", (128, 32), U32)
    GATEM = sb("GATEM", (128, 32))

    emaskall = sb("emaskall", (128, 256), BF16)
    t256 = sb("t256", (128, 256))
    POS = sb("POS", (128, 32))
    MYT = sb("MYT", (128, 32))
    SLOT = sb("SLOT", (128, 32))
    KEEP2 = sb("KEEP2", (128, 32))
    D1 = sb("D1", (128, 32))
    S.op(TE, lambda e: e.matmul(pT[:, MB:MB + 256], ones_b[:, 0:128],
                                OFFrh[:, :], start=True, stop=False),
         reads=["OFFrh", "ones_b"], writes=["pT_M"])
    S.op(TE, lambda e: e.matmul(pT[:, MB:MB + 256], ones_b[:, 0:128],
                                OFFrl[:, :], start=False, stop=True),
         reads=["OFFrl", "ones_b"], writes=["pT_M"])
    idxe = IDX[:, :].unsqueeze(2).broadcast_to((128, 32, 8))
    iote = iota8[:, :].unsqueeze(1).broadcast_to((128, 32, 8))
    emv = emaskall[:, :].rearrange("p (t e) -> p t e", e=8)
    S.op(V, lambda e: e.tensor_tensor(emv, idxe, iote, ALU.is_equal),
         reads=["IDX", "iota8"], writes=["emaskall"])
    S.op(V, lambda e: e.tensor_tensor(t256[:, :], SCNT[:, :],
                                      pT[:, MB:MB + 256], ALU.add),
         reads=["SCNT", "pT_M"], writes=["t256"])
    S.op(V, lambda e: e.tensor_tensor(t256[:, :], t256[:, :], emaskall[:, :],
                                      ALU.mult),
         reads=["t256", "emaskall"], writes=["t256"])
    S.op(V, lambda e: e.tensor_reduce(
        POS[:, :].unsqueeze(2), t256[:, :].rearrange("p (t e) -> p t e", e=8),
        axis=AX.X, op=ALU.add),
        reads=["t256"], writes=["POS"])
    mye = MYOFF[:, :].unsqueeze(1).broadcast_to((128, 32, 8))
    S.op(V, lambda e: e.tensor_tensor(
        t256[:, :].rearrange("p (t e) -> p t e", e=8), mye,
        emaskall[:, :].rearrange("p (t e) -> p t e", e=8), ALU.mult),
        reads=["MYOFF", "emaskall"], writes=["t256"])
    S.op(V, lambda e: e.tensor_reduce(
        MYT[:, :].unsqueeze(2), t256[:, :].rearrange("p (t e) -> p t e", e=8),
        axis=AX.X, op=ALU.add),
        reads=["t256"], writes=["MYT"])
    S.op(V, lambda e: e.scalar_tensor_tensor(SLOT[:, :], POS[:, :], -1.0,
                                             MYT[:, :], ALU.add,
                                             ALU.subtract),
         reads=["POS", "MYT"], writes=["SLOT"])
    S.op(V, lambda e: e.tensor_scalar(KEEP2[:, :], SLOT[:, :], CAP - 0.5,
                                      None, ALU.is_le),
         reads=["SLOT"], writes=["KEEP2"])
    S.op(V, lambda e: e.tensor_scalar(t256[:, 0:32], POS[:, :], 4096.0, None,
                                      ALU.is_le),
         reads=["POS"], writes=["t256"])
    S.op(V, lambda e: e.tensor_tensor(KEEP2[:, :], KEEP2[:, :], t256[:, 0:32],
                                      ALU.mult),
         reads=["KEEP2", "t256"], writes=["KEEP2"])
    S.op(V, lambda e: e.scalar_tensor_tensor(D1[:, :], IDX[:, :], float(CAP),
                                             SLOT[:, :], ALU.mult, ALU.add),
         reads=["IDX", "SLOT"], writes=["D1"])
    S.op(V, lambda e: e.tensor_scalar(t256[:, 0:32], KEEP2[:, :], -BIG, BIG,
                                      ALU.mult, ALU.add),
         reads=["KEEP2"], writes=["t256"])
    S.op(V, lambda e: e.tensor_tensor(t256[:, 32:64], D1[:, :], t256[:, 0:32],
                                      ALU.add),
         reads=["D1", "t256"], writes=["t256"])
    S.op(V, lambda e: e.tensor_copy(DEST_u[:, :], t256[:, 32:64]),
         reads=["t256"], writes=["DEST_u"])
    S.op(V, lambda e: e.tensor_tensor(t256[:, 0:32], D1[:, :], KEEP2[:, :],
                                      ALU.mult),
         reads=["D1", "KEEP2"], writes=["t256"])
    S.op(V, lambda e: e.tensor_copy(GSRC_u[:, :], t256[:, 0:32]),
         reads=["t256"], writes=["GSRC_u"])
    S.op(V, lambda e: e.tensor_tensor(GATEM[:, :], GATE[:, :], KEEP2[:, :],
                                      ALU.mult),
         reads=["GATE", "KEEP2"], writes=["GATEM"])
    tap("gatem", GATEM[:, :], ["GATEM"])
    tap("destu", DEST_u[:, :], ["DEST_u"])
    tap("gsrcu", GSRC_u[:, :], ["GSRC_u"])
    tap("posx", POS[:, :], ["POS"])
    tap("keep2", KEEP2[:, :], ["KEEP2"])

    xsc = [sb(f"xsc{i}", (128, D), BF16) for i in range(4)]
    for t in range(32):
        p4 = t % 4
        S.dma("sync", xsc[p4][:, :], xn_d[t * 128:(t + 1) * 128, :],
              reads=[f"xnd{t}"], writes=[f"xsc{p4}"])
        S.indirect(eb_d, bass.IndirectOffsetOnAxis(ap=DEST_u[:, t:t + 1],
                                                   axis=0),
                   xsc[p4][:, :], None,
                   reads=[f"xsc{p4}", "DEST_u"] + ebz_keys,
                   writes=[f"eb{t}"])
    eb_keys = [f"eb{t}" for t in range(32)] + ebz_keys

    CT = CAP // 128
    XeT = sb("XeT", (128, 8 * CAP), BF16)
    hT = sb("hT", (128, 32 * CAP), BF16)
    w1sb = [sb("w1sb0", (128, 4096), BF16), sb("w1sb1", (128, 4096), BF16)]
    w2res = sb("w2res", (128, 32 * 1024), BF16)
    b1c32 = sb("b1c32", (32, 128))
    b1cols = sb("b1cols", (128, 32))
    b2row = sb("b2row", (1, 1024))
    b2h = sb("b2h", (128, 1024), BF16)
    b2l = sb("b2l", (128, 1024), BF16)
    onepad = env["onepad"]
    S.op(V, lambda e: e.memset(b2h[:, :], 0.0), writes=["b2h"])
    S.op(V, lambda e: e.memset(b2l[:, :], 0.0), writes=["b2l"])
    outsb = [sb("outsb0", (128, D), BF16), sb("outsb1", (128, D), BF16)]
    xe_sb = [sb("xesb0", (128, D), BF16), sb("xesb1", (128, D), BF16)]

    psl = [(pT, 0, "pT_A"), (pT, 512, "pT_A"), (pA[0], 0, "pA0"),
           (pA[1], 0, "pA1"), (pY, 0, "pY"), (pT, 1024, "pT_M")]
    w1g = w1_h.rearrange("ee (k p) (tg j) -> ee p k tg j", p=128, j=512)
    XeT_v = XeT[:, :].rearrange("p (k cc) -> p k cc", k=8)

    def w1load(ex, tg):
        p2 = tg % 2
        S.dma("gpsimd", w1sb[p2][:, :].rearrange("p (k j) -> p k j", k=8),
              w1g[ex, :, :, tg, :], writes=[f"w1sb{p2}"])

    for ex in range(NEXP):
        w1load(ex, 0)
        S.dma("sync", b1c32[:, :],
              b1_h[ex:ex + 1, :].rearrange("o (t p) -> o t p", p=128
                                           ).squeeze(0),
              writes=["b1c32"])
        S.op(TE, lambda e: e.transpose(pT[:, MB:MB + 32], b1c32[:, :],
                                       id_f[0:32, 0:32]),
             reads=["b1c32", "id_f"], writes=["pT_M"])
        S.op(V, lambda e: e.tensor_copy(b1cols[:, :], pT[:, MB:MB + 32]),
             reads=["pT_M"], writes=["b1cols"])
        S.dma("sync", b2row[:, :], b2_h[ex:ex + 1, :], writes=["b2row"])
        S.op(V, lambda e: e.tensor_copy(b2h[0:1, :], b2row[:, :]),
             reads=["b2row"], writes=["b2h"])
        S.op(V, lambda e: e.tensor_tensor(b2l[0:1, :], b2row[:, :],
                                          b2h[0:1, :], ALU.subtract),
             reads=["b2row", "b2h"], writes=["b2l"])
        for i in range(CT):
            p2 = i % 2
            S.dma("sync", xe_sb[p2][:, :],
                  eb_d[ex * CAP + i * 128:ex * CAP + (i + 1) * 128, :],
                  reads=eb_keys, writes=[f"xesb{p2}"])
            for k in range(8):
                S.op(TE, lambda e, k=k, p2=p2: e.transpose(
                    pTB[:, k * 128:(k + 1) * 128],
                    xe_sb[p2][:, k * 128:(k + 1) * 128], id_b[:, :]),
                    reads=[f"xesb{p2}", "id_b"], writes=["pTB"])
            S.op(V, lambda e, i=i: e.tensor_copy(
                XeT_v[:, :, i * 128:(i + 1) * 128],
                pTB[:, 0:1024].rearrange("p (k q) -> p k q", k=8)),
                reads=["pTB"], writes=["XeT"])
        for tg in range(8):
            p2 = tg % 2
            if tg + 1 < 8:
                w1load(ex, tg + 1)
            for ftl in range(4):
                ft = tg * 4 + ftl
                for half, hw in ((0, 512), (1, CAP - 512)):
                    pa, preg = pA[half], f"pA{half}"
                    hsl = slice(half * 512, half * 512 + hw)
                    for k in range(8):
                        S.op(TE, lambda e, k=k, ftl=ftl, hsl=hsl, p2=p2,
                             pa=pa, hw=hw:
                             e.matmul(pa[:, 0:hw],
                                      w1sb[p2][:, k * 512 + ftl * 128:
                                               k * 512 + (ftl + 1) * 128],
                                      XeT_v[:, k, hsl],
                                      start=(k == 0), stop=(k == 7)),
                             reads=[f"w1sb{p2}", "XeT"], writes=[preg])
                    S.op(A, lambda e, ft=ft, hsl=hsl, hw=hw, pa=pa:
                         e.activation(
                             hT[:, ft * CAP + hsl.start:
                                ft * CAP + hsl.start + hw],
                             pa[:, 0:hw], AF.Relu, bias=b1cols[:, ft:ft + 1]),
                         reads=[preg, "b1cols"], writes=["hT"])
        for k in range(32):
            S.dma("gpsimd", w2res[:, k * 1024:(k + 1) * 1024],
                  w2_h[ex, k * 128:(k + 1) * 128, :], writes=[f"w2res{k}"])
        for ct in range(CT):
            p2 = ct % 2
            for dh in range(2):
                ps, pco, preg = psl[(ct % 3) * 2 + dh]
                S.op(TE, lambda e, ps=ps, pco=pco, dh=dh: e.matmul(
                    ps[:, pco:pco + 512], onepad[:, :],
                    b2h[:, dh * 512:(dh + 1) * 512], start=True, stop=False),
                    reads=["b2h", "onepad"], writes=[preg])
                S.op(TE, lambda e, ps=ps, pco=pco, dh=dh: e.matmul(
                    ps[:, pco:pco + 512], onepad[:, :],
                    b2l[:, dh * 512:(dh + 1) * 512], start=False, stop=False),
                    reads=["b2l", "onepad"], writes=[preg])
                for k in range(32):
                    S.op(TE, lambda e, ps=ps, pco=pco, k=k, ct=ct, dh=dh:
                         e.matmul(
                             ps[:, pco:pco + 512],
                             hT[:, k * CAP + ct * 128:
                                k * CAP + (ct + 1) * 128],
                             w2res[:, k * 1024 + dh * 512:
                                   k * 1024 + dh * 512 + 512],
                             start=False, stop=(k == 31)),
                         reads=["hT", f"w2res{k}"], writes=[preg])
            for dh in range(2):
                ps, pco, preg = psl[(ct % 3) * 2 + dh]
                S.op(V, lambda e, ps=ps, pco=pco, dh=dh, p2=p2: e.tensor_copy(
                    outsb[p2][:, dh * 512:(dh + 1) * 512],
                    ps[:, pco:pco + 512]),
                    reads=[preg], writes=[f"outsb{p2}"])
            S.dma("sync",
                  eo_d[ex * CAP + ct * 128:ex * CAP + (ct + 1) * 128, :],
                  outsb[p2][:, :], reads=[f"outsb{p2}"],
                  writes=[f"eo{ex}_{ct}"])
    eo_keys = [f"eo{ex}_{ct}" for ex in range(NEXP) for ct in range(CT)]

    if "ebdump" in DEBUG_TAPS:
        ebo = nc.dram_tensor("dbg_ebdump", [NEXP * CAP, D], BF16,
                             kind="ExternalOutput").ap()
        eoo = nc.dram_tensor("dbg_eodump", [NEXP * CAP, D], F32,
                             kind="ExternalOutput").ap()
        for i in range(NEXP * CAP // 512):
            sl_ = slice(i * 512, (i + 1) * 512)
            S.dma("sync", ebo[sl_, :], eb_d[sl_, :], reads=eb_keys)
            S.dma("sync", eoo[sl_, :], eo_d[sl_, :], reads=eo_keys)
    eo_sb = [outsb[0], outsb[1], sb("eosb2", (128, D), BF16),
             sb("eosb3", (128, D), BF16)]
    eo_rn = ["outsb0", "outsb1", "eosb2", "eosb3"]
    svb = [sb(f"svb{i}", (128, D)) for i in range(2)]
    gbuf = [sb(f"gbuf{i}", (128, D)) for i in range(2)]
    outb = [sb(f"outb{i}", (128, D)) for i in range(2)]
    for t in range(32):
        p4, p2 = t % 4, t % 2
        S.indirect(eo_sb[p4][:, :], None, eo_d,
                   bass.IndirectOffsetOnAxis(ap=GSRC_u[:, t:t + 1], axis=0),
                   reads=eo_keys + ["GSRC_u"], writes=[eo_rn[p4]]),
        tap(f"eo{t}", eo_sb[p4][:, :], [eo_rn[p4]])
        S.dma("sync", xc[p2][:, :], x_h[t * 128:(t + 1) * 128, :],
              writes=[f"xc{p2}"])
        S.dma("scalar", svb[p2][:, :], s_d[t * 128:(t + 1) * 128, :],
              reads=[f"sd{t}"], writes=[f"svb{p2}"])
        S.op(A, lambda e, t=t, p4=p4, p2=p2: e.activation(
            gbuf[p2][:, :], eo_sb[p4][:, :], AF.Silu,
            scale=GATEM[:, t:t + 1]),
            reads=[eo_rn[p4], "GATEM"], writes=[f"gbuf{p2}"])
        S.op(V, lambda e, p2=p2: e.tensor_tensor(
            gbuf[p2][:, :], gbuf[p2][:, :], svb[p2][:, :], ALU.mult),
            reads=[f"gbuf{p2}", f"svb{p2}"], writes=[f"gbuf{p2}"])
        S.op(V, lambda e, p2=p2: e.tensor_tensor(
            outb[p2][:, :], gbuf[p2][:, :], xc[p2][:, :], ALU.add),
            reads=[f"gbuf{p2}", f"xc{p2}"], writes=[f"outb{p2}"])
        S.dma("sync", out_h[t * 128:(t + 1) * 128, :], outb[p2][:, :],
              reads=[f"outb{p2}"], writes=[f"out{t}"])

    if "findump" in DEBUG_TAPS:
        outk = [f"out{t}" for t in range(32)]
        feo = nc.dram_tensor("dbg_feo", [NEXP * CAP, D], F32,
                             kind="ExternalOutput").ap()
        fsd = nc.dram_tensor("dbg_fsd", [T, D], F32,
                             kind="ExternalOutput").ap()
        fgs = nc.dram_tensor("dbg_fgs", [128, 32], U32,
                             kind="ExternalOutput").ap()
        fgm = nc.dram_tensor("dbg_fgm", [128, 32], F32,
                             kind="ExternalOutput").ap()
        fxc = nc.dram_tensor("dbg_fxc", [256, D], F32,
                             kind="ExternalOutput").ap()
        fes = nc.dram_tensor("dbg_fes", [256, D], F32,
                             kind="ExternalOutput").ap()
        for i in range(12):
            sl_ = slice(i * 512, (i + 1) * 512)
            S.dma("sync", feo[sl_, :], eo_d[sl_, :],
                  reads=eo_keys, extra=[S.regions["out31"][0]])
        for i in range(8):
            sl_ = slice(i * 512, (i + 1) * 512)
            S.dma("sync", fsd[sl_, :], s_d[sl_, :],
                  extra=[S.regions["out31"][0]])
        S.dma("sync", fgs[:, :], GSRC_u[:, :], reads=["GSRC_u"],
              extra=[S.regions["out31"][0]])
        S.dma("sync", fgm[:, :], GATEM[:, :], reads=["GATEM"],
              extra=[S.regions["out31"][0]])
        S.dma("sync", fxc[0:128, :], xc[0][:, :], reads=["xc0"],
              extra=[S.regions["out31"][0]])
        S.dma("sync", fxc[128:256, :], xc[1][:, :], reads=["xc1"],
              extra=[S.regions["out31"][0]])
        S.dma("sync", fes[0:128, :], eo_sb[0][:, :], reads=["eosb0"],
              extra=[S.regions["out31"][0]])
        S.dma("sync", fes[128:256, :], eo_sb[1][:, :], reads=["eosb1"],
              extra=[S.regions["out31"][0]])


def kernel(L0_attn_x, L0_attn_W_in, L0_attn_A_log, L0_attn_dt_bias,
           L0_attn_B_norm_bias, L0_attn_C_norm_bias, L0_attn_W_out,
           L0_ffn_router_w, L0_ffn_w1, L0_ffn_b1, L0_ffn_w2, L0_ffn_b2):
    x = np.ascontiguousarray(np.asarray(L0_attn_x, dtype=np.float32))
    consts = _consts()
    nc = build_program()
    in_maps = []
    for b in range(B):
        cm = np.zeros((1, 64), np.float32)
        cm[0, :b * 8] = 1.0
        m = dict(
            x=x[b],
            W_in=np.ascontiguousarray(np.asarray(L0_attn_W_in, np.float32)),
            A_log=np.asarray(L0_attn_A_log, np.float32).reshape(1, H),
            dt_bias=np.asarray(L0_attn_dt_bias, np.float32).reshape(1, H),
            B_nb=np.asarray(L0_attn_B_norm_bias, np.float32).reshape(1, N),
            C_nb=np.asarray(L0_attn_C_norm_bias, np.float32).reshape(1, N),
            W_out=np.ascontiguousarray(np.asarray(L0_attn_W_out, np.float32)),
            router_w=np.ascontiguousarray(
                np.asarray(L0_ffn_router_w, np.float32)),
            w1=np.ascontiguousarray(np.asarray(L0_ffn_w1, np.float32)),
            b1=np.ascontiguousarray(np.asarray(L0_ffn_b1, np.float32)),
            w2=np.ascontiguousarray(np.asarray(L0_ffn_w2, np.float32)),
            b2=np.ascontiguousarray(np.asarray(L0_ffn_b2, np.float32)),
            coremask64=cm,
        )
        m.update(consts)
        in_maps.append(m)
    res = run_bass_kernel_spmd(nc, in_maps, core_ids=list(range(B)))
    return np.stack([res.results[b]["out"] for b in range(B)], axis=0)


# version 32 (final)
# speedup vs baseline: 1.1720x; 1.0216x over previous
import numpy as np
import ml_dtypes
import concourse.bass as bass
import concourse.mybir as mybir
from concourse.bass_utils import run_bass_kernel_spmd

F32 = mybir.dt.float32
BF16 = mybir.dt.bfloat16
I32 = mybir.dt.int32
U32 = mybir.dt.uint32
AF = mybir.ActivationFunctionType
ALU = mybir.AluOpType
AX = mybir.AxisListType

B = 8
T = 4096
D = 1024
DP = 6432
DI = 2048
H = 32
P = 64
N = 64
L = 128
SCALE = float((N * 2) ** -0.5)
EPS = 1e-6
NEXP = 8
FF = 4096
CAP = 640
BIG = 1.0e6
TWO_PI = float(2.0 * np.pi)
HALF_PI = float(0.5 * np.pi)
PI = float(np.pi)

NCHUNK = 32
DO_FFN = True
DEBUG_TAPS = []
UNPIPE = False

ENG = ("sync", "tensor", "vector", "scalar", "gpsimd")
DMAQ = ("sync", "scalar", "gpsimd")
KPOOL = 12
AUTO_ORDER = ("tensor",)


class Sched:

    def __init__(self):
        self.sem_names = ["e_tensor", "e_vector", "e_scalar", "e_gpsimd", "cc"]
        self.sem_names += [f"q_{q}{i}" for q in DMAQ for i in range(KPOOL)]
        self.sem_index = {n: i for i, n in enumerate(self.sem_names)}
        self.nsem = len(self.sem_names)
        self.items = {e: [] for e in ENG}
        self.ecount = dict(tensor=0, vector=0, scalar=0, gpsimd=0)
        self.qcount = {q: 0 for q in DMAQ}
        self.cc_count = 0
        self.satisfied = {e: np.zeros(self.nsem, np.int64) for e in ENG}
        self.tokens = []
        self.regions = {}
        self.auto_extra = []

    def _dep_tokens(self, reads, writes):
        toks = list(self.auto_extra)
        for r in reads:
            reg = self.regions.get(r)
            if reg and reg[0] is not None:
                toks.append(reg[0])
        for w in writes:
            reg = self.regions.get(w)
            if reg:
                if reg[0] is not None:
                    toks.append(reg[0])
                toks.extend(reg[1])
        return toks

    def _update_regions(self, tok, reads, writes):
        for r in reads:
            self.regions.setdefault(r, [None, []])[1].append(tok)
        for w in writes:
            self.regions[w] = [tok, []]

    def _resolve(self, e, toks, forced=()):
        sat = self.satisfied[e]
        want = {}
        for t in toks:
            semi, th, _ = self.tokens[t]
            if sat[semi] < th:
                want[semi] = max(want.get(semi, 0), th)
        for (semname, val) in forced:
            si = self.sem_index[semname]
            if val > 0 and sat[si] < val:
                want[si] = max(want.get(si, 0), val)
        waits = sorted(want.items())
        for si, val in waits:
            if sat[si] < val:
                sat[si] = val
        for t in toks:
            np.maximum(sat, self.tokens[t][2], out=sat)
        return waits

    def _mk_token(self, e, semname, th):
        vc = self.satisfied[e].copy()
        si = self.sem_index[semname]
        vc[si] = max(vc[si], th)
        tok = len(self.tokens)
        self.tokens.append((si, th, vc))
        return tok

    def op(self, e, fn, reads=(), writes=(), extra=()):
        toks = self._dep_tokens(reads, writes) + [t for t in extra
                                                 if t is not None]
        waits = self._resolve(e, toks)
        self.ecount[e] += 1
        semname = "e_" + e
        tok = self._mk_token(e, semname, self.ecount[e])
        if e in AUTO_ORDER:
            self.satisfied[e][self.sem_index[semname]] = self.ecount[e]
        self.items[e].append((waits, fn, semname, 1))
        self._update_regions(tok, reads, writes)
        return tok

    def _dma_common(self, q, fn, reads, writes, extra):
        i = self.qcount[q]
        self.qcount[q] += 1
        semname = f"q_{q}{i % KPOOL}"
        rounds = i // KPOOL
        th = 16 * (rounds + 1)
        toks = self._dep_tokens(reads, writes) + [t for t in extra
                                                  if t is not None]
        forced = [(semname, 16 * rounds)] if rounds > 0 else []
        waits = self._resolve(q, toks, forced)
        tok = self._mk_token(q, semname, th)
        self.items[q].append((waits, fn, semname, 16))
        self._update_regions(tok, reads, writes)
        return tok

    def dma(self, q, out, in_, reads=(), writes=(), extra=(), **kw):
        def fn(eng, out=out, in_=in_, kw=kw):
            return eng.dma_start(out=out, in_=in_, **kw)
        return self._dma_common(q, fn, reads, writes, extra)

    def indirect(self, out, out_off, in_, in_off, reads=(), writes=(),
                 extra=()):
        def fn(eng, out=out, in_=in_):
            return eng.indirect_dma_start(out, out_off, in_, in_off,
                                          oob_is_err=False)
        return self._dma_common("gpsimd", fn, reads, writes, extra)

    def collective(self, fn, reads=(), writes=(), extra=()):
        toks = self._dep_tokens(reads, writes) + [t for t in extra
                                                  if t is not None]
        waits = self._resolve("gpsimd", toks)
        self.cc_count += 1
        tok = self._mk_token("gpsimd", "cc", self.cc_count)
        self.items["gpsimd"].append((waits, fn, "cc", 1))
        self._update_regions(tok, reads, writes)
        return tok

    def fence(self):
        forced = []
        for e in ("tensor", "vector", "scalar"):
            if self.ecount[e]:
                forced.append(("e_" + e, self.ecount[e]))
        for q in DMAQ:
            n = self.qcount[q]
            for i in range(min(n, KPOOL)):
                forced.append((f"q_{q}{i}", 16 * ((n - 1 - i) // KPOOL + 1)))
        waits = self._resolve("gpsimd", [], forced)
        self.ecount["gpsimd"] += 1
        tok = self._mk_token("gpsimd", "e_gpsimd",
                             self.ecount["gpsimd"])
        self.items["gpsimd"].append((waits, lambda eng: eng.nop(),
                                     "e_gpsimd", 1))
        return tok

    def finalize(self):
        forced = []
        for e in ("tensor", "vector", "scalar"):
            if self.ecount[e]:
                forced.append(("e_" + e, self.ecount[e]))
        for q in DMAQ:
            n = self.qcount[q]
            for i in range(min(n, KPOOL)):
                forced.append((f"q_{q}{i}", 16 * ((n - 1 - i) // KPOOL + 1)))
        waits = self._resolve("gpsimd", [], forced)
        self.items["gpsimd"].append((waits, lambda eng: eng.nop(), None, 0))

    def build(self, nc, block, ctx):
        sems = {n: ctx.enter_context(nc.semaphore(n)) for n in self.sem_names}
        names = self.sem_names
        for e in ENG:
            if not self.items[e]:
                continue

            def mk(e):
                items = self.items[e]

                def prog(eng):
                    for waits, fn, semname, inc in items:
                        for (si, val) in waits:
                            eng.wait_ge(sems[names[si]], val)
                        inst = fn(eng)
                        if inc:
                            inst.then_inc(sems[semname], inc)
                return prog

            getattr(block, e)(mk(e))


def _consts():
    bf = ml_dtypes.bfloat16
    c = {}
    c["id_f"] = np.eye(128, dtype=np.float32)
    c["id_b"] = np.eye(128, dtype=np.float32).astype(bf)
    c["triu_incl"] = np.triu(np.ones((128, 128), np.float32)).astype(bf)
    c["triu32_strict"] = np.triu(np.ones((32, 32), np.float32), 1).astype(bf)
    mask = np.where(np.triu(np.ones((128, 128), np.float32)) > 0, 0.0, -1e30)
    c["maskneg4"] = mask.astype(bf)
    c["maskscale"] = (np.triu(np.ones((128, 128), np.float32)) * SCALE
                      ).astype(np.float32)
    c["ones_b"] = np.ones((1, 128), np.float32).astype(bf)
    sel = np.zeros((128, 1024), np.float32)
    sel[0:8] = np.repeat(np.eye(8, dtype=np.float32), 128, axis=1)
    c["selrow32"] = sel.astype(bf)
    op = np.zeros((128, 128), np.float32)
    op[0] = 1.0
    c["onepad"] = op.astype(bf)
    c["shift1"] = np.eye(128, k=1, dtype=np.float32).astype(bf)
    s127 = np.zeros((128, 128), np.float32)
    s127[127, 0] = 1.0
    c["sel127"] = s127.astype(bf)
    c["iota8"] = np.tile(np.arange(8, dtype=np.float32)[None, :], (128, 1))
    c["iotamb"] = c["iota8"] - BIG
    return c


def build_program():
    from contextlib import ExitStack
    nc = bass.Bass("TRN2", target_bir_lowering=False, debug=False)
    S = Sched()
    ctx = ExitStack()

    def din(name, shape, dt=F32):
        return nc.dram_tensor(name, list(shape), dt, kind="ExternalInput").ap()

    x_h = din("x", (T, D))
    win_h = din("W_in", (D, DP))
    alog_h = din("A_log", (1, H))
    dtb_h = din("dt_bias", (1, H))
    bnb_h = din("B_nb", (1, N))
    cnb_h = din("C_nb", (1, N))
    wout_h = din("W_out", (DI, D))
    rw_h = din("router_w", (D, NEXP))
    w1_h = din("w1", (NEXP, D, FF))
    b1_h = din("b1", (NEXP, FF))
    w2_h = din("w2", (NEXP, FF, D))
    b2_h = din("b2", (NEXP, D))
    cmask_h = din("coremask64", (1, 64))
    id_f_h = din("id_f", (128, 128))
    id_b_h = din("id_b", (128, 128), BF16)
    triu_h = din("triu_incl", (128, 128), BF16)
    triu32_h = din("triu32_strict", (32, 32), BF16)
    maskneg_h = din("maskneg4", (128, 128), BF16)
    masksc_h = din("maskscale", (128, 128))
    ones_b_h = din("ones_b", (1, 128), BF16)
    selrow_h = din("selrow32", (128, 1024), BF16)
    onepad_h = din("onepad", (128, 128), BF16)
    shift1_h = din("shift1", (128, 128), BF16)
    sel127_h = din("sel127", (128, 128), BF16)
    iota8_h = din("iota8", (128, 8))
    iotamb_h = din("iotamb", (128, 8))

    out_h = nc.dram_tensor("out", [T, D], F32, kind="ExternalOutput").ap()

    xn_d = nc.dram_tensor("xn_d", [T, D], BF16).ap()
    s_d = nc.dram_tensor("s_d", [T, D], F32).ap()
    eb_d = nc.dram_tensor("eb_d", [NEXP * CAP, D], BF16).ap()
    eo_d = nc.dram_tensor("eo_d", [NEXP * CAP, D], BF16).ap()
    ccin_d = nc.dram_tensor("ccin_d", [1, NEXP], F32).ap()
    ccout_d = nc.dram_tensor("ccout_d", [B, NEXP], F32).ap()
    tot_d = nc.dram_tensor("tot_d", [1, 8 * NCHUNK], BF16).ap()
    off_d = nc.dram_tensor("off_d", [NCHUNK, 8], F32).ap()

    ctx1 = ExitStack()

    def sb(name, shape, dt=F32):
        return ctx.enter_context(nc.sbuf_tensor("sb_" + name, list(shape), dt))

    def sb1(name, shape, dt=F32):
        return ctx1.enter_context(nc.sbuf_tensor("sb_" + name, list(shape), dt))

    def psum(name, shape, dt=F32):
        return ctx.enter_context(nc.psum_tensor(name, list(shape), dt))

    xc = [sb("xc0", (128, D)), sb("xc1", (128, D))]
    id_f = sb("id_f", (128, 128))
    id_b = sb("id_b", (128, 128), BF16)
    triu32 = sb("triu32", (32, 32), BF16)
    ones_b = sb("ones_b", (1, 128), BF16)
    iota8 = sb("iota8", (128, 8))
    ones_c = sb("ones_c", (128, 1), BF16)
    cb_ln = sb("cb_ln", (128, 3))
    iotamb = sb("iotamb", (128, 8))
    cmask = sb("cmask", (1, 64))
    SCNT = sb("SCNT", (128, 8 * NCHUNK), BF16)
    IDX = sb("IDX", (128, NCHUNK))
    GATE = sb("GATE", (128, NCHUNK))
    emk = sb("emk", (128, 8), BF16)
    TOT = sb("TOT", (32, 8), BF16)
    OFFs = sb("OFFs", (32, 8))
    OFFrow = sb("OFFrow", (1, 256))
    OFFrh = sb("OFFrh", (1, 256), BF16)
    OFFrl = sb("OFFrl", (1, 256), BF16)
    ctot = sb("ctot", (1, 8))
    gtrow = sb("gtrow", (1, 64))
    myoff = sb("myoff", (1, 8))
    myoh = sb("myoh", (1, 8), BF16)
    myol = sb("myol", (1, 8), BF16)
    MYOFF = sb("MYOFF", (128, 8))
    zrow = sb("zrow", (1, 256), BF16)
    onepad = sb("onepad", (128, 128), BF16)
    WIN = sb1("WIN", (128, 8 * DP), BF16)
    WOUT = sb1("WOUT", (128, 16 * D), BF16)
    rw_f = sb1("rw_f", (128, 64))
    rwhl = sb1("rwhl", (128, 128), BF16)
    selrow = sb1("selrow", (128, 1024), BF16)
    shift1 = sb1("shift1", (128, 128), BF16)
    sel127 = sb1("sel127", (128, 128), BF16)
    triu = sb1("triu", (128, 128), BF16)
    maskneg = sb1("maskneg", (128, 128), BF16)
    masksc = sb1("masksc", (128, 128))
    abc = sb1("abc", (128, 32))
    dtbb = sb1("dtbb", (128, 32))
    bnb = sb1("bnb", (128, 64))
    cnb = sb1("cnb", (128, 64))
    rowtmp = sb1("rowtmp", (1, 64))
    rowh = sb1("rowh", (1, 64), BF16)
    rowl = sb1("rowl", (1, 64), BF16)
    xnT = sb1("xnT", (128, D), BF16)
    sz = sb1("szb", (128, DI), BF16)
    xh = sb1("xh", (128, 4096), BF16)
    xhp = sb1("xhp", (128, 4096), BF16)
    bcr = sb1("bcr", (128, 192), BF16)
    brt = bcr[:, 0:128]
    crt = bcr[:, 128:192]
    bbt = sb1("bbt", (128, 128), BF16)
    CrTz = [sb1("CrTz0", (128, 128), BF16), sb1("CrTz1", (128, 128), BF16)]
    lgbuf = sb1("lgbuf", (128, 24))
    lg8 = sb1("LG", (128, 8))
    cum = sb1("cum", (128, 64))
    cumg = [sb1("cumg0", (8, 128)), sb1("cumg1", (8, 128))]
    cumgh = [sb1("cumgh0", (128, 128), BF16), sb1("cumgh1", (128, 128), BF16)]
    cumgl = [sb1("cumgl0", (128, 128), BF16), sb1("cumgl1", (128, 128), BF16)]
    clast = sb1("clast", (1, 64))
    angchl = sb1("angchl", (128, 64), BF16)
    clhl = sb1("clhl", (128, 64), BF16)
    cdrow = sb1("cdrow", (1, 32))
    cdhl = sb1("cdhl", (128, 64), BF16)
    CD = sb1("CD", (64, 32))
    hilo = sb1("hilo", (128, 128), BF16)
    nchl = sb1("nchl", (128, 64), BF16)
    small = sb1("small", (128, 576))
    smcol = sb1("smcol", (128, 16))
    rms3 = sb1("rms3", (128, 4))
    scb = sb1("scb", (128, 64))
    sinb = scb[:, 0:32]
    cosb = scb[:, 32:64]
    ktmp = sb1("ktmp", (128, 64), I32)
    sgT = sb1("sgT", (128, 512), BF16)
    dgT = [sb1("dgT0", (128, 1024), BF16), sb1("dgT1", (128, 1024), BF16)]
    Wg = sb1("Wg", (128, 1024), BF16)
    xga = sb1("xga", (128, 1024), BF16)
    xba = sb1("xba", (128, 1024), BF16)
    ysc = sb1("ysc", (128, 512))
    YG = sb1("YG", (128, DI), BF16)
    Sst = sb1("Sst", (64, DI))
    sbfg = sb1("Sbf", (128, 512), BF16)
    BrT = [sb1("BrT0", (128, 132), BF16), sb1("BrT1", (128, 132), BF16)]
    SM = dict(dt=0, logdA=32, alpha=64, lam=96, lam1m=128, scal_g=160,
              scal_b=192, scal_gd=224, scal_bd=256, dec_in=288, dec_end=320,
              tdec=352, ang=384, red=416, kf=448, u2=480, rt1=512, rt2=544)
    s_sb = sz[:, :].bitcast(F32)
    xnTl = dgT[0][:, :]
    Wb = xnT[:, :]
    ysum = Wg[:, :].bitcast(F32)
    stmp = xga[0:64, :].bitcast(F32)
    sqb = YG[:, 0:384].bitcast(F32)
    Bn = YG[:, 384:640].bitcast(F32)
    Cn = YG[:, 640:768].bitcast(F32)
    praw = YG[:, 768:1344].bitcast(F32)
    xn = YG[:, :].bitcast(F32)
    ynT = xga[:, :]
    pA = [psum("pA0", (128, 512)), psum("pA1", (128, 512))]
    pT = psum("pT", (128, 1536))
    pTB = psum("pTB", (128, 1024), BF16)
    pY = psum("pY", (128, 512))
    pST = psum("pST", (64, 512))
    MB = 1024

    V, A, G_, TE = "vector", "scalar", "gpsimd", "tensor"
    dbg = []

    def tap(name, ap, reads):
        if name in DEBUG_TAPS:
            t = nc.dram_tensor("dbg_" + name, list(ap.shape),
                               ap.dtype, kind="ExternalOutput").ap()
            S.dma("sync", t, ap, reads=list(reads))
            dbg.append(name)

    for nm, dst, src in (("id_f", id_f, id_f_h), ("id_b", id_b, id_b_h),
                         ("triu", triu, triu_h), ("triu32", triu32, triu32_h),
                         ("maskneg", maskneg, maskneg_h),
                         ("masksc", masksc, masksc_h),
                         ("ones_b", ones_b, ones_b_h),
                         ("iota8", iota8, iota8_h),
                         ("iotamb", iotamb, iotamb_h),
                         ("selrow", selrow, selrow_h),
                         ("shift1", shift1, shift1_h),
                         ("sel127", sel127, sel127_h),
                         ("onepad", onepad, onepad_h),
                         ("cmask", cmask, cmask_h)):
        S.dma("sync", dst[:, :], src, writes=[nm])
    S.dma("sync", rw_f[:, :].rearrange("p (k e) -> p k e", k=8),
          rw_h.rearrange("(k p) e -> p k e", p=128), writes=["rw_f"])
    win_r = win_h.rearrange("(k p) f -> p k f", p=128)
    for k in range(8):
        S.dma("gpsimd", WIN[:, k * DP:(k + 1) * DP],
              win_r[:, k, :], writes=[f"WIN{k}"])
    wout_r = wout_h.rearrange("(k p) d -> p k d", p=128)
    for j in range(4):
        S.dma("gpsimd",
              WOUT[:, j * 4096:(j + 1) * 4096].rearrange(
                  "p (a d) -> p a d", a=4),
              wout_r[:, j * 4:(j + 1) * 4, :], writes=[f"WOUT{j}"])
    rwhl_v = rwhl[:, :].rearrange("p (k t e) -> p k t e", k=8, t=2)
    rwf_v = rw_f[:, :].rearrange("p (k e) -> p k e", k=8)
    S.op(V, lambda e: e.tensor_copy(rwhl_v[:, :, 0, :], rwf_v),
         reads=["rw_f"], writes=["rwhl"])
    S.op(V, lambda e: e.tensor_tensor(rwhl_v[:, :, 1, :], rwf_v,
                                      rwhl_v[:, :, 0, :], ALU.subtract),
         reads=["rw_f", "rwhl"], writes=["rwhl"])

    def bcast_row(src_h, width, dst, dreg, negexp=False):
        S.dma("sync", rowtmp[:, 0:width], src_h, writes=["rowtmp"])
        if negexp:
            S.op(A, lambda e: e.activation(rowtmp[:, 0:width],
                                           rowtmp[:, 0:width], AF.Exp),
                 reads=["rowtmp"], writes=["rowtmp"])
            S.op(V, lambda e: e.tensor_scalar(rowtmp[:, 0:width],
                                              rowtmp[:, 0:width], -1.0, None,
                                              ALU.mult),
                 reads=["rowtmp"], writes=["rowtmp"])
        S.op(V, lambda e: e.tensor_copy(rowh[:, 0:width], rowtmp[:, 0:width]),
             reads=["rowtmp"], writes=["rowh"])
        S.op(V, lambda e: e.tensor_tensor(rowl[:, 0:width], rowtmp[:, 0:width],
                                          rowh[:, 0:width], ALU.subtract),
             reads=["rowtmp", "rowh"], writes=["rowl"])
        S.op(TE, lambda e: e.matmul(pT[:, MB:MB + width], ones_b[:, 0:128],
                                    rowh[:, 0:width], start=True, stop=False),
             reads=["rowh", "ones_b"], writes=["pT_M"])
        S.op(TE, lambda e: e.matmul(pT[:, MB:MB + width], ones_b[:, 0:128],
                                    rowl[:, 0:width], start=False, stop=True),
             reads=["rowl", "ones_b"], writes=["pT_M"])
        S.op(V, lambda e: e.tensor_copy(dst[:, 0:width], pT[:, MB:MB + width]),
             reads=["pT_M"], writes=[dreg])

    bcast_row(alog_h, 32, abc, "abc", negexp=True)
    bcast_row(dtb_h, 32, dtbb, "dtbb")
    bcast_row(bnb_h, 64, bnb, "bnb")
    bcast_row(cnb_h, 64, cnb, "cnb")

    S.op("vector", lambda e: e.memset(ones_c[:, :], 1.0), writes=["ones_c"])
    S.op("vector", lambda e: e.memset(cb_ln[:, 0:1], 1e-5), writes=["cb_ln"])
    S.op("vector", lambda e: e.memset(cb_ln[:, 1:2], 1.0), writes=["cb_ln"])
    S.op("vector", lambda e: e.memset(cb_ln[:, 2:3], EPS), writes=["cb_ln"])
    for t_, r_ in ((Sst, "Sst"), (BrT[0], "BrT0"), (zrow, "zrow"),
                   (xh, "xh"), (bbt, "bbt"),
                   (angchl, "angchl"), (clhl, "clhl"), (cdhl, "cdhl"),
                   (cumgh[0], "cumgh0"), (cumgh[1], "cumgh1"),
                   (cumgl[0], "cumgl0"), (cumgl[1], "cumgl1"),
                   (CrTz[0], "CrTz0"), (CrTz[1], "CrTz1"),
                   (sbfg, "sbfg")):
        S.op(V, lambda e, t_=t_: e.memset(t_[:, :], 0.0), writes=[r_])
    for zq in range(4):
        S.dma("gpsimd", eb_d[0:1, zq * 256:(zq + 1) * 256], zrow[:, :],
              reads=["zrow"], writes=["ebz0"])
    ebz_keys = ["ebz0"]

    S.dma("sync", xc[0][:, :], x_h[0:128, :], writes=["xc0"])

    def sm(key, w=32):
        return small[:, SM[key]:SM[key] + w]

    def emit_ln(cc):
        xcb = xc[cc % 2]
        xreg = f"xc{cc % 2}"
        S.op(V, lambda e, xcb=xcb: e.reduce_sum(smcol[:, 0:1], xcb[:, :],
                                                axis=AX.X),
             reads=[xreg], writes=["smc0"])
        S.op(A, lambda e, xcb=xcb: e.activation(xga[:, :], xcb[:, :],
                                                AF.Square,
                                                accum_out=smcol[:, 1:2]),
             reads=[xreg], writes=["smc1", "xga"])
        S.op(G_, lambda e: e.tensor_scalar(smcol[:, 2:3], smcol[:, 0:1],
                                           1.0 / D, None, ALU.mult),
             reads=["smc0"], writes=["smc2"])
        S.op(G_, lambda e: e.tensor_scalar(smcol[:, 3:4], smcol[:, 1:2],
                                           1.0 / D, None, ALU.mult),
             reads=["smc1"], writes=["smc3"])
        S.op(G_, lambda e: e.tensor_tensor(smcol[:, 4:5], smcol[:, 2:3],
                                           smcol[:, 2:3], ALU.mult),
             reads=["smc2"], writes=["smc4"])
        S.op(G_, lambda e: e.tensor_tensor(smcol[:, 4:5], smcol[:, 3:4],
                                           smcol[:, 4:5], ALU.subtract),
             reads=["smc3", "smc4"], writes=["smc4"])
        S.op(A, lambda e: e.activation(smcol[:, 5:6], smcol[:, 4:5], AF.Ln,
                                       bias=cb_ln[:, 0:1]),
             reads=["smc4", "cb_ln"], writes=["smc5"])
        S.op(A, lambda e: e.activation(smcol[:, 5:6], smcol[:, 5:6], AF.Exp,
                                       scale=-0.5),
             reads=["smc5"], writes=["smc5"])
        S.op(G_, lambda e: e.tensor_scalar(smcol[:, 6:7], smcol[:, 2:3],
                                           -1.0, None, ALU.mult),
             reads=["smc2"], writes=["smc6"])
        S.op(V, lambda e, xcb=xcb: e.tensor_scalar(xn[:, :], xcb[:, :],
                                                   smcol[:, 6:7],
                                                   smcol[:, 5:6],
                                                   ALU.add, ALU.mult),
             reads=[xreg, "smc6", "smc5"], writes=["YG"])
        tap(f"xn{cc}", xn[:, :], ["YG"])
        if DO_FFN:
            S.dma("gpsimd", xn_d[cc * 128:(cc + 1) * 128, :], xn[:, :],
                  reads=["YG"], writes=[f"xnd{cc}"])


    for c in range(NCHUNK):
        par = c % 2
        xcb = xc[par]
        xreg = f"xc{par}"
        bpar = c % 2
        if c + 1 < NCHUNK:
            S.dma("sync", xc[1 - par][:, :],
                  x_h[(c + 1) * 128:(c + 2) * 128, :], writes=[f"xc{1 - par}"])
        if c == 0:
            emit_ln(0)

        for k in range(8):
            S.op(TE, lambda e, k=k: e.transpose(pT[:, k * 128:(k + 1) * 128],
                                                xn[:, k * 128:(k + 1) * 128],
                                                id_f[:, :]),
                 reads=["YG", "id_f"], writes=["pT_A"])
        S.op(A, lambda e: e.copy(xnT[:, :], pT[:, 0:1024]),
             reads=["pT_A"], writes=["xnT"])
        S.op(V, lambda e: e.tensor_tensor(xnTl[:, :], pT[:, 0:1024],
                                          xnT[:, :], ALU.subtract),
             reads=["pT_A", "xnT"], writes=["dgT0"])

        def emit_router(c=c):
            for k in range(8):
                S.op(TE, lambda e, k=k: e.matmul(
                    pT[:, MB:MB + 16], xnT[:, k * 128:(k + 1) * 128],
                    rwhl[:, k * 16:(k + 1) * 16], start=(k == 0), stop=False),
                    reads=["xnT", "rwhl"], writes=["pT_M"])
            for k in range(8):
                S.op(TE, lambda e, k=k: e.matmul(
                    pT[:, MB + 16:MB + 24], xnTl[:, k * 128:(k + 1) * 128],
                    rwhl[:, k * 16:k * 16 + 8], start=(k == 0), stop=(k == 7)),
                    reads=["dgT0", "rwhl"], writes=["pT_M"])
            S.op(A, lambda e: e.copy(lgbuf[:, :], pT[:, MB:MB + 24]),
                 reads=["pT_M"], writes=["lgbuf"])
            lgc = lg8[:, :]
            S.op(V, lambda e: e.tensor_tensor(lgbuf[:, 0:8], lgbuf[:, 0:8],
                                              lgbuf[:, 8:16], ALU.add),
                 reads=["lgbuf"], writes=["lgbuf"])
            S.op(V, lambda e, lgc=lgc: e.tensor_tensor(lgc, lgbuf[:, 0:8],
                                                       lgbuf[:, 16:24], ALU.add),
                 reads=["lgbuf"], writes=["LG"])
            tap(f"logits{c}", lgc, ["LG"])
            S.op(V, lambda e, lgc=lgc: e.reduce_max(smcol[:, 7:8], lgc, axis=AX.X),
                 reads=["LG"], writes=["smc7"])
            S.op(V, lambda e: e.tensor_scalar(smcol[:, 8:9], smcol[:, 7:8],
                                              -1.0, None, ALU.mult),
                 reads=["smc7"], writes=["smc8"])
            S.op(A, lambda e, lgc=lgc: e.activation(lgbuf[:, 0:8], lgc, AF.Exp,
                                                    bias=smcol[:, 8:9]),
                 reads=["LG", "smc8"], writes=["lgbuf"])
            S.op(V, lambda e: e.reduce_sum(smcol[:, 9:10], lgbuf[:, 0:8],
                                           axis=AX.X),
                 reads=["lgbuf"], writes=["smc9"])
            S.op(V, lambda e, c=c: e.reciprocal(GATE[:, c:c + 1], smcol[:, 9:10]),
                 reads=["smc9"], writes=["GATE"])
            S.op(V, lambda e, lgc=lgc: e.tensor_scalar(lgbuf[:, 8:16], lgc,
                                                       smcol[:, 7:8], None,
                                                       ALU.is_equal),
                 reads=["LG", "smc7"], writes=["lgbuf"])
            S.op(V, lambda e: e.scalar_tensor_tensor(lgbuf[:, 16:24],
                                                     lgbuf[:, 8:16], 1.0,
                                                     iotamb[:, :], ALU.mult,
                                                     ALU.mult),
                 reads=["lgbuf", "iotamb"], writes=["lgbuf"])
            S.op(V, lambda e: e.tensor_reduce(smcol[:, 10:11], lgbuf[:, 16:24],
                                              axis=AX.X, op=ALU.min),
                 reads=["lgbuf"], writes=["smc10"])
            S.op(V, lambda e, c=c: e.tensor_scalar(IDX[:, c:c + 1],
                                                   smcol[:, 10:11], BIG, None,
                                                   ALU.add),
                 reads=["smc10"], writes=["IDX"])
            S.op(V, lambda e, c=c: e.tensor_scalar(emk[:, :], iota8[:, :],
                                                   IDX[:, c:c + 1], None,
                                                   ALU.is_equal),
                 reads=["IDX", "iota8"], writes=["emk"])
            S.op(TE, lambda e: e.matmul(pT[:, MB:MB + 8], triu[:, :], emk[:, :],
                                        start=True, stop=True),
                 reads=["emk", "triu"], writes=["pT_M"])
            S.op(A, lambda e, c=c: e.copy(SCNT[:, c * 8:(c + 1) * 8],
                                          pT[:, MB:MB + 8]),
                 reads=["pT_M"], writes=["SCNT"])
            if DO_FFN and c == NCHUNK - 1:
                S.dma("sync", tot_d, SCNT[127:128, :], reads=["SCNT"],
                      writes=["tot_d"])
                S.dma("sync", TOT[:, :],
                      tot_d.rearrange("o (cc ee) -> (o cc) ee", ee=8),
                      reads=["tot_d"], writes=["TOT"])
                S.op(TE, lambda e: e.matmul(pT[0:32, MB + 256:MB + 264],
                                            triu32[:, :], TOT[:, :],
                                            start=True, stop=True),
                     reads=["TOT", "triu32"], writes=["pT_M"])
                S.op(V, lambda e: e.tensor_copy(OFFs[:, :],
                                                pT[0:32, MB + 256:MB + 264]),
                     reads=["pT_M"], writes=["OFFs"])
                S.op(TE, lambda e: e.matmul(pT[0:1, MB + 264:MB + 272],
                                            ones_c[0:32, 0:1], TOT[:, :],
                                            start=True, stop=True),
                     reads=["TOT", "ones_c"], writes=["pT_M"])
                S.op(V, lambda e: e.tensor_copy(ctot[:, :],
                                                pT[0:1, MB + 264:MB + 272]),
                     reads=["pT_M"], writes=["ctot"])
                S.dma("gpsimd", ccin_d, ctot[:, :], reads=["ctot"],
                      writes=["ccin"])

                def ccfn(e):
                    return e.collective_compute(
                        "AllGather", ALU.bypass,
                        replica_groups=[list(range(B))],
                        ins=[ccin_d.opt()], outs=[ccout_d.opt()])

                S.collective(ccfn, reads=["ccin"], writes=["ccout"])
                S.dma("sync", off_d, OFFs[:, :], reads=["OFFs"],
                      writes=["off_d"])
                S.dma("sync", OFFrow[:, :], off_d.flatten().unsqueeze(0),
                      reads=["off_d"], writes=["OFFrow"])

        def projmm(fstart, fwidth, pa, preg):
            for k in range(8):
                S.op(TE, lambda e, k=k, fstart=fstart, fwidth=fwidth, pa=pa:
                     e.matmul(pa[:, 0:fwidth],
                              xnT[:, k * 128:(k + 1) * 128],
                              WIN[:, k * DP + fstart:k * DP + fstart + fwidth],
                              start=(k == 0), stop=(k == 7)),
                     reads=["xnT", f"WIN{k}"], writes=[preg])

        projmm(6144, 288, pA[0], "pA0")
        S.op(A, lambda e: e.copy(praw[:, :], pA[0][:, 0:288]),
             reads=["pA0"], writes=["YG"])
        tap(f"praw{c}", praw[:, :], ["YG"])

        S.op(G_, lambda e: e.tensor_tensor(sm("dt"), praw[:, 192:224],
                                          dtbb[:, 0:32], ALU.add),
             reads=["YG", "dtbb"], writes=["sm_dt"])
        S.op(A, lambda e: e.activation(sm("red"), sm("dt"), AF.Abs),
             reads=["sm_dt"], writes=["sm_red"])
        S.op(A, lambda e: e.activation(sm("red"), sm("red"), AF.Exp,
                                       scale=-1.0),
             reads=["sm_red"], writes=["sm_red"])
        S.op(A, lambda e: e.activation(sm("red"), sm("red"), AF.Ln,
                                       bias=cb_ln[:, 1:2]),
             reads=["sm_red", "cb_ln"], writes=["sm_red"])
        S.op(G_, lambda e: e.tensor_scalar(sm("u2"), sm("dt"), 0.0, None,
                                          ALU.max),
             reads=["sm_dt"], writes=["sm_u2"])
        S.op(G_, lambda e: e.tensor_tensor(sm("dt"), sm("red"), sm("u2"),
                                          ALU.add),
             reads=["sm_red", "sm_u2"], writes=["sm_dt"])
        S.op(G_, lambda e: e.tensor_scalar(sm("dt"), sm("dt"), 1e-4, 0.5,
                                          ALU.max, ALU.min),
             reads=["sm_dt"], writes=["sm_dt"])
        tap(f"dt{c}", sm("dt"), ["sm_dt"])
        S.op(A, lambda e: e.activation(sm("lam1m"), praw[:, 224:256], AF.Exp,
                                       scale=-1.0),
             reads=["YG"], writes=["sm_lam1m"])
        S.op(G_, lambda e: e.tensor_scalar(sm("u2"), sm("lam1m"), 1.0, None,
                                          ALU.add),
             reads=["sm_lam1m"], writes=["sm_u2"])
        S.op(V, lambda e: e.reciprocal(sm("lam"), sm("u2")),
             reads=["sm_u2"], writes=["sm_lam"])
        S.op(G_, lambda e: e.tensor_tensor(sm("lam1m"), sm("lam"), sm("lam1m"),
                                          ALU.mult),
             reads=["sm_lam", "sm_lam1m"], writes=["sm_lam1m"])
        S.op(G_, lambda e: e.tensor_tensor(sm("logdA"), sm("dt"), abc[:, 0:32],
                                          ALU.mult),
             reads=["sm_dt", "abc"], writes=["sm_logdA"])
        S.op(A, lambda e: e.activation(sm("alpha"), sm("logdA"), AF.Exp),
             reads=["sm_logdA"], writes=["sm_alpha"])
        S.op(G_, lambda e: e.tensor_tensor(sm("scal_g"), sm("lam"), sm("dt"),
                                          ALU.mult),
             reads=["sm_lam", "sm_dt"], writes=["sm_scal_g"])
        S.op(G_, lambda e: e.tensor_tensor(sm("scal_b"), sm("lam1m"), sm("dt"),
                                          ALU.mult),
             reads=["sm_lam1m", "sm_dt"], writes=["sm_scal_b"])
        S.op(G_, lambda e: e.tensor_tensor(sm("scal_b"), sm("scal_b"),
                                          sm("alpha"), ALU.mult),
             reads=["sm_scal_b", "sm_alpha"], writes=["sm_scal_b"])
        S.op(V, lambda e: e.reduce_sum(smcol[:, 11:12], sm("dt"), axis=AX.X),
             reads=["sm_dt"], writes=["smc11"])
        S.op(G_, lambda e: e.tensor_scalar(smcol[:, 11:12], smcol[:, 11:12],
                                          1.0 / 32, None, ALU.mult),
             reads=["smc11"], writes=["smc11"])
        S.op(G_, lambda e: e.tensor_scalar(sm("ang"), praw[:, 256:288],
                                          smcol[:, 11:12], None, ALU.mult),
             reads=["YG", "smc11"], writes=["sm_ang"])
        S.op(G_, lambda e: e.tensor_copy(hilo[:, 0:32], sm("logdA")),
             reads=["sm_logdA"], writes=["hilo"])
        S.op(G_, lambda e: e.tensor_copy(hilo[:, 32:64], sm("ang")),
             reads=["sm_ang"], writes=["hilo"])
        S.op(G_, lambda e: e.tensor_tensor(hilo[:, 64:96], sm("logdA"),
                                          hilo[:, 0:32], ALU.subtract),
             reads=["sm_logdA", "hilo"], writes=["hilo"])
        S.op(G_, lambda e: e.tensor_tensor(hilo[:, 96:128], sm("ang"),
                                          hilo[:, 32:64], ALU.subtract),
             reads=["sm_ang", "hilo"], writes=["hilo"])
        S.op(A, lambda e: e.activation(sqb[:, :], praw[:, 0:192], AF.Square),
             reads=["YG"], writes=["YG"])
        for i in range(3):
            S.op(V, lambda e, i=i: e.reduce_sum(rms3[:, i:i + 1],
                                                sqb[:, i * 64:(i + 1) * 64],
                                                axis=AX.X),
                 reads=["YG"], writes=["rms3"])
        S.op(A, lambda e: e.activation(rms3[:, 0:3], rms3[:, 0:3], AF.Ln,
                                       scale=1.0 / 64, bias=cb_ln[:, 2:3]),
             reads=["rms3", "cb_ln"], writes=["rms3"])
        S.op(A, lambda e: e.activation(rms3[:, 0:3], rms3[:, 0:3], AF.Exp,
                                       scale=-0.5),
             reads=["rms3"], writes=["rms3"])
        for r in range(2):
            S.op(V, lambda e, r=r: e.scalar_tensor_tensor(
                Bn[:, r * 64:(r + 1) * 64], praw[:, r * 64:(r + 1) * 64],
                rms3[:, r:r + 1], bnb[:, 0:64], ALU.mult, ALU.mult),
                reads=["YG", "rms3", "bnb"], writes=["YG"])
        S.op(V, lambda e: e.scalar_tensor_tensor(
            Cn[:, :], praw[:, 128:192], rms3[:, 2:3], cnb[:, 0:64],
            ALU.mult, ALU.mult),
            reads=["YG", "rms3", "cnb"], writes=["YG"])

        for gz in range(4):
            pa, preg = pA[gz % 2], f"pA{gz % 2}"
            projmm(gz * 512, 512, pa, preg)
            S.op(A, lambda e, gz=gz, pa=pa: e.activation(
                sz[:, gz * 512:(gz + 1) * 512], pa[:, :], AF.Silu),
                reads=[preg], writes=["sz"])
        def emit_cum(c=c):
            S.op(TE, lambda e: e.matmul(pT[:, MB:MB + 64], triu[:, :],
                                        hilo[:, 0:64], start=True, stop=False),
                 reads=["hilo", "triu"], writes=["pT_M"])
            S.op(TE, lambda e: e.matmul(pT[:, MB:MB + 64], triu[:, :],
                                        hilo[:, 64:128], start=False, stop=False),
                 reads=["hilo", "triu"], writes=["pT_M"])
            S.op(TE, lambda e: e.matmul(pT[:, MB + 32:MB + 64], onepad[:, :],
                                        angchl[:, 0:32], start=False, stop=False),
                 reads=["angchl", "onepad"], writes=["pT_M"])
            S.op(TE, lambda e: e.matmul(pT[:, MB + 32:MB + 64], onepad[:, :],
                                        angchl[:, 32:64], start=False, stop=True),
                 reads=["angchl", "onepad"], writes=["pT_M"])
            S.op(A, lambda e: e.copy(cum[:, :], pT[:, MB:MB + 64]),
                 reads=["pT_M"], writes=["cum"])
            tap(f"cum{c}", cum[:, :], ["cum"])
            S.op(TE, lambda e: e.matmul(pT[0:1, MB + 64:MB + 128],
                                        id_f[:, 127:128], cum[:, :],
                                        start=True, stop=True),
                 reads=["cum", "id_f"], writes=["pT_M"])
            S.op(A, lambda e: e.copy(clast[:, :], pT[0:1, MB + 64:MB + 128]),
                 reads=["pT_M"], writes=["clast"])
            S.op(V, lambda e: e.tensor_scalar(sm("lam"), cum[:, 0:32], -1.0, None,
                                              ALU.mult),
                 reads=["cum"], writes=["sm_lam"])
            S.op(G_, lambda e: e.tensor_copy(nchl[:, 0:32], sm("lam")),
                 reads=["sm_lam"], writes=["nchl"])
            S.op(G_, lambda e: e.tensor_tensor(nchl[:, 32:64], sm("lam"),
                                               nchl[:, 0:32], ALU.subtract),
                 reads=["sm_lam", "nchl"], writes=["nchl"])
            S.op(G_, lambda e: e.tensor_copy(angchl[0:1, 0:32], clast[:, 32:64]),
                 reads=["clast"], writes=["angchl"])
            S.op(G_, lambda e: e.tensor_tensor(angchl[0:1, 32:64],
                                               clast[:, 32:64],
                                               angchl[0:1, 0:32], ALU.subtract),
                 reads=["clast", "angchl"], writes=["angchl"])
            S.op(G_, lambda e: e.tensor_copy(clhl[0:1, 0:32], clast[:, 0:32]),
                 reads=["clast"], writes=["clhl"])
            S.op(G_, lambda e: e.tensor_tensor(clhl[0:1, 32:64], clast[:, 0:32],
                                               clhl[0:1, 0:32], ALU.subtract),
                 reads=["clast", "clhl"], writes=["clhl"])
        emit_cum()
        for gx in range(8):
            pa, preg = pA[gx % 2], f"pA{gx % 2}"
            po = (gx % 2) * 512
            S.op(TE, lambda e, gx=gx, po=po: e.matmul(
                pT[:, po:po + 512], sel127[:, :],
                xh[:, gx * 512:(gx + 1) * 512], start=True, stop=False),
                reads=["xh", "sel127"], writes=["pT_A"])
            projmm(2048 + gx * 512, 512, pa, preg)
            S.op(A, lambda e, gx=gx, pa=pa: e.activation(
                xh[:, gx * 512:(gx + 1) * 512], pa[:, :], AF.Silu),
                reads=[preg], writes=["xh"])
            S.op(TE, lambda e, gx=gx, po=po: e.matmul(
                pT[:, po:po + 512], shift1[:, :],
                xh[:, gx * 512:(gx + 1) * 512], start=False, stop=True),
                reads=["xh", "shift1"], writes=["pT_A"])
            S.op(A, lambda e, gx=gx, po=po: e.copy(
                xhp[:, gx * 512:(gx + 1) * 512],
                pT[:, po:po + 512]),
                reads=["pT_A"], writes=["xhp"])
        tap(f"xh{c}", xh[:, :], ["xh"])
        emit_router()
        zsl = small[:, SM["red"]:SM["red"] + 64]
        t64 = small[:, SM["tdec"]:SM["tdec"] + 64]
        kf64 = small[:, SM["rt1"]:SM["rt1"] + 64]
        S.op(G_, lambda e: e.tensor_copy(zsl[:, 0:32], cum[:, 32:64]),
             reads=["cum"], writes=["sm_red", "sm_kf"])
        S.op(G_, lambda e: e.tensor_scalar(zsl[:, 32:64], cum[:, 32:64],
                                           HALF_PI, None, ALU.add),
             reads=["cum"], writes=["sm_red", "sm_kf"])
        S.op(G_, lambda e: e.tensor_scalar(t64, zsl, 1.0 / TWO_PI, None,
                                           ALU.mult),
             reads=["sm_red", "sm_kf"], writes=["sm_tdec", "sm_ang"])
        S.op(V, lambda e: e.tensor_copy(ktmp[:, :], t64),
             reads=["sm_tdec", "sm_ang"], writes=["ktmp"])
        S.op(V, lambda e: e.tensor_copy(kf64, ktmp[:, :]),
             reads=["ktmp"], writes=["sm_rt1", "sm_rt2"])
        S.op(V, lambda e: e.scalar_tensor_tensor(zsl, kf64, -TWO_PI,
                                                 zsl, ALU.mult, ALU.add),
             reads=["sm_rt1", "sm_rt2", "sm_red", "sm_kf"],
             writes=["sm_red", "sm_kf"])
        S.op(G_, lambda e: e.tensor_scalar(zsl, zsl, PI, -PI,
                                           ALU.min, ALU.max),
             reads=["sm_red", "sm_kf"], writes=["sm_red", "sm_kf"])
        S.op(A, lambda e: e.activation(scb[:, :], zsl, AF.Sin),
             reads=["sm_red", "sm_kf"], writes=["sinb", "cosb"])

        vsrc = YG[:, 384:768].bitcast(F32).rearrange(
            "p (g tw h) -> p g tw h", tw=2, h=32)
        bcrv = bcr[:, :].rearrange("p (g tw h) -> p g tw h", tw=2, h=32)
        cos_e = cosb.unsqueeze(1).broadcast_to((128, 3, 32))
        sin_e = sinb.unsqueeze(1).broadcast_to((128, 3, 32))
        rr1 = ysc[:, 0:96].rearrange("p (g h) -> p g h", h=32)
        rr2 = ysc[:, 96:192].rearrange("p (g h) -> p g h", h=32)
        S.op(V, lambda e: e.tensor_tensor(rr1, vsrc[:, :, 0, :], cos_e,
                                          ALU.mult),
             reads=["YG", "cosb"], writes=["ysc"])
        S.op(V, lambda e: e.tensor_tensor(rr2, vsrc[:, :, 1, :], sin_e,
                                          ALU.mult),
             reads=["YG", "sinb"], writes=["ysc"])
        S.op(V, lambda e: e.tensor_tensor(bcrv[:, :, 0, :], rr1, rr2,
                                          ALU.add),
             reads=["ysc"], writes=["brt", "crt"])
        S.op(V, lambda e: e.tensor_tensor(rr1, vsrc[:, :, 1, :], cos_e,
                                          ALU.mult),
             reads=["YG", "cosb"], writes=["ysc"])
        S.op(V, lambda e: e.tensor_tensor(rr2, vsrc[:, :, 0, :], sin_e,
                                          ALU.mult),
             reads=["YG", "sinb"], writes=["ysc"])
        S.op(V, lambda e: e.tensor_tensor(bcrv[:, :, 1, :], rr1, rr2,
                                          ALU.subtract),
             reads=["ysc"], writes=["brt", "crt"])
        tap(f"Br{c}", brt[:, :], ["brt"])
        tap(f"Cr{c}", crt[:, :], ["crt"])

        S.dma("gpsimd", bbt[1:128, :], brt[0:127, :], reads=["brt"],
              writes=["bbt"])

        S.op(TE, lambda e: e.transpose(pTB[:, 0:128], brt[:, :], id_b[:, :]),
             reads=["brt", "id_b"], writes=["pTB"])
        S.op(A, lambda e, bpar=bpar: e.copy(BrT[bpar][:, 1:129],
                                            pTB[:, 0:128]),
             reads=["pTB"], writes=[f"BrT{bpar}"])
        if c > 0:
            S.op(V, lambda e, bpar=bpar: e.tensor_copy(
                BrT[bpar][:, 0:1], BrT[1 - bpar][:, 128:129]),
                reads=[f"BrT{1 - bpar}"], writes=[f"BrT{bpar}"])
        S.op(TE, lambda e: e.transpose(pTB[0:64, 128:256], crt[:, :], id_b[:, :]),
             reads=["crt", "id_b"], writes=["pTB"])
        S.op(TE, lambda e: e.transpose(pTB[64:128, 128:256], crt[:, :],
                                       id_b[:, :]),
             reads=["crt", "id_b"], writes=["pTB"])
        S.op(A, lambda e: e.copy(CrTz[0][0:64, :], pTB[0:64, 128:256]),
             reads=["pTB"], writes=["CrTz0"])
        S.op(A, lambda e: e.copy(CrTz[1][64:128, :], pTB[64:128, 128:256]),
             reads=["pTB"], writes=["CrTz1"])

        for si in range(2):
            for r in range(2):
                co = (si * 2 + r) * 128
                lo = 1 - si
                S.op(TE, lambda e, r=r, lo=lo, co=co, bpar=bpar: e.matmul(
                    pT[:, co:co + 128],
                    BrT[bpar][:, lo:lo + 128],
                    CrTz[r][:, :], start=True, stop=True),
                    reads=[f"BrT{bpar}", f"CrTz{r}"], writes=["pT_A"])
                S.op(V, lambda e, co=co: e.tensor_tensor(
                    sgT[:, co:co + 128], pT[:, co:co + 128], masksc[:, :],
                    ALU.mult),
                    reads=["pT_A", "masksc"], writes=["sgT"])
        tap(f"sgT{c}", sgT[:, :], ["sgT"])
        tap(f"bbt{c}", bbt[:, :], ["bbt"])


        S.op(TE, lambda e: e.matmul(pT[:, MB + 64:MB + 96], onepad[:, :],
                                    clhl[:, 0:32], start=True, stop=False),
             reads=["clhl", "onepad"], writes=["pT_M"])
        S.op(TE, lambda e: e.matmul(pT[:, MB + 64:MB + 96], onepad[:, :],
                                    clhl[:, 32:64], start=False, stop=True),
             reads=["clhl", "onepad"], writes=["pT_M"])
        S.op(V, lambda e: e.tensor_tensor(sm("tdec"), pT[:, MB + 64:MB + 96],
                                          cum[:, 0:32], ALU.subtract),
             reads=["pT_M", "cum"], writes=["sm_tdec"])
        S.op(A, lambda e: e.activation(sm("dec_end"), sm("tdec"), AF.Exp),
             reads=["sm_tdec"], writes=["sm_dec_end"])
        S.op(A, lambda e: e.activation(sm("dec_in"), cum[:, 0:32], AF.Exp),
             reads=["cum"], writes=["sm_dec_in"])
        S.op(G_, lambda e: e.tensor_scalar(sm("dec_in"), sm("dec_in"), SCALE,
                                          None, ALU.mult),
             reads=["sm_dec_in"], writes=["sm_dec_in"])
        S.op(G_, lambda e: e.tensor_scalar(sm("tdec"), sm("dec_in"), -1.0,
                                          None, ALU.add),
             reads=["sm_dec_in"], writes=["sm_tdec"])
        tap(f"dec_in{c}", sm("dec_in"), ["sm_dec_in"])
        tap(f"dec_end{c}", sm("dec_end"), ["sm_dec_end"])
        S.op(G_, lambda e: e.tensor_tensor(sm("scal_gd"), sm("scal_g"),
                                          sm("dec_end"), ALU.mult),
             reads=["sm_scal_g", "sm_dec_end"], writes=["sm_scal_gd"])
        S.op(G_, lambda e: e.tensor_tensor(sm("scal_bd"), sm("scal_b"),
                                          sm("dec_end"), ALU.mult),
             reads=["sm_scal_b", "sm_dec_end"], writes=["sm_scal_bd"])
        S.op(A, lambda e: e.activation(cdrow[:, :], clast[:, 0:32], AF.Exp),
             reads=["clast"], writes=["cdrow"])
        S.op(G_, lambda e: e.tensor_copy(cdhl[0:1, 0:32], cdrow[:, :]),
             reads=["cdrow"], writes=["cdhl"])
        S.op(V, lambda e: e.tensor_tensor(cdhl[0:1, 32:64], cdrow[:, :],
                                          cdhl[0:1, 0:32], ALU.subtract),
             reads=["cdrow", "cdhl"], writes=["cdhl"])
        S.op(TE, lambda e: e.matmul(pT[:, MB + 96:MB + 128],
                                    onepad[:, :], cdhl[:, 0:32],
                                    start=True, stop=False),
             reads=["cdhl", "onepad"], writes=["pT_M"])
        S.op(TE, lambda e: e.matmul(pT[:, MB + 96:MB + 128],
                                    onepad[:, :], cdhl[:, 32:64],
                                    start=False, stop=True),
             reads=["cdhl", "onepad"], writes=["pT_M"])
        S.op(A, lambda e: e.copy(CD[:, :], pT[0:64, MB + 96:MB + 128]),
             reads=["pT_M"], writes=["CD"])
        def emit_bc(g, dpar):
            cg, rg = cumg[dpar], f"cumg{dpar}"
            cgh, rgh = cumgh[dpar], f"cumgh{dpar}"
            cgl, rgl = cumgl[dpar], f"cumgl{dpar}"
            S.op(TE, lambda e, g=g: e.transpose(
                pT[0:8, MB + 128:MB + 256], cum[:, g * 8:(g + 1) * 8],
                id_f[:, :]),
                reads=["cum", "id_f"], writes=["pT_M"])
            S.op(A, lambda e, cg=cg: e.copy(cg[:, :],
                                            pT[0:8, MB + 128:MB + 256]),
                 reads=["pT_M"], writes=[rg])
            S.op(G_, lambda e, cg=cg, cgh=cgh: e.tensor_copy(cgh[0:8, :],
                                                             cg[:, :]),
                 reads=[rg], writes=[rgh])
            S.op(G_, lambda e, cg=cg, cgh=cgh, cgl=cgl: e.tensor_tensor(
                cgl[0:8, :], cg[:, :], cgh[0:8, :], ALU.subtract),
                reads=[rg, rgh], writes=[rgl])
            mask_b = maskneg[:, :].unsqueeze(1).broadcast_to((128, 4, 128))
            for half in range(2):
                hsl = slice(half * 512, (half + 1) * 512)
                pvm = pT[:, hsl].rearrange("p (h l) -> p h l", h=4)
                S.op(TE, lambda e, pvm=pvm, mask_b=mask_b: e.matmul(
                    pvm, id_b[:, :], mask_b,
                    start=True, stop=False),
                    reads=["maskneg", "id_b"], writes=["pT_A"])
                for hh in range(4):
                    hr = half * 4 + hh
                    co = half * 512 + hh * 128
                    sel = selrow[:, hr * 128:(hr + 1) * 128]
                    S.op(TE, lambda e, sel=sel, co=co, cgh=cgh: e.matmul(
                        pT[:, co:co + 128], sel,
                        cgh[:, :], start=False, stop=False),
                        reads=[rgh, "selrow"], writes=["pT_A"])
                    S.op(TE, lambda e, sel=sel, co=co, cgl=cgl: e.matmul(
                        pT[:, co:co + 128], sel,
                        cgl[:, :], start=False, stop=False),
                        reads=[rgl, "selrow"], writes=["pT_A"])
                nce_h = nchl[:, g * 8 + half * 4:g * 8 + half * 4 + 4
                             ].unsqueeze(2).broadcast_to((128, 4, 128))
                nce_l = nchl[:, 32 + g * 8 + half * 4:32 + g * 8 + half * 4
                             + 4].unsqueeze(2).broadcast_to((128, 4, 128))
                S.op(TE, lambda e, pvm=pvm, nce_h=nce_h: e.matmul(
                    pvm, id_b[:, :], nce_h, start=False, stop=False),
                    reads=["nchl", "id_b"], writes=["pT_A"])
                S.op(TE, lambda e, pvm=pvm, nce_l=nce_l: e.matmul(
                    pvm, id_b[:, :], nce_l, start=False, stop=True),
                    reads=["nchl", "id_b"], writes=["pT_A"])
            S.op(A, lambda e, dpar=dpar: e.activation(
                dgT[dpar][:, :], pT[:, 0:1024], AF.Exp),
                reads=["pT_A"], writes=[f"dgT{dpar}"])

        for g in range(4):
            dpar = g % 2
            g8 = slice(g * 8, (g + 1) * 8)
            gsl = slice(g * 1024, (g + 1) * 1024)
            g512 = slice(g * 512, (g + 1) * 512)
            if g == 0 and not UNPIPE:
                emit_bc(0, 0)
            if UNPIPE:
                emit_bc(g, dpar)
            def xvar(eng, dst, dreg, src, sreg, key, g=g, gsl=gsl):
                sc = small[:, SM[key] + g * 8:SM[key] + g * 8 + 8]
                sce = sc.unsqueeze(2).broadcast_to((128, 8, 128))
                S.op(eng, lambda e, sce=sce, dst=dst, src=src, gsl=gsl:
                     e.tensor_tensor(
                         dst[:, :].rearrange("p (h q) -> p h q", h=8),
                         src[:, gsl].rearrange("p (h q) -> p h q", h=8),
                         sce, ALU.mult),
                     reads=[sreg, "sm_" + key], writes=[dreg])

            xvar(V, xga, "xga", xh, "xh", "scal_g")
            xvar(V, xba, "xba", xhp, "xhp", "scal_b")
            if g == 0:
                tap(f"xga{c}", xga[:, :], ["xga"])
                tap(f"xba{c}", xba[:, :], ["xba"])


            if g == 0:
                tap(f"dgT{c}", dgT[0][:, :], ["dgT0"])

            def wbuild(eng, dst, dreg, co):
                sge = sgT[:, co:co + 128].unsqueeze(1).broadcast_to(
                    (128, 8, 128))
                S.op(eng, lambda e, sge=sge, dst=dst, dpar=dpar:
                     e.tensor_tensor(
                         dst[:, :].rearrange("p (h l) -> p h l", h=8),
                         sge, dgT[dpar][:, :].rearrange("p (h l) -> p h l",
                                                        h=8),
                         ALU.mult),
                     reads=["sgT", f"dgT{dpar}"], writes=[dreg])

            if g + 1 < 4 and not UNPIPE:
                emit_bc(g + 1, (g + 1) % 2)
            S.op(A, lambda e, g512=g512: e.copy(sbfg[0:64, :], Sst[:, g512]),
                 reads=["Sst"], writes=["sbfg"])
            if g == 0:
                tap(f"sbfg{c}", sbfg[:, :], ["sbfg"])
            S.op(TE, lambda e: e.matmul(
                pY[:, 0:512], CrTz[0][:, :], sbfg[:, 0:512],
                start=True, stop=False),
                reads=["CrTz0", "sbfg"], writes=["pY"])
            dece = small[:, SM["tdec"] + g * 8:SM["tdec"] + g * 8 + 8
                         ].unsqueeze(2).broadcast_to((128, 8, 64))
            S.op(V, lambda e, dece=dece: e.tensor_tensor(
                ysc[:, :].rearrange("p (h q) -> p h q", h=8),
                pY[:, :].rearrange("p (h q) -> p h q", h=8),
                dece, ALU.mult),
                reads=["pY", "sm_tdec"], writes=["ysc"])
            if g == 0:
                tap(f"ysc{c}", ysc[:, :], ["ysc"])
            for r in range(2):
                wbuild(V, Wg, "Wg", r * 128)
                wbuild(V, Wb, "xnT", 256 + r * 128)
                if g == 0 and r == 0:
                    tap(f"Wg{c}", Wg[:, :], ["Wg"])
                    tap(f"Wb{c}", Wb[:, :], ["xnT"])
                for hh in range(8):
                    S.op(TE, lambda e, hh=hh, r=r: e.matmul(
                        pY[:, hh * 64:(hh + 1) * 64],
                        Wg[:, hh * 128:(hh + 1) * 128],
                        xga[:, hh * 128 + r * 64:hh * 128 + r * 64 + 64],
                        start=False, stop=False),
                        reads=["Wg", "xga", "ysc"], writes=["pY"])
                    S.op(TE, lambda e, hh=hh, r=r: e.matmul(
                        pY[:, hh * 64:(hh + 1) * 64],
                        Wb[:, hh * 128:(hh + 1) * 128],
                        xba[:, hh * 128 + r * 64:hh * 128 + r * 64 + 64],
                        start=False, stop=(r == 1 and hh == 7)),
                        reads=["xnT", "xba"], writes=["pY"])
            xvar(V, xga, "xga", xh, "xh", "scal_gd")
            xvar(V, xba, "xba", xhp, "xhp", "scal_bd")
            if g == 0:
                tap(f"xgd{c}", xga[:, :], ["xga"])
                tap(f"xbd{c}", xba[:, :], ["xba"])
            S.op(V, lambda e: e.tensor_tensor(ysum[:, :], pY[:, :], ysc[:, :],
                                              ALU.add),
                 reads=["pY", "ysc"], writes=["Wg"])
            if g == 0:
                tap(f"ysum{c}", ysum[:, :], ["Wg"])
            S.op(V, lambda e, g512=g512: e.tensor_tensor(
                YG[:, g512], ysum[:, :], sz[:, g512], ALU.mult),
                reads=["Wg", "sz"], writes=["YG"])
            pSTv = pST[:, :].rearrange("p (h q) -> p h q", h=8)
            for si in range(2):
                lb, lreg = (brt, "brt") if si == 0 else (bbt, "bbt")
                xv, xr2 = (xga, "xga") if si == 0 else (xba, "xba")
                for r in range(2):
                    st_f = (si == 0 and r == 0)
                    sp_f = (si == 1 and r == 1)
                    xvv = xv[:, :].rearrange("p (h q) -> p h q",
                                             h=8)[:, :, r * 64:(r + 1) * 64]
                    S.op(TE, lambda e, r=r, lb=lb, xvv=xvv, st_f=st_f,
                         sp_f=sp_f: e.matmul(
                             pSTv, lb[:, r * 64:(r + 1) * 64],
                             xvv, start=st_f, stop=sp_f),
                         reads=[lreg, xr2], writes=["pST"])
            cde = CD[:, g * 8:(g + 1) * 8].unsqueeze(2).broadcast_to(
                (64, 8, 64))
            S.op(V, lambda e, cde=cde, g512=g512: e.tensor_tensor(
                stmp[:, :].rearrange("p (h q) -> p h q", h=8),
                Sst[:, g512].rearrange("p (h q) -> p h q", h=8),
                cde, ALU.mult),
                reads=["Sst", "CD"], writes=["xga"])
            S.op(V, lambda e, g512=g512: e.tensor_tensor(
                Sst[:, g512], stmp[:, :], pST[:, :], ALU.add),
                reads=["xga", "pST"], writes=["Sst"])
            if g == 3:
                tap(f"Sst{c}", Sst[:, :], ["Sst"])
        tap(f"y{c}", YG[:, :], ["YG"])
        S.dma("gpsimd", bbt[0:1, :], brt[127:128, :], reads=["brt"],
              writes=["bbt"])

        S.op(A, lambda e: e.activation(sz[:, 0:2048], YG[:, :], AF.Square,
                                       accum_out=smcol[:, 13:14]),
             reads=["YG"], writes=["smc13", "sz"])
        S.op(A, lambda e: e.activation(smcol[:, 12:13], smcol[:, 13:14],
                                       AF.Ln, scale=1.0 / DI,
                                       bias=cb_ln[:, 2:3]),
             reads=["smc13", "cb_ln"], writes=["smc12"])
        S.op(A, lambda e: e.activation(smcol[:, 12:13], smcol[:, 12:13],
                                       AF.Exp, scale=-0.5),
             reads=["smc12"], writes=["smc12"])
        S.op(V, lambda e: e.tensor_scalar(sz[:, :], YG[:, :],
                                          smcol[:, 12:13], None, ALU.mult),
             reads=["YG", "smc12"], writes=["sz"])
        tap(f"ynorm{c}", sz[:, :], ["sz"])
        if c + 1 < NCHUNK:
            emit_ln(c + 1)
        for half in range(2):
            for kk in range(8):
                k = half * 8 + kk
                S.op(TE, lambda e, kk=kk, k=k: e.transpose(
                    pTB[:, kk * 128:(kk + 1) * 128],
                    sz[:, k * 128:(k + 1) * 128], id_b[:, :]),
                    reads=["sz", "id_b"], writes=["pTB"])
            S.op(A, lambda e: e.copy(ynT[:, :], pTB[:, 0:1024]),
                 reads=["pTB"], writes=["xga"])
            for fg in range(2):
                pa, preg = pA[fg], f"pA{fg}"
                for kk in range(8):
                    k = half * 8 + kk
                    S.op(TE, lambda e, kk=kk, k=k, fg=fg, pa=pa: e.matmul(
                        pa[:, :], ynT[:, kk * 128:(kk + 1) * 128],
                        WOUT[:, k * 1024 + fg * 512:k * 1024 + fg * 512
                             + 512],
                        start=(k == 0), stop=(k == 15)),
                        reads=["xga", f"WOUT{k // 4}"], writes=[preg])
        for fg in range(2):
            pa, preg = pA[fg], f"pA{fg}"
            S.op(A, lambda e, fg=fg, pa=pa: e.copy(
                s_sb[:, fg * 512:(fg + 1) * 512], pa[:, :]),
                reads=[preg, "sz"], writes=["sz"])
        tap(f"s{c}", s_sb[:, :], ["sz"])
        S.dma("sync", s_d[c * 128:(c + 1) * 128, :], s_sb[:, :],
              reads=["sz"], writes=[f"sd{c}"])

    if DO_FFN:
        S.dma("gpsimd", gtrow[:, :], ccout_d.flatten().unsqueeze(0),
              reads=["ccout"], writes=["gtrow"])
        S.op(V, lambda e: e.tensor_tensor(gtrow[:, :], gtrow[:, :],
                                          cmask[:, :], ALU.mult),
             reads=["gtrow", "cmask"], writes=["gtrow"])
        S.op(V, lambda e: e.tensor_reduce(
            myoff[:, :].unsqueeze(2),
            gtrow[:, :].rearrange("o (a e) -> o e a", a=8), axis=AX.X,
            op=ALU.add),
            reads=["gtrow"], writes=["myoff"])
        moff_b = myoff[:, :].unsqueeze(1).broadcast_to((1, 32, 8))
        S.op(V, lambda e: e.tensor_tensor(
            OFFrow[:, :].rearrange("o (cc ee) -> o cc ee", ee=8),
            OFFrow[:, :].rearrange("o (cc ee) -> o cc ee", ee=8),
            moff_b, ALU.add),
            reads=["OFFrow", "myoff"], writes=["OFFrow"])
        S.op(V, lambda e: e.tensor_copy(OFFrh[:, :], OFFrow[:, :]),
             reads=["OFFrow"], writes=["OFFrh"])
        S.op(V, lambda e: e.tensor_tensor(OFFrl[:, :], OFFrow[:, :],
                                          OFFrh[:, :], ALU.subtract),
             reads=["OFFrow", "OFFrh"], writes=["OFFrl"])
        S.op(V, lambda e: e.tensor_copy(myoh[:, :], myoff[:, :]),
             reads=["myoff"], writes=["myoh"])
        S.op(V, lambda e: e.tensor_tensor(myol[:, :], myoff[:, :],
                                          myoh[:, :], ALU.subtract),
             reads=["myoff", "myoh"], writes=["myol"])
        S.op(TE, lambda e: e.matmul(pT[:, MB:MB + 8], ones_b[:, 0:128],
                                    myoh[:, :], start=True, stop=False),
             reads=["myoh", "ones_b"], writes=["pT_M"])
        S.op(TE, lambda e: e.matmul(pT[:, MB:MB + 8], ones_b[:, 0:128],
                                    myol[:, :], start=False, stop=True),
             reads=["myol", "ones_b"], writes=["pT_M"])
        S.op(V, lambda e: e.tensor_copy(MYOFF[:, :], pT[:, MB:MB + 8]),
             reads=["pT_M"], writes=["MYOFF"])

    ctx1.close()
    if DO_FFN:
        _build_ffn(nc, S, ctx, sb, locals())

    S.finalize()
    block = ctx.enter_context(nc.Block())
    S.build(nc, block, ctx)
    ctx.close()
    return nc


def _build_ffn(nc, S, ctx, sb, env):
    V, A, G_, TE = "vector", "scalar", "gpsimd", "tensor"
    pT, pTB, pA, pY = env["pT"], env["pTB"], env["pA"], env["pY"]
    ones_b, triu32, id_f, id_b = (env["ones_b"], env["triu32"], env["id_f"],
                                  env["id_b"])
    SCNT, IDX, GATE, iota8, cmask = (env["SCNT"], env["IDX"], env["GATE"],
                                     env["iota8"], env["cmask"])
    emk, xc = env["emk"], env["xc"]
    xn_d, s_d, eb_d, eo_d = env["xn_d"], env["s_d"], env["eb_d"], env["eo_d"]
    x_h, out_h = env["x_h"], env["out_h"]
    w1_h, b1_h, w2_h, b2_h = env["w1_h"], env["b1_h"], env["w2_h"], env["b2_h"]
    ebz_keys, tap = env["ebz_keys"], env["tap"]
    OFFrh, OFFrl, MYOFF = env["OFFrh"], env["OFFrl"], env["MYOFF"]
    MB = 1024
    S.auto_extra = [S.fence()]

    DEST_u = sb("DEST_u", (128, 32), U32)
    GSRC_u = sb("GSRC_u", (128, 32), U32)
    GATEM = sb("GATEM", (128, 32))

    emaskall = sb("emaskall", (128, 256), BF16)
    t256 = sb("t256", (128, 256))
    POS = sb("POS", (128, 32))
    MYT = sb("MYT", (128, 32))
    SLOT = sb("SLOT", (128, 32))
    KEEP2 = sb("KEEP2", (128, 32))
    D1 = sb("D1", (128, 32))
    S.op(TE, lambda e: e.matmul(pT[:, MB:MB + 256], ones_b[:, 0:128],
                                OFFrh[:, :], start=True, stop=False),
         reads=["OFFrh", "ones_b"], writes=["pT_M"])
    S.op(TE, lambda e: e.matmul(pT[:, MB:MB + 256], ones_b[:, 0:128],
                                OFFrl[:, :], start=False, stop=True),
         reads=["OFFrl", "ones_b"], writes=["pT_M"])
    idxe = IDX[:, :].unsqueeze(2).broadcast_to((128, 32, 8))
    iote = iota8[:, :].unsqueeze(1).broadcast_to((128, 32, 8))
    emv = emaskall[:, :].rearrange("p (t e) -> p t e", e=8)
    S.op(V, lambda e: e.tensor_tensor(emv, idxe, iote, ALU.is_equal),
         reads=["IDX", "iota8"], writes=["emaskall"])
    S.op(V, lambda e: e.tensor_tensor(t256[:, :], SCNT[:, :],
                                      pT[:, MB:MB + 256], ALU.add),
         reads=["SCNT", "pT_M"], writes=["t256"])
    S.op(V, lambda e: e.tensor_tensor(t256[:, :], t256[:, :], emaskall[:, :],
                                      ALU.mult),
         reads=["t256", "emaskall"], writes=["t256"])
    S.op(V, lambda e: e.tensor_reduce(
        POS[:, :].unsqueeze(2), t256[:, :].rearrange("p (t e) -> p t e", e=8),
        axis=AX.X, op=ALU.add),
        reads=["t256"], writes=["POS"])
    mye = MYOFF[:, :].unsqueeze(1).broadcast_to((128, 32, 8))
    S.op(V, lambda e: e.tensor_tensor(
        t256[:, :].rearrange("p (t e) -> p t e", e=8), mye,
        emaskall[:, :].rearrange("p (t e) -> p t e", e=8), ALU.mult),
        reads=["MYOFF", "emaskall"], writes=["t256"])
    S.op(V, lambda e: e.tensor_reduce(
        MYT[:, :].unsqueeze(2), t256[:, :].rearrange("p (t e) -> p t e", e=8),
        axis=AX.X, op=ALU.add),
        reads=["t256"], writes=["MYT"])
    S.op(V, lambda e: e.scalar_tensor_tensor(SLOT[:, :], POS[:, :], -1.0,
                                             MYT[:, :], ALU.add,
                                             ALU.subtract),
         reads=["POS", "MYT"], writes=["SLOT"])
    S.op(V, lambda e: e.tensor_scalar(KEEP2[:, :], SLOT[:, :], CAP - 0.5,
                                      None, ALU.is_le),
         reads=["SLOT"], writes=["KEEP2"])
    S.op(V, lambda e: e.tensor_scalar(t256[:, 0:32], POS[:, :], 4096.0, None,
                                      ALU.is_le),
         reads=["POS"], writes=["t256"])
    S.op(V, lambda e: e.tensor_tensor(KEEP2[:, :], KEEP2[:, :], t256[:, 0:32],
                                      ALU.mult),
         reads=["KEEP2", "t256"], writes=["KEEP2"])
    S.op(V, lambda e: e.scalar_tensor_tensor(D1[:, :], IDX[:, :], float(CAP),
                                             SLOT[:, :], ALU.mult, ALU.add),
         reads=["IDX", "SLOT"], writes=["D1"])
    S.op(V, lambda e: e.tensor_scalar(t256[:, 0:32], KEEP2[:, :], -BIG, BIG,
                                      ALU.mult, ALU.add),
         reads=["KEEP2"], writes=["t256"])
    S.op(V, lambda e: e.tensor_tensor(t256[:, 32:64], D1[:, :], t256[:, 0:32],
                                      ALU.add),
         reads=["D1", "t256"], writes=["t256"])
    S.op(V, lambda e: e.tensor_copy(DEST_u[:, :], t256[:, 32:64]),
         reads=["t256"], writes=["DEST_u"])
    S.op(V, lambda e: e.tensor_tensor(t256[:, 0:32], D1[:, :], KEEP2[:, :],
                                      ALU.mult),
         reads=["D1", "KEEP2"], writes=["t256"])
    S.op(V, lambda e: e.tensor_copy(GSRC_u[:, :], t256[:, 0:32]),
         reads=["t256"], writes=["GSRC_u"])
    S.op(V, lambda e: e.tensor_tensor(GATEM[:, :], GATE[:, :], KEEP2[:, :],
                                      ALU.mult),
         reads=["GATE", "KEEP2"], writes=["GATEM"])
    tap("gatem", GATEM[:, :], ["GATEM"])
    tap("destu", DEST_u[:, :], ["DEST_u"])
    tap("gsrcu", GSRC_u[:, :], ["GSRC_u"])
    tap("posx", POS[:, :], ["POS"])
    tap("keep2", KEEP2[:, :], ["KEEP2"])

    xsc = [sb(f"xsc{i}", (128, D), BF16) for i in range(4)]
    for t in range(32):
        p4 = t % 4
        S.dma("sync", xsc[p4][:, :], xn_d[t * 128:(t + 1) * 128, :],
              reads=[f"xnd{t}"], writes=[f"xsc{p4}"])
        S.indirect(eb_d, bass.IndirectOffsetOnAxis(ap=DEST_u[:, t:t + 1],
                                                   axis=0),
                   xsc[p4][:, :], None,
                   reads=[f"xsc{p4}", "DEST_u"] + ebz_keys,
                   writes=[f"eb{t}"])
    eb_keys = [f"eb{t}" for t in range(32)] + ebz_keys

    CT = CAP // 128
    XeT = sb("XeT", (128, 8 * CAP), BF16)
    hT = sb("hT", (128, 32 * CAP), BF16)
    w1sb = [sb("w1sb0", (128, 4096), BF16), sb("w1sb1", (128, 4096), BF16)]
    w2res = sb("w2res", (128, 32 * 1024), BF16)
    b1c32 = sb("b1c32", (32, 128))
    b1cols = sb("b1cols", (128, 32))
    b2row = sb("b2row", (1, 1024))
    b2h = sb("b2h", (128, 1024), BF16)
    b2l = sb("b2l", (128, 1024), BF16)
    onepad = env["onepad"]
    S.op(V, lambda e: e.memset(b2h[:, :], 0.0), writes=["b2h"])
    S.op(V, lambda e: e.memset(b2l[:, :], 0.0), writes=["b2l"])
    outsb = [sb("outsb0", (128, D), BF16), sb("outsb1", (128, D), BF16)]
    xe_sb = [sb("xesb0", (128, D), BF16), sb("xesb1", (128, D), BF16)]

    psl = [(pT, 0, "pT_A"), (pT, 512, "pT_A"), (pA[0], 0, "pA0"),
           (pA[1], 0, "pA1"), (pY, 0, "pY"), (pT, 1024, "pT_M")]
    w1g = w1_h.rearrange("ee (k p) (tg j) -> ee p k tg j", p=128, j=512)
    XeT_v = XeT[:, :].rearrange("p (k cc) -> p k cc", k=8)

    def w1load(ex, tg):
        p2 = tg % 2
        S.dma("gpsimd", w1sb[p2][:, :].rearrange("p (k j) -> p k j", k=8),
              w1g[ex, :, :, tg, :], writes=[f"w1sb{p2}"])

    for ex in range(NEXP):
        w1load(ex, 0)
        S.dma("sync", b1c32[:, :],
              b1_h[ex:ex + 1, :].rearrange("o (t p) -> o t p", p=128
                                           ).squeeze(0),
              writes=["b1c32"])
        S.op(TE, lambda e: e.transpose(pT[:, MB:MB + 32], b1c32[:, :],
                                       id_f[0:32, 0:32]),
             reads=["b1c32", "id_f"], writes=["pT_M"])
        S.op(V, lambda e: e.tensor_copy(b1cols[:, :], pT[:, MB:MB + 32]),
             reads=["pT_M"], writes=["b1cols"])
        S.dma("sync", b2row[:, :], b2_h[ex:ex + 1, :], writes=["b2row"])
        S.op(V, lambda e: e.tensor_copy(b2h[0:1, :], b2row[:, :]),
             reads=["b2row"], writes=["b2h"])
        S.op(V, lambda e: e.tensor_tensor(b2l[0:1, :], b2row[:, :],
                                          b2h[0:1, :], ALU.subtract),
             reads=["b2row", "b2h"], writes=["b2l"])
        for i in range(CT):
            p2 = i % 2
            S.dma("sync", xe_sb[p2][:, :],
                  eb_d[ex * CAP + i * 128:ex * CAP + (i + 1) * 128, :],
                  reads=eb_keys, writes=[f"xesb{p2}"])
            for k in range(8):
                S.op(TE, lambda e, k=k, p2=p2: e.transpose(
                    pTB[:, k * 128:(k + 1) * 128],
                    xe_sb[p2][:, k * 128:(k + 1) * 128], id_b[:, :]),
                    reads=[f"xesb{p2}", "id_b"], writes=["pTB"])
            S.op(V, lambda e, i=i: e.tensor_copy(
                XeT_v[:, :, i * 128:(i + 1) * 128],
                pTB[:, 0:1024].rearrange("p (k q) -> p k q", k=8)),
                reads=["pTB"], writes=["XeT"])
        for tg in range(8):
            p2 = tg % 2
            if tg + 1 < 8:
                w1load(ex, tg + 1)
            for ftl in range(4):
                ft = tg * 4 + ftl
                for half, hw in ((0, 512), (1, CAP - 512)):
                    pa, preg = pA[half], f"pA{half}"
                    hsl = slice(half * 512, half * 512 + hw)
                    for k in range(8):
                        S.op(TE, lambda e, k=k, ftl=ftl, hsl=hsl, p2=p2,
                             pa=pa, hw=hw:
                             e.matmul(pa[:, 0:hw],
                                      w1sb[p2][:, k * 512 + ftl * 128:
                                               k * 512 + (ftl + 1) * 128],
                                      XeT_v[:, k, hsl],
                                      start=(k == 0), stop=(k == 7)),
                             reads=[f"w1sb{p2}", "XeT"], writes=[preg])
                    S.op(A, lambda e, ft=ft, hsl=hsl, hw=hw, pa=pa:
                         e.activation(
                             hT[:, ft * CAP + hsl.start:
                                ft * CAP + hsl.start + hw],
                             pa[:, 0:hw], AF.Relu, bias=b1cols[:, ft:ft + 1]),
                         reads=[preg, "b1cols"], writes=["hT"])
        for k in range(32):
            S.dma("gpsimd", w2res[:, k * 1024:(k + 1) * 1024],
                  w2_h[ex, k * 128:(k + 1) * 128, :], writes=[f"w2res{k}"])
        for ct in range(CT):
            p2 = ct % 2
            for dh in range(2):
                ps, pco, preg = psl[(ct % 3) * 2 + dh]
                S.op(TE, lambda e, ps=ps, pco=pco, dh=dh: e.matmul(
                    ps[:, pco:pco + 512], onepad[:, :],
                    b2h[:, dh * 512:(dh + 1) * 512], start=True, stop=False),
                    reads=["b2h", "onepad"], writes=[preg])
                S.op(TE, lambda e, ps=ps, pco=pco, dh=dh: e.matmul(
                    ps[:, pco:pco + 512], onepad[:, :],
                    b2l[:, dh * 512:(dh + 1) * 512], start=False, stop=False),
                    reads=["b2l", "onepad"], writes=[preg])
                for k in range(32):
                    S.op(TE, lambda e, ps=ps, pco=pco, k=k, ct=ct, dh=dh:
                         e.matmul(
                             ps[:, pco:pco + 512],
                             hT[:, k * CAP + ct * 128:
                                k * CAP + (ct + 1) * 128],
                             w2res[:, k * 1024 + dh * 512:
                                   k * 1024 + dh * 512 + 512],
                             start=False, stop=(k == 31)),
                         reads=["hT", f"w2res{k}"], writes=[preg])
            for dh in range(2):
                ps, pco, preg = psl[(ct % 3) * 2 + dh]
                S.op(V, lambda e, ps=ps, pco=pco, dh=dh, p2=p2: e.tensor_copy(
                    outsb[p2][:, dh * 512:(dh + 1) * 512],
                    ps[:, pco:pco + 512]),
                    reads=[preg], writes=[f"outsb{p2}"])
            S.dma("sync",
                  eo_d[ex * CAP + ct * 128:ex * CAP + (ct + 1) * 128, :],
                  outsb[p2][:, :], reads=[f"outsb{p2}"],
                  writes=[f"eo{ex}_{ct}"])
    eo_keys = [f"eo{ex}_{ct}" for ex in range(NEXP) for ct in range(CT)]

    if "ebdump" in DEBUG_TAPS:
        ebo = nc.dram_tensor("dbg_ebdump", [NEXP * CAP, D], BF16,
                             kind="ExternalOutput").ap()
        eoo = nc.dram_tensor("dbg_eodump", [NEXP * CAP, D], F32,
                             kind="ExternalOutput").ap()
        for i in range(NEXP * CAP // 512):
            sl_ = slice(i * 512, (i + 1) * 512)
            S.dma("sync", ebo[sl_, :], eb_d[sl_, :], reads=eb_keys)
            S.dma("sync", eoo[sl_, :], eo_d[sl_, :], reads=eo_keys)
    eo_sb = [outsb[0], outsb[1], sb("eosb2", (128, D), BF16),
             sb("eosb3", (128, D), BF16)]
    eo_rn = ["outsb0", "outsb1", "eosb2", "eosb3"]
    svb = [sb(f"svb{i}", (128, D)) for i in range(2)]
    gbuf = [sb(f"gbuf{i}", (128, D)) for i in range(2)]
    outb = [sb(f"outb{i}", (128, D)) for i in range(2)]
    for t in range(32):
        p4, p2 = t % 4, t % 2
        S.indirect(eo_sb[p4][:, :], None, eo_d,
                   bass.IndirectOffsetOnAxis(ap=GSRC_u[:, t:t + 1], axis=0),
                   reads=eo_keys + ["GSRC_u"], writes=[eo_rn[p4]]),
        tap(f"eo{t}", eo_sb[p4][:, :], [eo_rn[p4]])
        S.dma("sync", xc[p2][:, :], x_h[t * 128:(t + 1) * 128, :],
              writes=[f"xc{p2}"])
        S.dma("scalar", svb[p2][:, :], s_d[t * 128:(t + 1) * 128, :],
              reads=[f"sd{t}"], writes=[f"svb{p2}"])
        S.op(A, lambda e, t=t, p4=p4, p2=p2: e.activation(
            gbuf[p2][:, :], eo_sb[p4][:, :], AF.Silu,
            scale=GATEM[:, t:t + 1]),
            reads=[eo_rn[p4], "GATEM"], writes=[f"gbuf{p2}"])
        S.op(V, lambda e, p2=p2: e.tensor_tensor(
            gbuf[p2][:, :], gbuf[p2][:, :], svb[p2][:, :], ALU.mult),
            reads=[f"gbuf{p2}", f"svb{p2}"], writes=[f"gbuf{p2}"])
        S.op(V, lambda e, p2=p2: e.tensor_tensor(
            outb[p2][:, :], gbuf[p2][:, :], xc[p2][:, :], ALU.add),
            reads=[f"gbuf{p2}", f"xc{p2}"], writes=[f"outb{p2}"])
        S.dma("sync", out_h[t * 128:(t + 1) * 128, :], outb[p2][:, :],
              reads=[f"outb{p2}"], writes=[f"out{t}"])

    if "findump" in DEBUG_TAPS:
        outk = [f"out{t}" for t in range(32)]
        feo = nc.dram_tensor("dbg_feo", [NEXP * CAP, D], F32,
                             kind="ExternalOutput").ap()
        fsd = nc.dram_tensor("dbg_fsd", [T, D], F32,
                             kind="ExternalOutput").ap()
        fgs = nc.dram_tensor("dbg_fgs", [128, 32], U32,
                             kind="ExternalOutput").ap()
        fgm = nc.dram_tensor("dbg_fgm", [128, 32], F32,
                             kind="ExternalOutput").ap()
        fxc = nc.dram_tensor("dbg_fxc", [256, D], F32,
                             kind="ExternalOutput").ap()
        fes = nc.dram_tensor("dbg_fes", [256, D], F32,
                             kind="ExternalOutput").ap()
        for i in range(12):
            sl_ = slice(i * 512, (i + 1) * 512)
            S.dma("sync", feo[sl_, :], eo_d[sl_, :],
                  reads=eo_keys, extra=[S.regions["out31"][0]])
        for i in range(8):
            sl_ = slice(i * 512, (i + 1) * 512)
            S.dma("sync", fsd[sl_, :], s_d[sl_, :],
                  extra=[S.regions["out31"][0]])
        S.dma("sync", fgs[:, :], GSRC_u[:, :], reads=["GSRC_u"],
              extra=[S.regions["out31"][0]])
        S.dma("sync", fgm[:, :], GATEM[:, :], reads=["GATEM"],
              extra=[S.regions["out31"][0]])
        S.dma("sync", fxc[0:128, :], xc[0][:, :], reads=["xc0"],
              extra=[S.regions["out31"][0]])
        S.dma("sync", fxc[128:256, :], xc[1][:, :], reads=["xc1"],
              extra=[S.regions["out31"][0]])
        S.dma("sync", fes[0:128, :], eo_sb[0][:, :], reads=["eosb0"],
              extra=[S.regions["out31"][0]])
        S.dma("sync", fes[128:256, :], eo_sb[1][:, :], reads=["eosb1"],
              extra=[S.regions["out31"][0]])


def kernel(L0_attn_x, L0_attn_W_in, L0_attn_A_log, L0_attn_dt_bias,
           L0_attn_B_norm_bias, L0_attn_C_norm_bias, L0_attn_W_out,
           L0_ffn_router_w, L0_ffn_w1, L0_ffn_b1, L0_ffn_w2, L0_ffn_b2):
    x = np.ascontiguousarray(np.asarray(L0_attn_x, dtype=np.float32))
    consts = _consts()
    nc = build_program()
    in_maps = []
    for b in range(B):
        cm = np.zeros((1, 64), np.float32)
        cm[0, :b * 8] = 1.0
        m = dict(
            x=x[b],
            W_in=np.ascontiguousarray(np.asarray(L0_attn_W_in, np.float32)),
            A_log=np.asarray(L0_attn_A_log, np.float32).reshape(1, H),
            dt_bias=np.asarray(L0_attn_dt_bias, np.float32).reshape(1, H),
            B_nb=np.asarray(L0_attn_B_norm_bias, np.float32).reshape(1, N),
            C_nb=np.asarray(L0_attn_C_norm_bias, np.float32).reshape(1, N),
            W_out=np.ascontiguousarray(np.asarray(L0_attn_W_out, np.float32)),
            router_w=np.ascontiguousarray(
                np.asarray(L0_ffn_router_w, np.float32)),
            w1=np.ascontiguousarray(np.asarray(L0_ffn_w1, np.float32)),
            b1=np.ascontiguousarray(np.asarray(L0_ffn_b1, np.float32)),
            w2=np.ascontiguousarray(np.asarray(L0_ffn_w2, np.float32)),
            b2=np.ascontiguousarray(np.asarray(L0_ffn_b2, np.float32)),
            coremask64=cm,
        )
        m.update(consts)
        in_maps.append(m)
    res = run_bass_kernel_spmd(nc, in_maps, core_ids=list(range(B)))
    return np.stack([res.results[b]["out"] for b in range(B)], axis=0)
